# version 07
# baseline (speedup 1.0000x reference)
import numpy as np
import ml_dtypes

import concourse.bass as bass
import concourse.mybir as mybir
from concourse.bass_utils import run_bass_kernel_spmd

F32 = mybir.dt.float32
BF16 = mybir.dt.bfloat16
AF = mybir.ActivationFunctionType
ALU = mybir.AluOpType
AX = mybir.AxisListType

NCORES = 8
L = 4096
C = 1024
HEADS = 16
NE = 64
PS = 32
E2 = NE * PS
EPC = NE // NCORES

TT = 122
TILES = [(i * TT, TT) for i in range(L // TT)] + [(L - L % TT, L % TT)]
NT = len(TILES)
GROUPS = [list(range(i, min(i + 4, NT))) for i in range(0, NT, 4)]
HALO = 4
SCALE = 1.0 / 32.0
USE_GP_TT = False


class Sched:
    ENGS = ("T", "V", "S", "G", "Y")
    QENG = {"qy": "Y", "qs": "S", "qg": "G"}

    NROT = 8
    EROT = 4

    def __init__(self):
        self.prog = {e: [] for e in self.ENGS}
        self.cnt = {e: 0 for e in self.ENGS}
        self.dcnt = {(q, j): 0 for q in ("qy", "qs", "qg") for j in range(self.NROT)}
        self.didx = {q: 0 for q in ("qy", "qs", "qg")}
        self.sems = None
        self.dsems = None

    def step(self, eng, fn, after=()):
        deps = []
        for d in after:
            if d is None:
                continue
            key, val = d
            if isinstance(key, tuple):
                deps.append((key, self.dcnt[key]))
            else:
                deps.append(d)
        self.prog[eng].append((tuple(deps), fn))
        self.cnt[eng] += 1
        return (eng, self.cnt[eng])

    def dma(self, q, out_ap, in_ap, after=(), **kw):
        sch = self
        j = self.didx[q] % self.NROT
        self.didx[q] += 1

        def fn(e, _o=out_ap, _i=in_ap, _kw=dict(kw), _k=(q, j)):
            e.dma_start(out=_o, in_=_i, **_kw).then_inc(sch.dsems[_k], 16)
        fn._dmaq = (q, j)
        self.step(self.QENG[q], fn, after=after)
        self.dcnt[(q, j)] += 16
        return ((q, j), self.dcnt[(q, j)])

    def dma_tr(self, q, out_ap, in_ap, after=()):
        sch = self
        j = self.didx[q] % self.NROT
        self.didx[q] += 1

        def fn(e, _o=out_ap, _i=in_ap, _k=(q, j)):
            e.dma_start_transpose(_o, _i).then_inc(sch.dsems[_k], 16)
        fn._dmaq = (q, j)
        self.step(self.QENG[q], fn, after=after)
        self.dcnt[(q, j)] += 16
        return ((q, j), self.dcnt[(q, j)])

    def check_deadlock(self):
        pos = {e: 0 for e in self.ENGS}
        done = {e: 0 for e in self.ENGS}
        dmadone = {k: 0 for k in self.dcnt}
        progressed = True
        while progressed:
            progressed = False
            for e in self.ENGS:
                while pos[e] < len(self.prog[e]):
                    waits, fn = self.prog[e][pos[e]]
                    ok = True
                    for key, val in waits:
                        if key == e:
                            continue
                        if isinstance(key, tuple):
                            if dmadone[key] < val:
                                ok = False
                                break
                        elif done[key] < val:
                            ok = False
                            break
                    if not ok:
                        break
                    pos[e] += 1
                    done[e] += 1
                    dk = getattr(fn, "_dmaq", None)
                    if dk:
                        dmadone[dk] += 16
                    progressed = True
        stuck = {e: (pos[e], len(self.prog[e])) for e in self.ENGS if pos[e] < len(self.prog[e])}
        if stuck:
            detail = {}
            for e in stuck:
                waits, fn = self.prog[e][pos[e]]
                detail[e] = (pos[e], waits[:6])
            raise RuntimeError(f"DEADLOCK: stuck={stuck} blocking_waits={detail}")

    def runner(self, ename):
        chunks = self.prog[ename]
        sch = self

        class Proxy:
            __slots__ = ("_e", "last")

            def __init__(self, e):
                self._e = e
                self.last = None

            def __getattr__(self, n):
                a = getattr(self._e, n)
                if callable(a):
                    px = self

                    def w(*args, **kw):
                        r = a(*args, **kw)
                        if r is not None and hasattr(r, "then_inc"):
                            px.last = r
                        return r
                    return w
                return a

        def run(e):
            px = Proxy(e)
            wm = {}
            nout = 0
            R = sch.EROT
            for waits, fn in chunks:
                for key, val in waits:
                    if key == ename:
                        continue
                    if isinstance(key, tuple):
                        if wm.get(key, 0) >= val:
                            continue
                        e.wait_ge(sch.dsems[key], val)
                        wm[key] = val
                    else:
                        j = (val - 1) % R
                        v = (val - 1) // R + 1
                        wk = (key, j)
                        if wm.get(wk, 0) >= v:
                            continue
                        e.wait_ge(sch.sems[(key, j)], v)
                        wm[wk] = v
                px.last = None
                fn(px)
                sem = sch.sems[(ename, nout % R)]
                nout += 1
                if getattr(fn, "_dmaq", None) is not None or px.last is None:
                    e.nop().then_inc(sem, 1)
                else:
                    px.last.then_inc(sem, 1)
        return run


class Arena:

    def __init__(self, nc_sbt, name, dtype):
        self.sbt = nc_sbt
        self.name = name
        self.dtype = dtype
        self.regions = {}
        self.hwm = 0

    def alloc(self, tag, cols):
        cur = self.regions.get(tag, 0)
        off = cur
        self.regions[tag] = cur + cols
        self.hwm = max(self.hwm, cur + cols)
        return off


def build_nc(debug=False, stop="full"):
    nc = bass.Bass("TRN2", target_bir_lowering=False, debug=False)

    xT = nc.dram_tensor("xT", [C, L], F32, kind="ExternalInput").ap()
    wqkv_d = nc.dram_tensor("wqkv", [C, 3 * C], F32, kind="ExternalInput").ap()
    wout_d = nc.dram_tensor("wout", [C, C], F32, kind="ExternalInput").ap()
    bout_d = nc.dram_tensor("bout", [1, C], F32, kind="ExternalInput").ap()
    phiT_d = nc.dram_tensor("phiT", [C, E2], F32, kind="ExternalInput").ap()
    phib_d = nc.dram_tensor("phib", [1, E2], F32, kind="ExternalInput").ap()
    expw_d = nc.dram_tensor("expw", [EPC, C, C], F32, kind="ExternalInput").ap()
    expb_d = nc.dram_tensor("expb", [EPC, C], F32, kind="ExternalInput").ap()
    ident_d = nc.dram_tensor("ident", [128, 128], BF16, kind="ExternalInput").ap()
    ident64_d = nc.dram_tensor("ident64", [128, 64], BF16, kind="ExternalInput").ap()
    onescol_d = nc.dram_tensor("onescol", [128, 1], BF16, kind="ExternalInput").ap()
    onesrow_d = nc.dram_tensor("onesrow", [1, 128], BF16, kind="ExternalInput").ap()
    maskL_d = nc.dram_tensor("maskL", [3, 128, TT], BF16, kind="ExternalInput").ap()
    maskR_d = nc.dram_tensor("maskR", [3, 128, TT], BF16, kind="ExternalInput").ap()
    cnt_d = nc.dram_tensor("cnt", [1, L], BF16, kind="ExternalInput").ap()

    out_d = nc.dram_tensor("out", [L, C], F32, kind="ExternalOutput").ap()

    d_x1 = nc.dram_tensor("d_x1", [L, C], BF16).ap()
    d_D = nc.dram_tensor("d_D", [L, E2], BF16).ap()
    d_C = nc.dram_tensor("d_C", [NT, 128, 16 * 124], BF16).ap()
    d_Xp = nc.dram_tensor("d_Xp", [E2, C], F32)
    d_Xl = nc.dram_tensor("d_Xl", [E2 // NCORES, C], F32)
    d_Yl = nc.dram_tensor("d_Yl", [E2 // NCORES, C], BF16)
    d_Yt = nc.dram_tensor("d_Yt", [E2, C], BF16)

    dbg = {}
    if debug:
        for name, shape, dt in [
            ("dbg_qk", [128, 496], F32), ("dbg_E", [128, 488], F32),
            ("dbg_EL", [128, 488], F32), ("dbg_vo", [128, 1056], F32),
            ("dbg_acc", [128, 1024], F32), ("dbg_s1", [128, 488], F32),
            ("dbg_x1", [128, 488], F32),
            ("dbg_elog", [128, 2048], F32), ("dbg_D", [128, 2048], F32),
            ("dbg_C", [128, 2048], F32), ("dbg_cfm", [128, 1984], F32),
            ("dbg_Xp", [128, 1024], F32), ("dbg_Xln", [128, 1024], F32),
            ("dbg_Yt", [128, 1024], F32), ("dbg_S2", [128, 1024], F32),
            ("dbg_YtF", [128, 16384], F32),
        ]:
            dbg[name] = nc.dram_tensor(name, shape, dt, kind="ExternalOutput").ap()
        for name, shape, dt in [
            ("dbg_dD", [L, E2], BF16), ("dbg_dx1", [L, C], BF16),
            ("dbg_XpL", [E2, C], F32),
            ("dbg_O", [128, 1024], F32),
        ]:
            dbg[name] = nc.dram_tensor(name, shape, dt, kind="ExternalOutput").ap()

    import contextlib
    stack = contextlib.ExitStack()
    sbt = lambda name, shape, dt: stack.enter_context(nc.sbuf_tensor(name, shape, dt))
    pst = lambda name, shape, dt=F32: stack.enter_context(nc.psum_tensor(name, shape, dt))

    with stack:
        W_qkv = sbt("W_qkv", [128, 8 * 3 * C], BF16)
        W_out = sbt("W_out", [128, 8 * C], BF16)
        W_phiT = sbt("W_phiT", [128, 8 * E2], BF16)
        B_out = sbt("B_out", [1, C], BF16)
        B_phi = sbt("B_phi", [1, E2], BF16)
        B_ex1 = sbt("B_ex1", [1, C], BF16)
        IDN = sbt("IDN", [128, 128], BF16)
        ID64 = sbt("ID64", [128, 64], BF16)
        ONESC = sbt("ONESC", [128, 1], BF16)
        ONESR = sbt("ONESR", [1, 128], BF16)
        MKL = sbt("MKL", [128, 3 * TT], BF16)
        MKR = sbt("MKR", [128, 3 * TT], BF16)
        CNTG = sbt("CNTG", [1, 512], BF16)
        EPSC = sbt("EPSC", [128, 1], F32)

        ab = Arena(None, "AB", BF16)
        o_QK = ab.alloc("A", 16 * 496)
        o_XH = ab.alloc("A", 8 * 496)
        o_S1 = ab.alloc("A", 8 * 488)
        o_X1F = ab.alloc("A", 8 * 512)
        o_ACCF = ab.alloc("A", 8 * 488)
        o_DST = ab.alloc("X", 7 * E2)
        o_X1S = ab.alloc("X", 7 * C)
        AB = sbt("AB", [128, ab.hwm], BF16)
        QK = AB[:, o_QK:o_QK + 16 * 496].rearrange("p (c t) -> p c t", t=496)
        XH = AB[:, o_XH:o_XH + 8 * 496].rearrange("p (c t) -> p c t", t=496)
        S1 = AB[:, o_S1:o_S1 + 8 * 488].rearrange("p (c t) -> p c t", t=488)
        X1F = AB[:, o_X1F:o_X1F + 8 * 512].rearrange("p (c t) -> p c t", t=512)
        ACCF = AB[:, o_ACCF:o_ACCF + 8 * 488].rearrange("p (c t) -> p c t", t=488)
        DST = AB[:, o_DST:o_DST + 7 * E2].rearrange("p (j e) -> p j e", e=E2)
        X1S = AB[:, o_X1S:o_X1S + 7 * C].rearrange("p (j e) -> p j e", e=C)
        EXPW = AB[:, o_DST:o_DST + 8 * C].rearrange("p (j e) -> p j e", e=C)

        af = Arena(None, "AF", F32)
        o_TMPX = af.alloc("A", 2 * 488)
        o_TMPL = af.alloc("A", 2 * 256)
        o_TMPR = af.alloc("A", 2 * 256)
        o_XLOC = af.alloc("X", 2 * C)
        o_XSTG = af.alloc("X", 1024)
        o_S2 = af.alloc("B", 2 * 1024)
        o_YOUT = af.alloc("B", 1024)
        AF_ = sbt("AF_", [128, af.hwm], F32)
        TMPX = AF_[:, o_TMPX:o_TMPX + 2 * 488]
        TMPL = AF_[:, o_TMPL:o_TMPL + 2 * 256]
        TMPR = AF_[:, o_TMPR:o_TMPR + 2 * 256]
        XLOC = AF_[:, o_XLOC:o_XLOC + 2 * C].rearrange("p (c t) -> p c t", t=C)
        XSTG = AF_[:, o_XSTG:o_XSTG + 1024]
        S2 = AF_[:, o_S2:o_S2 + 2 * 1024]
        YOUT = AF_[:, o_YOUT:o_YOUT + 1024]

        VO = sbt("VO", [128, 16 * 66], BF16)
        EE = sbt("EE", [128, 2 * 4 * TT], BF16)
        EL = sbt("EL", [128, 2 * 4 * TT], BF16)
        ER = sbt("ER", [128, 2 * 4 * TT], BF16)
        ACCT = sbt("ACCT", [128, 2048], BF16)
        RZ = sbt("RZ", [128, 8], F32)
        ELOG = sbt("ELOG", [128, E2], BF16)
        RP = sbt("RP", [128, NE], F32)
        RN = sbt("RN", [128, PS], F32)
        DTM = sbt("DTM", [128, E2], BF16)
        CTM = sbt("CTM", [128, 2 * E2], BF16)
        CFM = sbt("CFM", [128, 16 * 124], BF16)
        X1T = sbt("X1T", [128, C], BF16)
        XLNT = AB[:, o_X1S:o_X1S + EPC * 8 * PS]
        YSTG = AB[:, o_X1S + EPC * 8 * PS:o_X1S + EPC * 8 * PS + C]
        ROWA = sbt("ROWA", [1, 512], F32)
        ROWB = sbt("ROWB", [1, 512], F32)
        ROWC = sbt("ROWC", [1, 512], F32)
        RBA = sbt("RBA", [1, 512], BF16)
        RBB = sbt("RBB", [1, 512], BF16)

        PG = [pst(f"PG{i}", [128, 512]) for i in range(7)]
        PT = pst("PT", [128, 1024], BF16)

        sems = {(e, j): stack.enter_context(nc.semaphore(f"sem{e}{j}"))
                for e in Sched.ENGS for j in range(Sched.EROT)}
        dsems = {(q, j): stack.enter_context(nc.semaphore(f"dma{q}{j}"))
                 for q in ("qy", "qs", "qg") for j in range(Sched.NROT)}
        ccsem = stack.enter_context(nc.semaphore("ccsem"))

        sch = Sched()
        sch.sems = sems
        sch.dsems = dsems

        guard = {}

        def G(key):
            return list(guard.get(key, ()))

        def setg(key, *labels):
            guard[key] = list(labels)

        def addg(key, *labels):
            guard.setdefault(key, []).extend(labels)

        p0 = []
        p0.append(sch.dma("qg", IDN[:, :], ident_d))
        p0.append(sch.dma("qg", ID64[:, :], ident64_d))
        p0.append(sch.dma("qg", ONESC[:, :], onescol_d))
        p0.append(sch.dma("qg", ONESR[:, :], onesrow_d))
        p0.append(sch.dma("qg", MKL[:, :].rearrange("p (n t) -> p n t", t=TT),
                          maskL_d.rearrange("n p t -> p n t")))
        p0.append(sch.dma("qg", MKR[:, :].rearrange("p (n t) -> p n t", t=TT),
                          maskR_d.rearrange("n p t -> p n t")))
        p0.append(sch.dma("qg", B_out[:, :], bout_d))
        p0.append(sch.dma("qg", B_phi[:, :], phib_d))
        p0.append(sch.dma("qg", W_qkv[:, :].rearrange("p (c k) -> p c k", k=3 * C),
                          wqkv_d.rearrange("(c p) k -> p c k", p=128)))
        p0.append(sch.dma("qg", W_out[:, :].rearrange("p (c k) -> p c k", k=C),
                          wout_d.rearrange("(c p) k -> p c k", p=128)))
        p0.append(sch.dma("qg", W_phiT[:, :].rearrange("p (c k) -> p c k", k=E2),
                          phiT_d.rearrange("(c p) k -> p c k", p=128)))

        def f_init(e):
            e.memset(EPSC[:, :], 1e-5)
            e.memset(EE[:, :], 0.0)
            e.memset(VO[:, :], 1.0)
            e.memset(AB[:, :], 0.0)
            e.memset(AF_[:, :], 0.0)
        l_init = sch.step("V", f_init)

        def f_init2(e):
            for t in (EL, ER, ACCT, ELOG, RP, RN, DTM, CTM, CFM, X1T,
                      RZ, ROWA, ROWB, ROWC, RBA, RBB):
                e.memset(t[:, :], 0.0)
        l_init2 = sch.step("G", f_init2)
        consts = p0 + [l_init, l_init2]

        if stop == "P0":
            sch.step("Y", lambda e: None, after=consts)
            sch.check_deadlock()
            _emit(nc, {e: sch.runner(e) for e in Sched.ENGS})
            return nc

        x1_spills = []
        D_spills = []
        C_spills = []
        prev_group = list(consts)
        pend_ct = [None]
        C_spill_by_tile = {}

        def emit_ct(ti2, tn2, cp2, lC2):
            cts = []
            for rnd in range(2):
                def f_ct(e, _rnd=rnd, _tn=tn2, _cp=cp2):
                    for cc in range(8):
                        e.transpose(PT[0:128, 128 * cc:128 * cc + _tn],
                                    CTM[0:_tn, E2 * _cp + 128 * (8 * _rnd + cc):
                                        E2 * _cp + 128 * (8 * _rnd + cc + 1)],
                                    IDN[0:_tn, 0:_tn])
                lct = sch.step("T", f_ct, after=[lC2] + G("pt"))

                def f_cev(e, _rnd=rnd, _tn=tn2):
                    e.copy(CFM[:, :].rearrange("p (c w) -> p c w", w=124)
                           [:, 8 * _rnd:8 * (_rnd + 1), 0:_tn],
                           PT[:, :].rearrange("p (c t) -> p c t", t=128)[:, :, 0:_tn])
                lcev = sch.step("S", f_cev, after=[lct] + G("cfm"))
                setg("pt", lcev)
                cts.append(lcev)
            addg(f"ctm{cp2}", cts[-1])
            spC = sch.dma("qy", d_C[ti2, :, :], CFM[:, :], after=cts)
            setg("cfm", spC)
            C_spill_by_tile[ti2] = spC
            C_spills.append(spC)
            return [spC]

        for gi, group in enumerate(GROUPS):
            g0 = TILES[group[0]][0]
            gn = sum(TILES[t][1] for t in group)
            h0, hn = g0 - HALO, gn + 2 * HALO

            c0, c1 = max(h0, 0), min(h0 + hn, L)
            off = c0 - h0
            lh = sch.dma("qg", XH[:, :, off:off + (c1 - c0)],
                         xT.rearrange("(c p) t -> p c t", p=128)[:, :, c0:c1],
                         after=prev_group)
            lcnt = sch.dma("qg", CNTG[0:1, 0:gn], cnt_d[0:1, g0:g0 + gn],
                           after=prev_group)
            xh_ready = [lh]
            if off > 0 or off + (c1 - c0) < hn:
                def f_pad(e, _off=off, _w=c1 - c0, _hn=hn):
                    if _off > 0:
                        e.memset(XH[:, :, 0:_off], 0.0)
                    if _off + _w < _hn:
                        e.memset(XH[:, :, _off + _w:_hn], 0.0)
                xh_ready.append(sch.step("G", f_pad, after=[lh]))

            qk_ready = []
            for qc in range(16):
                bank = qc % 2

                def f_qk(e, _qc=qc, _hn=hn, _bank=bank):
                    for dc in range(8):
                        e.matmul(PG[_bank][:, 0:_hn],
                                 W_qkv[:, 3 * C * dc + 128 * _qc:3 * C * dc + 128 * (_qc + 1)],
                                 XH[:, dc, 0:_hn],
                                 start=(dc == 0), stop=(dc == 7))
                lm = sch.step("T", f_qk, after=xh_ready + G(f"pg{bank}"))

                def f_ev(e, _qc=qc, _hn=hn, _bank=bank):
                    e.copy(QK[:, _qc, 0:_hn], PG[_bank][:, 0:_hn])
                ev = sch.step("S", f_ev, after=[lm])
                setg(f"pg{bank}", ev)
                qk_ready.append(ev)
            qk_done = qk_ready[-2:]

            if debug and gi == 0:
                sch.dma("qg", dbg["dbg_qk"], QK[:, 8, 0:496], after=qk_ready)

            if stop == "G0QK" and gi == 0:
                sch.step("Y", lambda e: None, after=qk_ready)
                sch.check_deadlock()
                _emit(nc, {e: sch.runner(e) for e in Sched.ENGS})
                return nc

            s1_ready = []
            acc_done = []
            for ti in group:
                t0, tn = TILES[ti]
                kn = min(tn + 6, 128)
                tloc = t0 - g0
                koff = tloc - 2 + HALO
                qoff = tloc + HALO
                mvar = 1 if ti == 0 else (2 if ti == NT - 1 else 0)

                lvev = None
                for half in range(2):
                    def f_v(e, _koff=koff, _kn=kn, _half=half):
                        for dc in range(8):
                            e.matmul(PG[6][0:_kn, 0:512],
                                     XH[:, dc, _koff:_koff + _kn],
                                     W_qkv[:, 3 * C * dc + 2048 + 512 * _half:
                                           3 * C * dc + 2048 + 512 * (_half + 1)],
                                     start=(dc == 0), stop=(dc == 7))
                    lv = sch.step("T", f_v, after=xh_ready + qk_done + G("pg6"))

                    def f_vev(e, _kn=kn, _half=half):
                        e.copy(
                            VO[0:_kn, 8 * 66 * _half:8 * 66 * (_half + 1)]
                            .rearrange("p (h c) -> p h c", c=66)[:, :, 0:64],
                            PG[6][0:_kn, 0:512].rearrange("p (h c) -> p h c", c=64))
                    lvev = sch.step("S", f_vev,
                                    after=[lv] + (G("vo") + G("vodbg") if half == 0 else []))
                    setg("pg6", lvev)

                if debug and ti == 0:
                    setg("vodbg", sch.dma("qg", dbg["dbg_vo"], VO[:, 0:1056], after=[lvev]))

                if stop == "T0V" and gi == 0 and ti == 0:
                    sch.step("Y", lambda e: None, after=[lvev])
                    sch.check_deadlock()
                    _emit(nc, {e: sch.runner(e) for e in Sched.ENGS})
                    return nc
                W4 = 4 * TT
                acctp = ti % 2

                def emit_u_cmb(hb, lm_pair, _tn=tn, _acctp=acctp):
                    heads = tuple(range(4 * hb, 4 * hb + 4))
                    p = hb % 2

                    def f_u(e, _heads=heads, _p=p, _tn2=_tn):
                        perm = (0, 2, 1, 3)
                        for j, h in enumerate(_heads):
                            cp = perm[j]
                            e.matmul(PG[4][0:_tn2, 65 * j:65 * j + 65],
                                     EL[:, W4 * _p + TT * cp:W4 * _p + TT * cp + _tn2],
                                     VO[:, 66 * h:66 * h + 65],
                                     start=True, stop=True)
                            e.matmul(PG[5][0:_tn2, 65 * j:65 * j + 65],
                                     ER[:, W4 * _p + TT * cp:W4 * _p + TT * cp + _tn2],
                                     VO[:, 66 * h:66 * h + 65],
                                     start=True, stop=True)
                    lu = sch.step("T", f_u, after=list(lm_pair) + G("pg4") + G("pg5"))
                    setg(f"lastu{p}", lu)

                    def f_cmb(e, _hb=hb, _tn2=_tn, _ap=_acctp):
                        zL = PG[4][0:_tn2, 0:260].rearrange("p (s c) -> p s c", c=65)[:, :, 64:65]
                        zR = PG[5][0:_tn2, 0:260].rearrange("p (s c) -> p s c", c=65)[:, :, 64:65]
                        e.tensor_scalar(RZ[0:_tn2, 0:4].unsqueeze(2), zL, 1e-30, None, ALU.add)
                        e.tensor_scalar(RZ[0:_tn2, 4:8].unsqueeze(2), zR, 1e-30, None, ALU.add)
                        e.drain()
                        e.reciprocal(RZ[0:_tn2, 0:8], RZ[0:_tn2, 0:8])
                        e.drain()
                        uL = PG[4][0:_tn2, 0:260].rearrange("p (s c) -> p s c", c=65)[:, :, 0:64]
                        uR = PG[5][0:_tn2, 0:260].rearrange("p (s c) -> p s c", c=65)[:, :, 0:64]
                        rL = RZ[0:_tn2, 0:4].unsqueeze(2).broadcast_to([_tn2, 4, 64])
                        rR = RZ[0:_tn2, 4:8].unsqueeze(2).broadcast_to([_tn2, 4, 64])
                        o = 256 * (_hb % 2)
                        e.tensor_tensor(TMPL[0:_tn2, o:o + 256]
                                        .rearrange("p (s c) -> p s c", c=64),
                                        uL, rL, ALU.mult)
                        e.tensor_tensor(TMPR[0:_tn2, o:o + 256]
                                        .rearrange("p (s c) -> p s c", c=64),
                                        uR, rR, ALU.mult)
                        e.drain()
                        e.tensor_tensor(ACCT[0:_tn2, 1024 * _ap + 256 * _hb:
                                             1024 * _ap + 256 * (_hb + 1)],
                                        TMPL[0:_tn2, o:o + 256], TMPR[0:_tn2, o:o + 256],
                                        ALU.add)
                    lc = sch.step("V", f_cmb, after=[lu] + G(f"acct{_acctp}"))
                    setg("pg4", lc)
                    setg("pg5", lc)
                    return lu, lc

                pend = None
                batch_last = None
                lu_prev = None
                for hb in range(4):
                    heads = tuple(range(4 * hb, 4 * hb + 4))
                    p = hb % 2

                    def f_gram(e, _heads=heads, _kn=kn, _tn=tn, _koff=koff, _qoff=qoff):
                        for j, h in enumerate(_heads):
                            poff = 64 * (h % 2)
                            bank = h % 2
                            slot = j // 2
                            e.matmul(PG[bank][0:_kn, TT * slot:TT * slot + _tn],
                                     QK[poff:poff + 64, 8 + h // 2, _koff:_koff + _kn],
                                     QK[poff:poff + 64, h // 2, _qoff:_qoff + _tn],
                                     start=True, stop=True, tile_position=(poff, 0))
                    lg = sch.step("T", f_gram, after=[lvev] + qk_done + G("pg0") + G("pg1"))

                    if pend is not None:
                        lu_prev, batch_last = emit_u_cmb(*pend)

                    def f_exp(e, _kn=kn, _p=p):
                        e.activation(EE[0:_kn, W4 * _p:W4 * _p + 2 * TT],
                                     PG[0][0:_kn, 0:2 * TT], AF.Exp, scale=SCALE)
                        e.activation(EE[0:_kn, W4 * _p + 2 * TT:W4 * _p + 4 * TT],
                                     PG[1][0:_kn, 0:2 * TT], AF.Exp, scale=SCALE)
                    le = sch.step("S", f_exp, after=[lg] + G(f"lastm{p}") + G("eedbg"))
                    setg("pg0", le)
                    setg("pg1", le)

                    def f_mL(e, _mv=mvar, _tn=tn, _p=p):
                        src = MKL[:, :].rearrange("p (n t) -> p n t", t=TT)[:, _mv, 0:_tn]
                        e.tensor_tensor(
                            EL[:, W4 * _p:W4 * (_p + 1)]
                            .rearrange("p (i t) -> p i t", t=TT)[:, :, 0:_tn],
                            EE[:, W4 * _p:W4 * (_p + 1)]
                            .rearrange("p (i t) -> p i t", t=TT)[:, :, 0:_tn],
                            src.unsqueeze(1).broadcast_to([128, 4, _tn]), ALU.mult)
                    lmL = sch.step("V", f_mL, after=[le] + G(f"lastu{p}") + G("eldbg"))

                    def f_mR(e, _mv=mvar, _tn=tn, _p=p):
                        src = MKR[:, :].rearrange("p (n t) -> p n t", t=TT)[:, _mv, 0:_tn]
                        e.tensor_tensor(
                            ER[:, W4 * _p:W4 * (_p + 1)]
                            .rearrange("p (i t) -> p i t", t=TT)[:, :, 0:_tn],
                            EE[:, W4 * _p:W4 * (_p + 1)]
                            .rearrange("p (i t) -> p i t", t=TT)[:, :, 0:_tn],
                            src.unsqueeze(1).broadcast_to([128, 4, _tn]), ALU.mult)
                    lmR = sch.step("V", f_mR, after=[le] + G(f"lastu{p}"))
                    setg(f"lastm{p}", lmL, lmR)

                    if debug and ti == 0 and hb == 0:
                        setg("eedbg", sch.dma("qg", dbg["dbg_E"], EE[:, 0:488], after=[lmL, lmR]))
                        setg("eldbg", sch.dma("qg", dbg["dbg_EL"], EL[:, 0:488], after=[lmL, lmR]))

                    pend = (hb, (lmL, lmR))
                lu_prev, batch_last = emit_u_cmb(*pend)

                def f_tr(e, _tn=tn, _ap=acctp):
                    for cc in range(8):
                        e.transpose(PT[0:128, 128 * cc:128 * cc + _tn],
                                    ACCT[0:_tn, 1024 * _ap + 128 * cc:
                                         1024 * _ap + 128 * (cc + 1)],
                                    IDN[0:_tn, 0:_tn])
                ltr = sch.step("T", f_tr, after=[batch_last] + G("pt"))
                setg(f"acct{acctp}", ltr)

                def f_trev(e, _tn=tn, _tloc=tloc):
                    e.copy(ACCF[:, :, _tloc:_tloc + _tn],
                           PT[:, :].rearrange("p (c t) -> p c t", t=128)[:, :, 0:_tn])
                ltrev = sch.step("S", f_trev, after=[ltr] + G("accdbg"))
                setg("pt", ltrev)
                setg("vo", lu_prev)
                acc_done.append(ltrev)

                if debug and ti == 0:
                    setg("accdbg", sch.dma("qg", dbg["dbg_acc"],
                         ACCF[:, :, :].rearrange("p c t -> p (c t)")[:, 0:1024], after=[ltrev]))

            if stop == "G0ATT" and gi == 0:
                sch.step("Y", lambda e: None, after=acc_done)
                sch.check_deadlock()
                _emit(nc, {e: sch.runner(e) for e in Sched.ENGS})
                return nc

            s1_steps = []
            for oc in range(8):
                bk = 6

                def f_wo(e, _oc=oc, _gn=gn, _bk=bk):
                    for dc in range(8):
                        e.matmul(PG[_bk][:, 0:_gn],
                                 W_out[:, C * dc + 128 * _oc:C * dc + 128 * (_oc + 1)],
                                 ACCF[:, dc, 0:_gn],
                                 start=(dc == 0), stop=False)
                    e.matmul(PG[_bk][:, 0:_gn],
                             B_out[0:1, 128 * _oc:128 * (_oc + 1)],
                             CNTG[0:1, 0:_gn],
                             start=False, stop=True)
                lw = sch.step("T", f_wo, after=acc_done + [lcnt] + G(f"pg{bk}"))

                def f_s1(e, _oc=oc, _gn=gn, _bk=bk):
                    e.tensor_tensor(S1[:, _oc, 0:_gn], PG[_bk][:, 0:_gn],
                                    XH[:, _oc, HALO:HALO + _gn], ALU.add)
                ls1 = sch.step("V", f_s1, after=[lw] + G("s1dbg"))
                setg(f"pg{bk}", ls1)
                s1_steps.append(ls1)

            if debug and gi == 0:
                addg("s1dbg", sch.dma("qg", dbg["dbg_s1"], S1[:, 0, 0:488], after=s1_steps))

            def f_sq(e, _gn=gn):
                for dc in range(8):
                    e.activation(XH[:, dc, 0:_gn], S1[:, dc, 0:_gn], AF.Square)
            lsq = sch.step("S", f_sq, after=s1_steps)

            def f_stat(e, _gn=gn):
                for dc in range(8):
                    e.matmul(PG[4][0:1, 0:_gn], ONESC[:, 0:1], S1[:, dc, 0:_gn],
                             start=(dc == 0), stop=(dc == 7))
                for dc in range(8):
                    e.matmul(PG[5][0:1, 0:_gn], ONESC[:, 0:1], XH[:, dc, 0:_gn],
                             start=(dc == 0), stop=(dc == 7))
            lst = sch.step("T", f_stat, after=s1_steps + [lsq] + G("pg4") + G("pg5"))

            def f_rows(e, _gn=gn):
                e.tensor_scalar(ROWA[0:1, 0:_gn], PG[4][0:1, 0:_gn], 1.0 / C, None, ALU.mult)
                e.tensor_scalar(ROWB[0:1, 0:_gn], PG[5][0:1, 0:_gn], 1.0 / C, None, ALU.mult)
                e.drain()
                e.tensor_tensor(ROWC[0:1, 0:_gn], ROWA[0:1, 0:_gn], ROWA[0:1, 0:_gn], ALU.mult)
                e.drain()
                e.tensor_tensor(ROWC[0:1, 0:_gn], ROWB[0:1, 0:_gn], ROWC[0:1, 0:_gn],
                                ALU.subtract)
            lrows = sch.step("V", f_rows, after=[lst])

            def f_rs(e, _gn=gn):
                e.activation(ROWB[0:1, 0:_gn], ROWC[0:1, 0:_gn], AF.Ln, bias=EPSC[0:1, 0:1])
                e.drain()
                e.activation(ROWB[0:1, 0:_gn], ROWB[0:1, 0:_gn], AF.Exp, scale=-0.5)
            lrs = sch.step("S", f_rs, after=[lrows])

            def f_rows2(e, _gn=gn):
                e.drain()
                e.tensor_tensor(ROWC[0:1, 0:_gn], ROWA[0:1, 0:_gn], ROWB[0:1, 0:_gn], ALU.mult)
                e.tensor_copy(RBA[0:1, 0:_gn], ROWB[0:1, 0:_gn])
                e.drain()
                e.tensor_copy(RBB[0:1, 0:_gn], ROWC[0:1, 0:_gn])
            lrows2 = sch.step("V", f_rows2, after=[lrs])

            def f_bc(e, _gn=gn):
                e.matmul(PG[2][:, 0:_gn], ONESR[0:1, :], RBA[0:1, 0:_gn], start=True, stop=True)
                e.matmul(PG[3][:, 0:_gn], ONESR[0:1, :], RBB[0:1, 0:_gn], start=True, stop=True)
            lbc = sch.step("T", f_bc, after=[lrows2, lrows] + G("pg2") + G("pg3"))
            setg("pg4", lrows)
            setg("pg5", lrows)

            x1_steps = []
            for dc in range(8):
                def f_ap(e, _dc=dc, _gn=gn):
                    o = 488 * (_dc % 2)
                    e.tensor_tensor(TMPX[:, o:o + _gn], S1[:, _dc, 0:_gn], PG[2][:, 0:_gn],
                                    ALU.mult)
                    e.drain()
                    e.tensor_tensor(X1F[:, _dc, 0:_gn], TMPX[:, o:o + _gn], PG[3][:, 0:_gn],
                                    ALU.subtract)
                x1_steps.append(sch.step("V", f_ap,
                                          after=[lbc] + (G("x1f") if dc == 0 else [])))
            guard["x1f"] = []
            setg("pg2", x1_steps[-1])
            setg("pg3", x1_steps[-1])

            if debug and gi == 0:
                addg("x1f", sch.dma("qg", dbg["dbg_x1"], X1F[:, 0, 0:488], after=x1_steps))

            if stop == "G0X1" and gi == 0:
                sch.step("Y", lambda e: None, after=x1_steps)
                sch.check_deadlock()
                _emit(nc, {e: sch.runner(e) for e in Sched.ENGS})
                return nc

            tile_tail = []
            for ti in group:
                t0, tn = TILES[ti]
                tloc = t0 - g0

                trs = []
                for dc in range(8):
                    trs.append(sch.dma_tr("qy", X1T[0:128, 128 * dc:128 * (dc + 1)],
                                          X1F[:, dc, tloc:tloc + 128],
                                          after=x1_steps + G("x1t")))
                sp1 = sch.dma("qy", d_x1[t0:t0 + tn, :], X1T[0:tn, :], after=trs)
                setg("x1t", sp1)
                x1_spills.append(sp1)

                lgs = []
                lee = None
                for rnd in range(2):
                    def f_lg(e, _tn=tn, _tloc=tloc, _r=rnd):
                        for dc in range(8):
                            for ecl in range(2):
                                e.matmul(PG[2 + ecl][0:_tn, 0:512],
                                         X1F[:, dc, _tloc:_tloc + _tn],
                                         W_phiT[:, E2 * dc + 512 * (2 * _r + ecl):
                                                E2 * dc + 512 * (2 * _r + ecl + 1)],
                                         start=(dc == 0), stop=False)
                        for ecl in range(2):
                            e.matmul(PG[2 + ecl][0:_tn, 0:512],
                                     ONESR[0:1, 0:_tn],
                                     B_phi[0:1, 512 * (2 * _r + ecl):512 * (2 * _r + ecl + 1)],
                                     start=False, stop=True)
                    llg = sch.step("T", f_lg, after=x1_steps + G("pg2") + G("pg3"))
                    lgs.append(llg)

                    def f_eexp(e, _tn=tn, _r=rnd):
                        e.activation(ELOG[0:_tn, 1024 * _r:1024 * _r + 512],
                                     PG[2][0:_tn, 0:512], AF.Exp)
                        e.activation(ELOG[0:_tn, 1024 * _r + 512:1024 * (_r + 1)],
                                     PG[3][0:_tn, 0:512], AF.Exp)
                    lee = sch.step("S", f_eexp,
                                   after=[llg] + (G("elog") if rnd == 0 else []))
                    setg("pg2", lee)
                    setg("pg3", lee)
                guard["elog"] = []

                if debug and ti == 0:
                    addg("elog", sch.dma("qg", dbg["dbg_elog"], ELOG[0:128, :], after=[lee]))

                def f_sumP(e, _tn=tn):
                    e.reduce_sum(RP[0:_tn, :],
                                 ELOG[0:_tn, :].rearrange("p (n s) -> p n s", s=PS), axis=AX.X)
                    e.drain()
                    e.reciprocal(RP[0:_tn, :], RP[0:_tn, :])
                    e.drain()
                lsp = sch.step("V", f_sumP, after=[lee])

                def f_sumN(e, _tn=tn):
                    e.reduce_sum(RN[0:_tn, :],
                                 ELOG[0:_tn, :].rearrange("p (n s) -> p s n", s=PS), axis=AX.X)
                    e.drain()
                    e.reciprocal(RN[0:_tn, :], RN[0:_tn, :])
                    e.drain()
                lrn = sch.step("V", f_sumN, after=[lee])

                def f_D(e, _tn=tn):
                    e.tensor_tensor(DTM[0:_tn, :].rearrange("p (n s) -> p n s", s=PS),
                                    ELOG[0:_tn, :].rearrange("p (n s) -> p n s", s=PS),
                                    RP[0:_tn, :].unsqueeze(2).broadcast_to([_tn, NE, PS]),
                                    ALU.mult)
                lD = sch.step("V", f_D, after=[lsp] + G("dtm"))
                guard["dtm"] = []

                ctp = ti % 2

                def f_C(e, _tn=tn, _cp=ctp):
                    e.tensor_tensor(CTM[0:_tn, E2 * _cp:E2 * (_cp + 1)]
                                    .rearrange("p (n s) -> p s n", s=PS),
                                    ELOG[0:_tn, :].rearrange("p (n s) -> p s n", s=PS),
                                    RN[0:_tn, :].unsqueeze(2).broadcast_to([_tn, PS, NE]),
                                    ALU.mult)
                lC = sch.step("G", f_C, after=[lrn] + G(f"ctm{ctp}"))
                guard[f"ctm{ctp}"] = []
                addg("elog", lD, lC)

                if debug and ti == 0:
                    addg("dtm", sch.dma("qg", dbg["dbg_D"], DTM[0:128, :], after=[lD]))
                    lCbar = sch.step("G", lambda e: e.drain(), after=[lC])
                    addg("ctm0", sch.dma("qg", dbg["dbg_C"], CTM[0:128, 0:E2], after=[lCbar]))

                spD = sch.dma("qy", d_D[t0:t0 + tn, :], DTM[0:tn, :], after=[lD])
                addg("dtm", spD)
                D_spills.append(spD)

                if pend_ct[0] is not None:
                    tile_tail += emit_ct(*pend_ct[0])
                pend_ct[0] = (ti, tn, ctp, lC)

                tile_tail += [lee, lD, lC, spD, lsp, lrn]
                addg("x1f", lgs[-1], *trs)

            prev_group = s1_steps + x1_steps + [lst] + tile_tail + trs
            if stop == "G0" and gi == 0:
                sch.step("Y", lambda e: None, after=prev_group)
                sch.check_deadlock()
                _emit(nc, {e: sch.runner(e) for e in Sched.ENGS})
                return nc

        if stop == "A":
            sch.step("Y", lambda e: None, after=list(prev_group) + D_spills + C_spills + x1_spills)
            sch.check_deadlock()
            runs = {e: sch.runner(e) for e in Sched.ENGS}
            _emit(nc, runs)
            return nc

        if pend_ct[0] is not None:
            prev_group = prev_group + emit_ct(*pend_ct[0])
            pend_ct[0] = None

        SCH = [list(range(i, min(i + 7, NT))) for i in range(0, NT, 7)]
        xp_w = []
        prevX = list(prev_group) + D_spills[-1:] + x1_spills[-1:]
        lev_prev = None
        for si, sc in enumerate(SCH):
            loads = []
            for j, ti in enumerate(sc):
                t0, tn = TILES[ti]
                loads.append(sch.dma("qy", DST[0:tn, j, :], d_D[t0:t0 + tn, :],
                                     after=prevX + [D_spills[ti]]))
                loads.append(sch.dma("qs", X1S[0:tn, j, :], d_x1[t0:t0 + tn, :],
                                     after=prevX + [x1_spills[ti]]))
            lx2_list = []
            for ec in range(16):
                def f_x(e, _sc=tuple(sc), _ec=ec):
                    for half in range(2):
                        for j, ti in enumerate(_sc):
                            tn = TILES[ti][1]
                            e.matmul(PG[half][:, 0:512],
                                     DST[0:tn, j, 128 * _ec:128 * (_ec + 1)],
                                     X1S[0:tn, j, 512 * half:512 * (half + 1)],
                                     start=(j == 0), stop=(j == len(_sc) - 1))
                lx2 = sch.step("T", f_x, after=loads + G("pg0") + G("pg1"))
                lx2_list.append(lx2)

                def f_xev(e):
                    e.copy(XSTG[:, 0:512], PG[0][:, 0:512])
                    e.copy(XSTG[:, 512:1024], PG[1][:, 0:512])
                lev = sch.step("S", f_xev, after=[lx2] + ([lev_prev] if lev_prev else []) + xp_w[-1:])
                setg("pg0", lev)
                setg("pg1", lev)
                lev_prev = lev
                kw = {} if si == 0 else {"accum_op": ALU.add}
                prior = [xp_w[len(xp_w) - 16]] if len(xp_w) >= 16 else []
                xp_w.append(sch.dma("qg", d_Xp.ap()[128 * ec:128 * (ec + 1), :], XSTG[:, :],
                                    after=[lev] + prior, **kw))
            prevX = loads + lx2_list[-1:]

        if debug:
            dcp = sch.dma("qg", dbg["dbg_dD"], d_D, after=xp_w)
            dcp2 = sch.dma("qg", dbg["dbg_dx1"], d_x1, after=xp_w)
            dcp3 = sch.dma("qg", dbg["dbg_XpL"], d_Xp.ap(), after=xp_w)
            xp_w = xp_w + [dcp, dcp2, dcp3]

        def f_rsc(e):
            e.collective_compute("ReduceScatter", ALU.add,
                                 replica_groups=[list(range(NCORES))],
                                 ins=[d_Xp.ap().opt()], outs=[d_Xl.ap().opt()]).then_inc(ccsem, 1)
            e.wait_ge(ccsem, 1)
            e.nop()
        lrsc = sch.step("G", f_rsc, after=xp_w)

        lxl = sch.dma("qy", XLOC[:, :, :], d_Xl.ap().rearrange("(c p) d -> p c d", p=128),
                      after=[lrsc])
        if debug:
            setg("xpdbg", sch.dma("qg", dbg["dbg_Xp"], XLOC[:, 0, :], after=[lxl]))

        lch_done = []
        for ch in range(2):
            def f_cl(e, _ch=ch):
                e.tensor_scalar(XLOC[:, _ch, :], XLOC[:, _ch, :], -33000.0, 65000.0,
                                ALU.max, ALU.min)
                e.drain()
                e.reduce_sum(RZ[:, 0:1], XLOC[:, _ch, :], axis=AX.X)
                e.drain()
                e.tensor_scalar(RZ[:, 0:1], RZ[:, 0:1], 1.0 / C, None, ALU.mult)
                e.drain()
                e.tensor_scalar(XLOC[:, _ch, :], XLOC[:, _ch, :], RZ[:, 0:1], None,
                                ALU.subtract)
                e.drain()
                e.tensor_tensor(XSTG[:, :], XLOC[:, _ch, :], XLOC[:, _ch, :], ALU.mult)
                e.drain()
                e.reduce_sum(RZ[:, 1:2], XSTG[:, :], axis=AX.X)
                e.drain()
                e.tensor_scalar(RZ[:, 1:2], RZ[:, 1:2], 1.0 / C, None, ALU.mult)
            lcl = sch.step("V", f_cl, after=[lxl] + G("xpdbg"))

            def f_crs(e, _ch=ch):
                e.activation(RZ[:, 2:3], RZ[:, 1:2], AF.Ln, bias=EPSC[:, 0:1])
                e.drain()
                e.activation(RZ[:, 2:3], RZ[:, 2:3], AF.Exp, scale=-0.5)
            lcrs = sch.step("S", f_crs, after=[lcl])

            def f_cap(e, _ch=ch):
                e.tensor_scalar(XLOC[:, _ch, :], XLOC[:, _ch, :], RZ[:, 2:3], None, ALU.mult)
            lcap = sch.step("V", f_cap, after=[lcrs])

            def f_cth(e, _ch=ch):
                e.activation(ELOG[:, C * _ch:C * (_ch + 1)], XLOC[:, _ch, :], AF.Tanh)
            lch_done.append(sch.step("S", f_cth, after=[lcap]))

        if debug:
            sch.dma("qg", dbg["dbg_Xln"],
                    ELOG[:, 0:1024], after=lch_done)

        xlt = []
        for pr in range(4):
            ch, npp = pr // 2, pr % 2
            r0 = 64 * npp

            def f_xt(e, _ch=ch, _r0=r0):
                for dc in range(8):
                    e.transpose(PT[0:128, 128 * dc:128 * dc + 64],
                                ELOG[_r0:_r0 + 64, C * _ch + 128 * dc:C * _ch + 128 * (dc + 1)],
                                ID64[_r0:_r0 + 64, 0:64])
            lt1 = sch.step("T", f_xt, after=lch_done + G("pt"))

            def f_xte(e, _ch=ch, _npp=npp):
                for q in range(2):
                    n = 4 * _ch + 2 * _npp + q
                    e.copy(XLNT[:, :].rearrange("p (a s) -> p a s", s=PS)
                           [:, 8 * n:8 * (n + 1), :],
                           PT[:, :].rearrange("p (c s) -> p c s", s=128)
                           [:, :, 32 * q:32 * (q + 1)])
            lte = sch.step("S", f_xte, after=[lt1])
            setg("pt", lte)
            xlt.append(lte)

        yt_w = []
        lwe_prev = None
        for n in range(EPC):
            lwe = sch.dma("qg", EXPW[:, :, :],
                          expw_d[n].rearrange("(c p) k -> p c k", p=128),
                          after=xlt + yt_w[-1:] + prevX)
            lbe = sch.dma("qg", B_ex1[0:1, :], expb_d[n:n + 1, :], after=yt_w[-1:])

            def f_yt(e, _n=n):
                for half in range(2):
                    for dc in range(8):
                        e.matmul(PG[half][0:PS, 0:512],
                                 XLNT[:, :].rearrange("p (a s) -> p a s", s=PS)[:, 8 * _n + dc, :],
                                 EXPW[:, dc, 512 * half:512 * (half + 1)],
                                 start=(dc == 0), stop=False)
                    e.matmul(PG[half][0:PS, 0:512],
                             ONESR[0:1, 0:PS],
                             B_ex1[0:1, 512 * half:512 * (half + 1)],
                             start=False, stop=True)
            lyt = sch.step("T", f_yt, after=[lwe, lbe] + G("pg0") + G("pg1"))

            def f_ye(e):
                e.tensor_copy(YSTG[0:PS, 0:512], PG[0][0:PS, 0:512])
                e.tensor_copy(YSTG[0:PS, 512:1024], PG[1][0:PS, 0:512])
            lye = sch.step("V", f_ye, after=[lyt] + yt_w[-1:])
            setg("pg0", lye)
            setg("pg1", lye)
            yt_w.append(sch.dma("qy", d_Yl.ap()[PS * n:PS * (n + 1), :], YSTG[0:PS, :],
                                after=[lye]))

        def f_ag(e):
            e.collective_compute("AllGather", ALU.bypass,
                                 replica_groups=[list(range(NCORES))],
                                 ins=[d_Yl.ap().opt()], outs=[d_Yt.ap().opt()]).then_inc(ccsem, 1)
            e.wait_ge(ccsem, 2)
            e.nop()
        lag = sch.step("G", f_ag, after=yt_w)

        lytf = sch.dma("qy", W_phiT[:, :].rearrange("p (c k) -> p c k", k=C),
                       d_Yt.ap().rearrange("(c p) k -> p c k", p=128), after=[lag])
        if debug:
            sch.dma("qg", dbg["dbg_Yt"], W_phiT[:, 0:1024], after=[lytf])
            sch.dma("qg", dbg["dbg_YtF"], W_phiT[:, :], after=[lytf])

        if stop == "X":
            sch.step("Y", lambda e: None, after=[lytf])
            sch.check_deadlock()
            runs = {e: sch.runner(e) for e in Sched.ENGS}
            _emit(nc, runs)
            return nc

        out_dmas = []
        pend_o = [None]
        lc_prev = None
        for ti in range(NT):
            t0, tn = TILES[ti]
            lc1 = sch.dma("qy", CFM[:, :], d_C[ti, :, :],
                          after=[lytf, C_spill_by_tile[ti]] + G("cfm"))
            lx1b = sch.dma("qs", X1T[0:tn, :], d_x1[t0:t0 + tn, :],
                           after=[lytf, x1_spills[ti]] + G("x1t"))

            b0 = 0 if ti % 2 == 0 else 2
            b1 = b0 + 1
            so = 1024 * (ti % 2)
            rb = 4 * (ti % 2)

            def f_y(e, _tn=tn, _b0=b0, _b1=b1):
                for ec in range(16):
                    e.matmul(PG[_b0][0:_tn, 0:512],
                             CFM[:, :].rearrange("p (c w) -> p c w", w=124)[:, ec, 0:_tn],
                             W_phiT[:, C * ec:C * ec + 512],
                             start=(ec == 0), stop=(ec == 15))
                for ec in range(16):
                    e.matmul(PG[_b1][0:_tn, 0:512],
                             CFM[:, :].rearrange("p (c w) -> p c w", w=124)[:, ec, 0:_tn],
                             W_phiT[:, C * ec + 512:C * ec + 1024],
                             start=(ec == 0), stop=(ec == 15))
            ly = sch.step("T", f_y, after=[lc1, lytf] + G(f"pg{b0}") + G(f"pg{b1}"))
            setg("cfm", ly)

            def f_s2(e, _tn=tn, _b0=b0, _b1=b1, _so=so, _rb=rb):
                e.scalar_tensor_tensor(S2[0:_tn, _so:_so + 512], PG[_b0][0:_tn, 0:512], 1.0,
                                       X1T[0:_tn, 0:512], ALU.mult, ALU.add)
                e.scalar_tensor_tensor(S2[0:_tn, _so + 512:_so + 1024], PG[_b1][0:_tn, 0:512],
                                       1.0, X1T[0:_tn, 512:1024], ALU.mult, ALU.add)
                e.drain()
                e.reduce_sum(RZ[0:_tn, _rb:_rb + 1], S2[0:_tn, _so:_so + 1024], axis=AX.X)
                e.drain()
                e.tensor_scalar(RZ[0:_tn, _rb:_rb + 1], RZ[0:_tn, _rb:_rb + 1], 1.0 / C,
                                None, ALU.mult)
            ls2 = sch.step("V", f_s2, after=[ly, lx1b] + G(f"s2_{ti % 2}") + G("s2dbg2"))
            setg(f"pg{b0}", ls2)
            setg(f"pg{b1}", ls2)
            setg("x1t", ls2)

            if debug and ti == 0:
                setg("s2dbg2", sch.dma("qg", dbg["dbg_S2"], S2[0:128, 0:1024], after=[ls2]))

            def f_sq2(e, _tn=tn, _so=so, _rb=rb):
                e.activation(YSTG[0:_tn, :], S2[0:_tn, _so:_so + 1024], AF.Square,
                             accum_out=RZ[0:_tn, _rb + 1:_rb + 2])
            lsq2 = sch.step("S", f_sq2, after=[ls2] + G("junk"))

            def f_v2(e, _tn=tn, _rb=rb):
                e.tensor_scalar(RZ[0:_tn, _rb + 1:_rb + 2], RZ[0:_tn, _rb + 1:_rb + 2],
                                1.0 / C, None, ALU.mult)
                e.tensor_tensor(RZ[0:_tn, _rb + 2:_rb + 3], RZ[0:_tn, _rb:_rb + 1],
                                RZ[0:_tn, _rb:_rb + 1], ALU.mult)
                e.drain()
                e.tensor_tensor(RZ[0:_tn, _rb + 1:_rb + 2], RZ[0:_tn, _rb + 1:_rb + 2],
                                RZ[0:_tn, _rb + 2:_rb + 3], ALU.subtract)
            lv2 = sch.step("V", f_v2, after=[lsq2])

            def f_r2(e, _tn=tn, _rb=rb):
                e.activation(RZ[0:_tn, _rb + 3:_rb + 4], RZ[0:_tn, _rb + 1:_rb + 2], AF.Ln,
                             bias=EPSC[0:_tn, 0:1])
                e.drain()
                e.activation(RZ[0:_tn, _rb + 3:_rb + 4], RZ[0:_tn, _rb + 3:_rb + 4], AF.Exp,
                             scale=-0.5)
            lr2 = sch.step("S", f_r2, after=[lv2])
            setg("junk", lr2)

            def f_o(e, _tn=tn, _so=so, _rb=rb):
                e.tensor_scalar(YOUT[0:_tn, :], S2[0:_tn, _so:_so + 1024],
                                RZ[0:_tn, _rb:_rb + 1], RZ[0:_tn, _rb + 3:_rb + 4],
                                ALU.subtract, ALU.mult)
            lo = sch.step("V", f_o, after=[lr2] + G("yout") + G("odbg"))
            setg(f"s2_{ti % 2}", lo)

            if debug and ti == 0:
                setg("odbg", sch.dma("qg", dbg["dbg_O"], YOUT[0:128, :], after=[lo]))
            od = sch.dma("qy", out_d[t0:t0 + tn, :], YOUT[0:tn, :], after=[lo])
            setg("yout", od)
            out_dmas.append(od)

        if pend_o[0] is not None:
            emit_o(*pend_o[0])
        sch.step("Y", lambda e: None, after=out_dmas)

        sch.check_deadlock()
        runs = {e: sch.runner(e) for e in Sched.ENGS}
        _emit(nc, runs)

    return nc


def _emit(nc, runs):
    with nc.Block() as block:
        @block.tensor
        def _(t):
            runs["T"](t)

        @block.vector
        def _(v):
            runs["V"](v)

        @block.scalar
        def _(s):
            runs["S"](s)

        @block.gpsimd
        def _(g):
            runs["G"](g)

        @block.sync
        def _(sy):
            runs["Y"](sy)


def _build_consts():
    bf = ml_dtypes.bfloat16
    ident = np.eye(128, dtype=np.float32).astype(bf)
    ident64 = np.tile(np.eye(64, dtype=np.float32), (2, 1)).astype(bf)
    onescol = np.ones((128, 1), np.float32).astype(bf)
    onesrow = np.ones((1, 128), np.float32).astype(bf)
    maskL = np.zeros((3, 128, TT), np.float32)
    maskR = np.zeros((3, 128, TT), np.float32)
    for mv, (t0, tn) in ((0, TILES[1]), (1, TILES[0]), (2, TILES[-1])):
        for tp in range(tn):
            t = t0 + tp
            p = tp // 2
            gp = t // 2
            if gp - 1 >= 0:
                maskL[mv, 2 * p:2 * p + 4, tp] = 1.0
            if gp <= (L - 4) // 2:
                maskR[mv, 2 * p + 2:2 * p + 6, tp] = 1.0
    cnt = np.zeros((1, L), np.float32)
    W = (L - 4) // 2 + 1
    idx = (np.arange(W)[:, None] * 2 + np.arange(4)[None, :]).reshape(-1)
    np.add.at(cnt[0], idx, 1.0)
    return dict(ident=ident, ident64=ident64, onescol=onescol, onesrow=onesrow,
                maskL=maskL.astype(bf), maskR=maskR.astype(bf), cnt=cnt.astype(bf))


_NC_CACHE = {}


def kernel(L0_attn_x, L0_attn_w_qkv, L0_attn_w_out, L0_attn_b_out,
           L0_ffn_phi_w, L0_ffn_phi_b, L0_ffn_expert_w, L0_ffn_expert_b,
           debug=False, stop="full"):
    x = np.asarray(L0_attn_x, dtype=np.float32)
    wqkv = np.ascontiguousarray(np.asarray(L0_attn_w_qkv, dtype=np.float32))
    wout = np.ascontiguousarray(np.asarray(L0_attn_w_out, dtype=np.float32))
    bout = np.asarray(L0_attn_b_out, dtype=np.float32).reshape(1, C)
    phiT = np.ascontiguousarray(np.asarray(L0_ffn_phi_w, dtype=np.float32).T)
    phib = np.asarray(L0_ffn_phi_b, dtype=np.float32).reshape(1, E2)
    expw = np.ascontiguousarray(np.asarray(L0_ffn_expert_w, dtype=np.float32))
    expb = np.ascontiguousarray(np.asarray(L0_ffn_expert_b, dtype=np.float32))

    consts = _build_consts()
    key = ("nc", debug, stop)
    if key not in _NC_CACHE:
        _NC_CACHE[key] = build_nc(debug=debug, stop=stop)
    nc = _NC_CACHE[key]

    in_maps = []
    for b in range(NCORES):
        in_maps.append(dict(
            xT=np.ascontiguousarray(x[b].T),
            wqkv=wqkv, wout=wout, bout=bout, phiT=phiT, phib=phib,
            expw=np.ascontiguousarray(expw[b * EPC:(b + 1) * EPC]),
            expb=np.ascontiguousarray(expb[b * EPC:(b + 1) * EPC]),
            ident=consts["ident"], ident64=consts["ident64"],
            onescol=consts["onescol"], onesrow=consts["onesrow"],
            maskL=consts["maskL"], maskR=consts["maskR"], cnt=consts["cnt"],
        ))

    res = run_bass_kernel_spmd(nc, in_maps, core_ids=list(range(NCORES)))
    out = np.stack([np.asarray(res.results[b]["out"], dtype=np.float32)
                    for b in range(NCORES)])
    if debug:
        return out, res
    return out


# version 08
# speedup vs baseline: 1.0903x; 1.0903x over previous
import numpy as np
import ml_dtypes

import concourse.bass as bass
import concourse.mybir as mybir
from concourse.bass_utils import run_bass_kernel_spmd

F32 = mybir.dt.float32
BF16 = mybir.dt.bfloat16
AF = mybir.ActivationFunctionType
ALU = mybir.AluOpType
AX = mybir.AxisListType

NCORES = 8
L = 4096
C = 1024
HEADS = 16
NE = 64
PS = 32
E2 = NE * PS
EPC = NE // NCORES

TT = 122
TILES = [(i * TT, TT) for i in range(L // TT)] + [(L - L % TT, L % TT)]
NT = len(TILES)
GROUPS = [list(range(i, min(i + 4, NT))) for i in range(0, NT, 4)]
HALO = 4
SCALE = 1.0 / 32.0
USE_GP_TT = False


class Sched:
    ENGS = ("T", "V", "S", "G", "Y")
    QENG = {"qy": "Y", "qs": "S", "qg": "G"}

    NROT = 8
    EROT = 4

    def __init__(self):
        self.prog = {e: [] for e in self.ENGS}
        self.cnt = {e: 0 for e in self.ENGS}
        self.dcnt = {(q, j): 0 for q in ("qy", "qs", "qg") for j in range(self.NROT)}
        self.didx = {q: 0 for q in ("qy", "qs", "qg")}
        self.sems = None
        self.dsems = None

    def step(self, eng, fn, after=()):
        deps = []
        for d in after:
            if d is None:
                continue
            key, val = d
            if isinstance(key, tuple):
                deps.append((key, self.dcnt[key]))
            else:
                deps.append(d)
        self.prog[eng].append((tuple(deps), fn))
        self.cnt[eng] += 1
        return (eng, self.cnt[eng])

    def dma(self, q, out_ap, in_ap, after=(), **kw):
        sch = self
        j = self.didx[q] % self.NROT
        self.didx[q] += 1

        def fn(e, _o=out_ap, _i=in_ap, _kw=dict(kw), _k=(q, j)):
            e.dma_start(out=_o, in_=_i, **_kw).then_inc(sch.dsems[_k], 16)
        fn._dmaq = (q, j)
        self.step(self.QENG[q], fn, after=after)
        self.dcnt[(q, j)] += 16
        return ((q, j), self.dcnt[(q, j)])

    def dma_tr(self, q, out_ap, in_ap, after=()):
        sch = self
        j = self.didx[q] % self.NROT
        self.didx[q] += 1

        def fn(e, _o=out_ap, _i=in_ap, _k=(q, j)):
            e.dma_start_transpose(_o, _i).then_inc(sch.dsems[_k], 16)
        fn._dmaq = (q, j)
        self.step(self.QENG[q], fn, after=after)
        self.dcnt[(q, j)] += 16
        return ((q, j), self.dcnt[(q, j)])

    def check_deadlock(self):
        pos = {e: 0 for e in self.ENGS}
        done = {e: 0 for e in self.ENGS}
        dmadone = {k: 0 for k in self.dcnt}
        progressed = True
        while progressed:
            progressed = False
            for e in self.ENGS:
                while pos[e] < len(self.prog[e]):
                    waits, fn = self.prog[e][pos[e]]
                    ok = True
                    for key, val in waits:
                        if key == e:
                            continue
                        if isinstance(key, tuple):
                            if dmadone[key] < val:
                                ok = False
                                break
                        elif done[key] < val:
                            ok = False
                            break
                    if not ok:
                        break
                    pos[e] += 1
                    done[e] += 1
                    dk = getattr(fn, "_dmaq", None)
                    if dk:
                        dmadone[dk] += 16
                    progressed = True
        stuck = {e: (pos[e], len(self.prog[e])) for e in self.ENGS if pos[e] < len(self.prog[e])}
        if stuck:
            detail = {}
            for e in stuck:
                waits, fn = self.prog[e][pos[e]]
                detail[e] = (pos[e], waits[:6])
            raise RuntimeError(f"DEADLOCK: stuck={stuck} blocking_waits={detail}")

    def runner(self, ename):
        chunks = self.prog[ename]
        sch = self

        class Proxy:
            __slots__ = ("_e", "last")

            def __init__(self, e):
                self._e = e
                self.last = None

            def __getattr__(self, n):
                a = getattr(self._e, n)
                if callable(a):
                    px = self

                    def w(*args, **kw):
                        r = a(*args, **kw)
                        if r is not None and hasattr(r, "then_inc"):
                            px.last = r
                        return r
                    return w
                return a

        def run(e):
            px = Proxy(e)
            wm = {}
            nout = 0
            R = sch.EROT
            for waits, fn in chunks:
                for key, val in waits:
                    if key == ename:
                        continue
                    if isinstance(key, tuple):
                        if wm.get(key, 0) >= val:
                            continue
                        e.wait_ge(sch.dsems[key], val)
                        wm[key] = val
                    else:
                        j = (val - 1) % R
                        v = (val - 1) // R + 1
                        wk = (key, j)
                        if wm.get(wk, 0) >= v:
                            continue
                        e.wait_ge(sch.sems[(key, j)], v)
                        wm[wk] = v
                px.last = None
                fn(px)
                sem = sch.sems[(ename, nout % R)]
                nout += 1
                if getattr(fn, "_dmaq", None) is not None or px.last is None:
                    e.nop().then_inc(sem, 1)
                else:
                    px.last.then_inc(sem, 1)
        return run


class Arena:

    def __init__(self, nc_sbt, name, dtype):
        self.sbt = nc_sbt
        self.name = name
        self.dtype = dtype
        self.regions = {}
        self.hwm = 0

    def alloc(self, tag, cols):
        cur = self.regions.get(tag, 0)
        off = cur
        self.regions[tag] = cur + cols
        self.hwm = max(self.hwm, cur + cols)
        return off


def build_nc(debug=False, stop="full"):
    nc = bass.Bass("TRN2", target_bir_lowering=False, debug=False)

    xT = nc.dram_tensor("xT", [C, L], F32, kind="ExternalInput").ap()
    wqkv_d = nc.dram_tensor("wqkv", [C, 3 * C], F32, kind="ExternalInput").ap()
    wout_d = nc.dram_tensor("wout", [C, C], F32, kind="ExternalInput").ap()
    bout_d = nc.dram_tensor("bout", [1, C], F32, kind="ExternalInput").ap()
    phiT_d = nc.dram_tensor("phiT", [C, E2], F32, kind="ExternalInput").ap()
    phib_d = nc.dram_tensor("phib", [1, E2], F32, kind="ExternalInput").ap()
    expw_d = nc.dram_tensor("expw", [EPC, C, C], F32, kind="ExternalInput").ap()
    expb_d = nc.dram_tensor("expb", [EPC, C], F32, kind="ExternalInput").ap()
    ident_d = nc.dram_tensor("ident", [128, 128], BF16, kind="ExternalInput").ap()
    ident64_d = nc.dram_tensor("ident64", [128, 64], BF16, kind="ExternalInput").ap()
    onescol_d = nc.dram_tensor("onescol", [128, 1], BF16, kind="ExternalInput").ap()
    onesrow_d = nc.dram_tensor("onesrow", [1, 128], BF16, kind="ExternalInput").ap()
    maskL_d = nc.dram_tensor("maskL", [3, 128, TT], BF16, kind="ExternalInput").ap()
    maskR_d = nc.dram_tensor("maskR", [3, 128, TT], BF16, kind="ExternalInput").ap()
    cnt_d = nc.dram_tensor("cnt", [1, L], BF16, kind="ExternalInput").ap()

    out_d = nc.dram_tensor("out", [L, C], F32, kind="ExternalOutput").ap()

    d_x1 = nc.dram_tensor("d_x1", [L, C], BF16).ap()
    d_D = nc.dram_tensor("d_D", [L, E2], BF16).ap()
    d_C = nc.dram_tensor("d_C", [NT, 128, 16 * 124], BF16).ap()
    d_Xp = nc.dram_tensor("d_Xp", [E2, C], F32)
    d_Xl = nc.dram_tensor("d_Xl", [E2 // NCORES, C], F32)
    d_Yl = nc.dram_tensor("d_Yl", [E2 // NCORES, C], BF16)
    d_Yt = nc.dram_tensor("d_Yt", [E2, C], BF16)

    dbg = {}
    if debug:
        for name, shape, dt in [
            ("dbg_qk", [128, 496], F32), ("dbg_E", [128, 488], F32),
            ("dbg_EL", [128, 488], F32), ("dbg_vo", [128, 1056], F32),
            ("dbg_acc", [128, 1024], F32), ("dbg_s1", [128, 488], F32),
            ("dbg_x1", [128, 488], F32),
            ("dbg_elog", [128, 2048], F32), ("dbg_D", [128, 2048], F32),
            ("dbg_C", [128, 2048], F32), ("dbg_cfm", [128, 1984], F32),
            ("dbg_Xp", [128, 1024], F32), ("dbg_Xln", [128, 1024], F32),
            ("dbg_Yt", [128, 1024], F32), ("dbg_S2", [128, 1024], F32),
            ("dbg_YtF", [128, 16384], F32),
        ]:
            dbg[name] = nc.dram_tensor(name, shape, dt, kind="ExternalOutput").ap()
        for name, shape, dt in [
            ("dbg_dD", [L, E2], BF16), ("dbg_dx1", [L, C], BF16),
            ("dbg_XpL", [E2, C], F32),
            ("dbg_O", [128, 1024], F32),
        ]:
            dbg[name] = nc.dram_tensor(name, shape, dt, kind="ExternalOutput").ap()

    import contextlib
    stack = contextlib.ExitStack()
    sbt = lambda name, shape, dt: stack.enter_context(nc.sbuf_tensor(name, shape, dt))
    pst = lambda name, shape, dt=F32: stack.enter_context(nc.psum_tensor(name, shape, dt))

    with stack:
        WXB = sbt("WXB", [128, 8 * 3 * C * 2 + 8 * C * 2], mybir.dt.uint8)
        _wxb_addr = nc.lookup_mloc(WXB).addr

        def _overlay(name, shape, dtype, off):
            return nc.alloc_sbuf_tensor_at(name, shape, dtype,
                                           offset=_wxb_addr + off)

        W_qkv = _overlay("W_qkv", [128, 8 * 3 * C], BF16, 0)
        W_out = _overlay("W_out", [128, 8 * C], BF16, 8 * 3 * C * 2)
        XACC = _overlay("XACC", [128, 16 * C], F32, 0)
        W_phiT = sbt("W_phiT", [128, 8 * E2], BF16)
        B_out = sbt("B_out", [1, C], BF16)
        B_phi = sbt("B_phi", [1, E2], BF16)
        B_ex1 = sbt("B_ex1", [1, C], BF16)
        IDN = sbt("IDN", [128, 128], BF16)
        ID64 = sbt("ID64", [128, 64], BF16)
        ONESC = sbt("ONESC", [128, 1], BF16)
        ONESR = sbt("ONESR", [1, 128], BF16)
        MKL = sbt("MKL", [128, 3 * TT], BF16)
        MKR = sbt("MKR", [128, 3 * TT], BF16)
        CNTG = sbt("CNTG", [1, 512], BF16)
        EPSC = sbt("EPSC", [128, 1], F32)

        ab = Arena(None, "AB", BF16)
        o_QK = ab.alloc("A", 16 * 496)
        o_XH = ab.alloc("A", 8 * 496)
        o_S1 = ab.alloc("A", 8 * 488)
        o_X1F = ab.alloc("A", 8 * 512)
        o_ACCF = ab.alloc("A", 8 * 488)
        o_DST = ab.alloc("X", 7 * E2)
        o_X1S = ab.alloc("X", 7 * C)
        AB = sbt("AB", [128, ab.hwm], BF16)
        QK = AB[:, o_QK:o_QK + 16 * 496].rearrange("p (c t) -> p c t", t=496)
        XH = AB[:, o_XH:o_XH + 8 * 496].rearrange("p (c t) -> p c t", t=496)
        S1 = AB[:, o_S1:o_S1 + 8 * 488].rearrange("p (c t) -> p c t", t=488)
        X1F = AB[:, o_X1F:o_X1F + 8 * 512].rearrange("p (c t) -> p c t", t=512)
        ACCF = AB[:, o_ACCF:o_ACCF + 8 * 488].rearrange("p (c t) -> p c t", t=488)
        DST = AB[:, o_DST:o_DST + 7 * E2].rearrange("p (j e) -> p j e", e=E2)
        X1S = AB[:, o_X1S:o_X1S + 7 * C].rearrange("p (j e) -> p j e", e=C)
        EXPW = AB[:, o_DST:o_DST + 8 * C].rearrange("p (j e) -> p j e", e=C)

        af = Arena(None, "AF", F32)
        o_TMPX = af.alloc("A", 2 * 488)
        o_TMPL = af.alloc("A", 2 * 256)
        o_TMPR = af.alloc("A", 2 * 256)
        o_XLOC = af.alloc("X", 2 * C)
        o_XSTG = af.alloc("X", 1024)
        o_S2 = af.alloc("B", 2 * 1024)
        o_YOUT = af.alloc("B", 1024)
        AF_ = sbt("AF_", [128, af.hwm], F32)
        TMPX = AF_[:, o_TMPX:o_TMPX + 2 * 488]
        TMPL = AF_[:, o_TMPL:o_TMPL + 2 * 256]
        TMPR = AF_[:, o_TMPR:o_TMPR + 2 * 256]
        XLOC = AF_[:, o_XLOC:o_XLOC + 2 * C].rearrange("p (c t) -> p c t", t=C)
        XSTG = AF_[:, o_XSTG:o_XSTG + 1024]
        S2 = AF_[:, o_S2:o_S2 + 2 * 1024]
        YOUT = AF_[:, o_YOUT:o_YOUT + 1024]

        VO = sbt("VO", [128, 16 * 66], BF16)
        EE = sbt("EE", [128, 2 * 4 * TT], BF16)
        EL = sbt("EL", [128, 2 * 4 * TT], BF16)
        ER = sbt("ER", [128, 2 * 4 * TT], BF16)
        ACCT = sbt("ACCT", [128, 2048], BF16)
        RZ = sbt("RZ", [128, 8], F32)
        ELOG = sbt("ELOG", [128, E2], BF16)
        RP = sbt("RP", [128, NE], F32)
        RN = sbt("RN", [128, PS], F32)
        DTM = sbt("DTM", [128, E2], BF16)
        CTM = sbt("CTM", [128, 2 * E2], BF16)
        CFM = sbt("CFM", [128, 16 * 124], BF16)
        X1T = sbt("X1T", [128, C], BF16)
        XLNT = AB[:, o_X1S:o_X1S + EPC * 8 * PS]
        YSTG = AB[:, o_X1S + EPC * 8 * PS:o_X1S + EPC * 8 * PS + C]
        ROWA = sbt("ROWA", [1, 512], F32)
        ROWB = sbt("ROWB", [1, 512], F32)
        ROWC = sbt("ROWC", [1, 512], F32)
        RBA = sbt("RBA", [1, 512], BF16)
        RBB = sbt("RBB", [1, 512], BF16)

        PG = [pst(f"PG{i}", [128, 512]) for i in range(7)]
        PT = pst("PT", [128, 1024], BF16)

        sems = {(e, j): stack.enter_context(nc.semaphore(f"sem{e}{j}"))
                for e in Sched.ENGS for j in range(Sched.EROT)}
        dsems = {(q, j): stack.enter_context(nc.semaphore(f"dma{q}{j}"))
                 for q in ("qy", "qs", "qg") for j in range(Sched.NROT)}
        ccsem = stack.enter_context(nc.semaphore("ccsem"))

        sch = Sched()
        sch.sems = sems
        sch.dsems = dsems

        guard = {}

        def G(key):
            return list(guard.get(key, ()))

        def setg(key, *labels):
            guard[key] = list(labels)

        def addg(key, *labels):
            guard.setdefault(key, []).extend(labels)

        p0 = []
        p0.append(sch.dma("qg", IDN[:, :], ident_d))
        p0.append(sch.dma("qg", ID64[:, :], ident64_d))
        p0.append(sch.dma("qg", ONESC[:, :], onescol_d))
        p0.append(sch.dma("qg", ONESR[:, :], onesrow_d))
        p0.append(sch.dma("qg", MKL[:, :].rearrange("p (n t) -> p n t", t=TT),
                          maskL_d.rearrange("n p t -> p n t")))
        p0.append(sch.dma("qg", MKR[:, :].rearrange("p (n t) -> p n t", t=TT),
                          maskR_d.rearrange("n p t -> p n t")))
        p0.append(sch.dma("qg", B_out[:, :], bout_d))
        p0.append(sch.dma("qg", B_phi[:, :], phib_d))
        p0.append(sch.dma("qg", W_qkv[:, :].rearrange("p (c k) -> p c k", k=3 * C),
                          wqkv_d.rearrange("(c p) k -> p c k", p=128)))
        p0.append(sch.dma("qg", W_out[:, :].rearrange("p (c k) -> p c k", k=C),
                          wout_d.rearrange("(c p) k -> p c k", p=128)))
        p0.append(sch.dma("qg", W_phiT[:, :].rearrange("p (c k) -> p c k", k=E2),
                          phiT_d.rearrange("(c p) k -> p c k", p=128)))

        def f_init(e):
            e.memset(EPSC[:, :], 1e-5)
            e.memset(EE[:, :], 0.0)
            e.memset(VO[:, :], 1.0)
            e.memset(AB[:, :], 0.0)
            e.memset(AF_[:, :], 0.0)
        l_init = sch.step("V", f_init)

        def f_init2(e):
            for t in (EL, ER, ACCT, ELOG, RP, RN, DTM, CTM, CFM, X1T,
                      RZ, ROWA, ROWB, ROWC, RBA, RBB):
                e.memset(t[:, :], 0.0)
        l_init2 = sch.step("G", f_init2)
        consts = p0 + [l_init, l_init2]

        if stop == "P0":
            sch.step("Y", lambda e: None, after=consts)
            sch.check_deadlock()
            _emit(nc, {e: sch.runner(e) for e in Sched.ENGS})
            return nc

        x1_spills = []
        D_spills = []
        C_spills = []
        prev_group = list(consts)
        pend_ct = [None]
        C_spill_by_tile = {}

        def emit_ct(ti2, tn2, cp2, lC2):
            cts = []
            for rnd in range(2):
                def f_ct(e, _rnd=rnd, _tn=tn2, _cp=cp2):
                    for cc in range(8):
                        e.transpose(PT[0:128, 128 * cc:128 * cc + _tn],
                                    CTM[0:_tn, E2 * _cp + 128 * (8 * _rnd + cc):
                                        E2 * _cp + 128 * (8 * _rnd + cc + 1)],
                                    IDN[0:_tn, 0:_tn])
                lct = sch.step("T", f_ct, after=[lC2] + G("pt"))

                def f_cev(e, _rnd=rnd, _tn=tn2):
                    e.copy(CFM[:, :].rearrange("p (c w) -> p c w", w=124)
                           [:, 8 * _rnd:8 * (_rnd + 1), 0:_tn],
                           PT[:, :].rearrange("p (c t) -> p c t", t=128)[:, :, 0:_tn])
                lcev = sch.step("S", f_cev, after=[lct] + G("cfm"))
                setg("pt", lcev)
                cts.append(lcev)
            addg(f"ctm{cp2}", cts[-1])
            spC = sch.dma("qy", d_C[ti2, :, :], CFM[:, :], after=cts)
            setg("cfm", spC)
            C_spill_by_tile[ti2] = spC
            C_spills.append(spC)
            return [spC]

        for gi, group in enumerate(GROUPS):
            g0 = TILES[group[0]][0]
            gn = sum(TILES[t][1] for t in group)
            h0, hn = g0 - HALO, gn + 2 * HALO

            c0, c1 = max(h0, 0), min(h0 + hn, L)
            off = c0 - h0
            lh = sch.dma("qg", XH[:, :, off:off + (c1 - c0)],
                         xT.rearrange("(c p) t -> p c t", p=128)[:, :, c0:c1],
                         after=prev_group)
            lcnt = sch.dma("qg", CNTG[0:1, 0:gn], cnt_d[0:1, g0:g0 + gn],
                           after=prev_group)
            xh_ready = [lh]
            if off > 0 or off + (c1 - c0) < hn:
                def f_pad(e, _off=off, _w=c1 - c0, _hn=hn):
                    if _off > 0:
                        e.memset(XH[:, :, 0:_off], 0.0)
                    if _off + _w < _hn:
                        e.memset(XH[:, :, _off + _w:_hn], 0.0)
                xh_ready.append(sch.step("G", f_pad, after=[lh]))

            qk_ready = []
            for qc in range(16):
                bank = qc % 2

                def f_qk(e, _qc=qc, _hn=hn, _bank=bank):
                    for dc in range(8):
                        e.matmul(PG[_bank][:, 0:_hn],
                                 W_qkv[:, 3 * C * dc + 128 * _qc:3 * C * dc + 128 * (_qc + 1)],
                                 XH[:, dc, 0:_hn],
                                 start=(dc == 0), stop=(dc == 7))
                lm = sch.step("T", f_qk, after=xh_ready + G(f"pg{bank}"))

                def f_ev(e, _qc=qc, _hn=hn, _bank=bank):
                    e.copy(QK[:, _qc, 0:_hn], PG[_bank][:, 0:_hn])
                ev = sch.step("S", f_ev, after=[lm])
                setg(f"pg{bank}", ev)
                qk_ready.append(ev)
            qk_done = qk_ready[-2:]

            if debug and gi == 0:
                sch.dma("qg", dbg["dbg_qk"], QK[:, 8, 0:496], after=qk_ready)

            if stop == "G0QK" and gi == 0:
                sch.step("Y", lambda e: None, after=qk_ready)
                sch.check_deadlock()
                _emit(nc, {e: sch.runner(e) for e in Sched.ENGS})
                return nc

            s1_ready = []
            acc_done = []
            for ti in group:
                t0, tn = TILES[ti]
                kn = min(tn + 6, 128)
                tloc = t0 - g0
                koff = tloc - 2 + HALO
                qoff = tloc + HALO
                mvar = 1 if ti == 0 else (2 if ti == NT - 1 else 0)

                lvev = None
                for half in range(2):
                    def f_v(e, _koff=koff, _kn=kn, _half=half):
                        for dc in range(8):
                            e.matmul(PG[6][0:_kn, 0:512],
                                     XH[:, dc, _koff:_koff + _kn],
                                     W_qkv[:, 3 * C * dc + 2048 + 512 * _half:
                                           3 * C * dc + 2048 + 512 * (_half + 1)],
                                     start=(dc == 0), stop=(dc == 7))
                    lv = sch.step("T", f_v, after=xh_ready + qk_done + G("pg6"))

                    def f_vev(e, _kn=kn, _half=half):
                        e.copy(
                            VO[0:_kn, 8 * 66 * _half:8 * 66 * (_half + 1)]
                            .rearrange("p (h c) -> p h c", c=66)[:, :, 0:64],
                            PG[6][0:_kn, 0:512].rearrange("p (h c) -> p h c", c=64))
                    lvev = sch.step("S", f_vev,
                                    after=[lv] + (G("vo") + G("vodbg") if half == 0 else []))
                    setg("pg6", lvev)

                if debug and ti == 0:
                    setg("vodbg", sch.dma("qg", dbg["dbg_vo"], VO[:, 0:1056], after=[lvev]))

                if stop == "T0V" and gi == 0 and ti == 0:
                    sch.step("Y", lambda e: None, after=[lvev])
                    sch.check_deadlock()
                    _emit(nc, {e: sch.runner(e) for e in Sched.ENGS})
                    return nc
                W4 = 4 * TT
                acctp = ti % 2

                def emit_u_cmb(hb, lm_pair, _tn=tn, _acctp=acctp):
                    heads = tuple(range(4 * hb, 4 * hb + 4))
                    p = hb % 2

                    def f_u(e, _heads=heads, _p=p, _tn2=_tn):
                        perm = (0, 2, 1, 3)
                        for j, h in enumerate(_heads):
                            cp = perm[j]
                            e.matmul(PG[4][0:_tn2, 65 * j:65 * j + 65],
                                     EL[:, W4 * _p + TT * cp:W4 * _p + TT * cp + _tn2],
                                     VO[:, 66 * h:66 * h + 65],
                                     start=True, stop=True)
                            e.matmul(PG[5][0:_tn2, 65 * j:65 * j + 65],
                                     ER[:, W4 * _p + TT * cp:W4 * _p + TT * cp + _tn2],
                                     VO[:, 66 * h:66 * h + 65],
                                     start=True, stop=True)
                    lu = sch.step("T", f_u, after=list(lm_pair) + G("pg4") + G("pg5"))
                    setg(f"lastu{p}", lu)

                    def f_cmb(e, _hb=hb, _tn2=_tn, _ap=_acctp):
                        zL = PG[4][0:_tn2, 0:260].rearrange("p (s c) -> p s c", c=65)[:, :, 64:65]
                        zR = PG[5][0:_tn2, 0:260].rearrange("p (s c) -> p s c", c=65)[:, :, 64:65]
                        e.tensor_scalar(RZ[0:_tn2, 0:4].unsqueeze(2), zL, 1e-30, None, ALU.add)
                        e.tensor_scalar(RZ[0:_tn2, 4:8].unsqueeze(2), zR, 1e-30, None, ALU.add)
                        e.drain()
                        e.reciprocal(RZ[0:_tn2, 0:8], RZ[0:_tn2, 0:8])
                        e.drain()
                        uL = PG[4][0:_tn2, 0:260].rearrange("p (s c) -> p s c", c=65)[:, :, 0:64]
                        uR = PG[5][0:_tn2, 0:260].rearrange("p (s c) -> p s c", c=65)[:, :, 0:64]
                        rL = RZ[0:_tn2, 0:4].unsqueeze(2).broadcast_to([_tn2, 4, 64])
                        rR = RZ[0:_tn2, 4:8].unsqueeze(2).broadcast_to([_tn2, 4, 64])
                        o = 256 * (_hb % 2)
                        e.tensor_tensor(TMPL[0:_tn2, o:o + 256]
                                        .rearrange("p (s c) -> p s c", c=64),
                                        uL, rL, ALU.mult)
                        e.tensor_tensor(TMPR[0:_tn2, o:o + 256]
                                        .rearrange("p (s c) -> p s c", c=64),
                                        uR, rR, ALU.mult)
                        e.drain()
                        e.tensor_tensor(ACCT[0:_tn2, 1024 * _ap + 256 * _hb:
                                             1024 * _ap + 256 * (_hb + 1)],
                                        TMPL[0:_tn2, o:o + 256], TMPR[0:_tn2, o:o + 256],
                                        ALU.add)
                    lc = sch.step("V", f_cmb, after=[lu] + G(f"acct{_acctp}"))
                    setg("pg4", lc)
                    setg("pg5", lc)
                    return lu, lc

                pend = None
                batch_last = None
                lu_prev = None
                for hb in range(4):
                    heads = tuple(range(4 * hb, 4 * hb + 4))
                    p = hb % 2

                    def f_gram(e, _heads=heads, _kn=kn, _tn=tn, _koff=koff, _qoff=qoff):
                        for j, h in enumerate(_heads):
                            poff = 64 * (h % 2)
                            bank = h % 2
                            slot = j // 2
                            e.matmul(PG[bank][0:_kn, TT * slot:TT * slot + _tn],
                                     QK[poff:poff + 64, 8 + h // 2, _koff:_koff + _kn],
                                     QK[poff:poff + 64, h // 2, _qoff:_qoff + _tn],
                                     start=True, stop=True, tile_position=(poff, 0))
                    lg = sch.step("T", f_gram, after=[lvev] + qk_done + G("pg0") + G("pg1"))

                    if pend is not None:
                        lu_prev, batch_last = emit_u_cmb(*pend)

                    def f_exp(e, _kn=kn, _p=p):
                        e.activation(EE[0:_kn, W4 * _p:W4 * _p + 2 * TT],
                                     PG[0][0:_kn, 0:2 * TT], AF.Exp, scale=SCALE)
                        e.activation(EE[0:_kn, W4 * _p + 2 * TT:W4 * _p + 4 * TT],
                                     PG[1][0:_kn, 0:2 * TT], AF.Exp, scale=SCALE)
                    le = sch.step("S", f_exp, after=[lg] + G(f"lastm{p}") + G("eedbg"))
                    setg("pg0", le)
                    setg("pg1", le)

                    def f_mL(e, _mv=mvar, _tn=tn, _p=p):
                        src = MKL[:, :].rearrange("p (n t) -> p n t", t=TT)[:, _mv, 0:_tn]
                        e.tensor_tensor(
                            EL[:, W4 * _p:W4 * (_p + 1)]
                            .rearrange("p (i t) -> p i t", t=TT)[:, :, 0:_tn],
                            EE[:, W4 * _p:W4 * (_p + 1)]
                            .rearrange("p (i t) -> p i t", t=TT)[:, :, 0:_tn],
                            src.unsqueeze(1).broadcast_to([128, 4, _tn]), ALU.mult)
                    lmL = sch.step("V", f_mL, after=[le] + G(f"lastu{p}") + G("eldbg"))

                    def f_mR(e, _mv=mvar, _tn=tn, _p=p):
                        src = MKR[:, :].rearrange("p (n t) -> p n t", t=TT)[:, _mv, 0:_tn]
                        e.tensor_tensor(
                            ER[:, W4 * _p:W4 * (_p + 1)]
                            .rearrange("p (i t) -> p i t", t=TT)[:, :, 0:_tn],
                            EE[:, W4 * _p:W4 * (_p + 1)]
                            .rearrange("p (i t) -> p i t", t=TT)[:, :, 0:_tn],
                            src.unsqueeze(1).broadcast_to([128, 4, _tn]), ALU.mult)
                    lmR = sch.step("V", f_mR, after=[le] + G(f"lastu{p}"))
                    setg(f"lastm{p}", lmL, lmR)

                    if debug and ti == 0 and hb == 0:
                        setg("eedbg", sch.dma("qg", dbg["dbg_E"], EE[:, 0:488], after=[lmL, lmR]))
                        setg("eldbg", sch.dma("qg", dbg["dbg_EL"], EL[:, 0:488], after=[lmL, lmR]))

                    pend = (hb, (lmL, lmR))
                lu_prev, batch_last = emit_u_cmb(*pend)

                def f_tr(e, _tn=tn, _ap=acctp):
                    for cc in range(8):
                        e.transpose(PT[0:128, 128 * cc:128 * cc + _tn],
                                    ACCT[0:_tn, 1024 * _ap + 128 * cc:
                                         1024 * _ap + 128 * (cc + 1)],
                                    IDN[0:_tn, 0:_tn])
                ltr = sch.step("T", f_tr, after=[batch_last] + G("pt"))
                setg(f"acct{acctp}", ltr)

                def f_trev(e, _tn=tn, _tloc=tloc):
                    e.copy(ACCF[:, :, _tloc:_tloc + _tn],
                           PT[:, :].rearrange("p (c t) -> p c t", t=128)[:, :, 0:_tn])
                ltrev = sch.step("S", f_trev, after=[ltr] + G("accdbg"))
                setg("pt", ltrev)
                setg("vo", lu_prev)
                acc_done.append(ltrev)

                if debug and ti == 0:
                    setg("accdbg", sch.dma("qg", dbg["dbg_acc"],
                         ACCF[:, :, :].rearrange("p c t -> p (c t)")[:, 0:1024], after=[ltrev]))

            if stop == "G0ATT" and gi == 0:
                sch.step("Y", lambda e: None, after=acc_done)
                sch.check_deadlock()
                _emit(nc, {e: sch.runner(e) for e in Sched.ENGS})
                return nc

            s1_steps = []
            for oc in range(8):
                bk = 6

                def f_wo(e, _oc=oc, _gn=gn, _bk=bk):
                    for dc in range(8):
                        e.matmul(PG[_bk][:, 0:_gn],
                                 W_out[:, C * dc + 128 * _oc:C * dc + 128 * (_oc + 1)],
                                 ACCF[:, dc, 0:_gn],
                                 start=(dc == 0), stop=False)
                    e.matmul(PG[_bk][:, 0:_gn],
                             B_out[0:1, 128 * _oc:128 * (_oc + 1)],
                             CNTG[0:1, 0:_gn],
                             start=False, stop=True)
                lw = sch.step("T", f_wo, after=acc_done + [lcnt] + G(f"pg{bk}"))

                def f_s1(e, _oc=oc, _gn=gn, _bk=bk):
                    e.tensor_tensor(S1[:, _oc, 0:_gn], PG[_bk][:, 0:_gn],
                                    XH[:, _oc, HALO:HALO + _gn], ALU.add)
                ls1 = sch.step("V", f_s1, after=[lw] + G("s1dbg"))
                setg(f"pg{bk}", ls1)
                s1_steps.append(ls1)

            if debug and gi == 0:
                addg("s1dbg", sch.dma("qg", dbg["dbg_s1"], S1[:, 0, 0:488], after=s1_steps))

            def f_sq(e, _gn=gn):
                for dc in range(8):
                    e.activation(XH[:, dc, 0:_gn], S1[:, dc, 0:_gn], AF.Square)
            lsq = sch.step("S", f_sq, after=s1_steps)

            def f_stat(e, _gn=gn):
                for dc in range(8):
                    e.matmul(PG[4][0:1, 0:_gn], ONESC[:, 0:1], S1[:, dc, 0:_gn],
                             start=(dc == 0), stop=(dc == 7))
                for dc in range(8):
                    e.matmul(PG[5][0:1, 0:_gn], ONESC[:, 0:1], XH[:, dc, 0:_gn],
                             start=(dc == 0), stop=(dc == 7))
            lst = sch.step("T", f_stat, after=s1_steps + [lsq] + G("pg4") + G("pg5"))

            def f_rows(e, _gn=gn):
                e.tensor_scalar(ROWA[0:1, 0:_gn], PG[4][0:1, 0:_gn], 1.0 / C, None, ALU.mult)
                e.tensor_scalar(ROWB[0:1, 0:_gn], PG[5][0:1, 0:_gn], 1.0 / C, None, ALU.mult)
                e.drain()
                e.tensor_tensor(ROWC[0:1, 0:_gn], ROWA[0:1, 0:_gn], ROWA[0:1, 0:_gn], ALU.mult)
                e.drain()
                e.tensor_tensor(ROWC[0:1, 0:_gn], ROWB[0:1, 0:_gn], ROWC[0:1, 0:_gn],
                                ALU.subtract)
            lrows = sch.step("V", f_rows, after=[lst])

            def f_rs(e, _gn=gn):
                e.activation(ROWB[0:1, 0:_gn], ROWC[0:1, 0:_gn], AF.Ln, bias=EPSC[0:1, 0:1])
                e.drain()
                e.activation(ROWB[0:1, 0:_gn], ROWB[0:1, 0:_gn], AF.Exp, scale=-0.5)
            lrs = sch.step("S", f_rs, after=[lrows])

            def f_rows2(e, _gn=gn):
                e.drain()
                e.tensor_tensor(ROWC[0:1, 0:_gn], ROWA[0:1, 0:_gn], ROWB[0:1, 0:_gn], ALU.mult)
                e.tensor_copy(RBA[0:1, 0:_gn], ROWB[0:1, 0:_gn])
                e.drain()
                e.tensor_copy(RBB[0:1, 0:_gn], ROWC[0:1, 0:_gn])
            lrows2 = sch.step("V", f_rows2, after=[lrs])

            def f_bc(e, _gn=gn):
                e.matmul(PG[2][:, 0:_gn], ONESR[0:1, :], RBA[0:1, 0:_gn], start=True, stop=True)
                e.matmul(PG[3][:, 0:_gn], ONESR[0:1, :], RBB[0:1, 0:_gn], start=True, stop=True)
            lbc = sch.step("T", f_bc, after=[lrows2, lrows] + G("pg2") + G("pg3"))
            setg("pg4", lrows)
            setg("pg5", lrows)

            x1_steps = []
            for dc in range(8):
                def f_ap(e, _dc=dc, _gn=gn):
                    o = 488 * (_dc % 2)
                    e.tensor_tensor(TMPX[:, o:o + _gn], S1[:, _dc, 0:_gn], PG[2][:, 0:_gn],
                                    ALU.mult)
                    e.drain()
                    e.tensor_tensor(X1F[:, _dc, 0:_gn], TMPX[:, o:o + _gn], PG[3][:, 0:_gn],
                                    ALU.subtract)
                x1_steps.append(sch.step("V", f_ap,
                                          after=[lbc] + (G("x1f") if dc == 0 else [])))
            guard["x1f"] = []
            setg("pg2", x1_steps[-1])
            setg("pg3", x1_steps[-1])

            if debug and gi == 0:
                addg("x1f", sch.dma("qg", dbg["dbg_x1"], X1F[:, 0, 0:488], after=x1_steps))

            if stop == "G0X1" and gi == 0:
                sch.step("Y", lambda e: None, after=x1_steps)
                sch.check_deadlock()
                _emit(nc, {e: sch.runner(e) for e in Sched.ENGS})
                return nc

            tile_tail = []
            for ti in group:
                t0, tn = TILES[ti]
                tloc = t0 - g0

                trs = []
                for dc in range(8):
                    trs.append(sch.dma_tr("qy", X1T[0:128, 128 * dc:128 * (dc + 1)],
                                          X1F[:, dc, tloc:tloc + 128],
                                          after=x1_steps + G("x1t")))
                sp1 = sch.dma("qy", d_x1[t0:t0 + tn, :], X1T[0:tn, :], after=trs)
                setg("x1t", sp1)
                x1_spills.append(sp1)

                lgs = []
                lee = None
                for rnd in range(2):
                    def f_lg(e, _tn=tn, _tloc=tloc, _r=rnd):
                        for dc in range(8):
                            for ecl in range(2):
                                e.matmul(PG[2 + ecl][0:_tn, 0:512],
                                         X1F[:, dc, _tloc:_tloc + _tn],
                                         W_phiT[:, E2 * dc + 512 * (2 * _r + ecl):
                                                E2 * dc + 512 * (2 * _r + ecl + 1)],
                                         start=(dc == 0), stop=False)
                        for ecl in range(2):
                            e.matmul(PG[2 + ecl][0:_tn, 0:512],
                                     ONESR[0:1, 0:_tn],
                                     B_phi[0:1, 512 * (2 * _r + ecl):512 * (2 * _r + ecl + 1)],
                                     start=False, stop=True)
                    llg = sch.step("T", f_lg, after=x1_steps + G("pg2") + G("pg3"))
                    lgs.append(llg)

                    def f_eexp(e, _tn=tn, _r=rnd):
                        e.activation(ELOG[0:_tn, 1024 * _r:1024 * _r + 512],
                                     PG[2][0:_tn, 0:512], AF.Exp)
                        e.activation(ELOG[0:_tn, 1024 * _r + 512:1024 * (_r + 1)],
                                     PG[3][0:_tn, 0:512], AF.Exp)
                    lee = sch.step("S", f_eexp,
                                   after=[llg] + (G("elog") if rnd == 0 else []))
                    setg("pg2", lee)
                    setg("pg3", lee)
                guard["elog"] = []

                if debug and ti == 0:
                    addg("elog", sch.dma("qg", dbg["dbg_elog"], ELOG[0:128, :], after=[lee]))

                def f_sumP(e, _tn=tn):
                    e.reduce_sum(RP[0:_tn, :],
                                 ELOG[0:_tn, :].rearrange("p (n s) -> p n s", s=PS), axis=AX.X)
                    e.drain()
                    e.reciprocal(RP[0:_tn, :], RP[0:_tn, :])
                    e.drain()
                lsp = sch.step("V", f_sumP, after=[lee])

                def f_sumN(e, _tn=tn):
                    e.reduce_sum(RN[0:_tn, :],
                                 ELOG[0:_tn, :].rearrange("p (n s) -> p s n", s=PS), axis=AX.X)
                    e.drain()
                    e.reciprocal(RN[0:_tn, :], RN[0:_tn, :])
                    e.drain()
                lrn = sch.step("V", f_sumN, after=[lee])

                def f_D(e, _tn=tn):
                    e.tensor_tensor(DTM[0:_tn, :].rearrange("p (n s) -> p n s", s=PS),
                                    ELOG[0:_tn, :].rearrange("p (n s) -> p n s", s=PS),
                                    RP[0:_tn, :].unsqueeze(2).broadcast_to([_tn, NE, PS]),
                                    ALU.mult)
                lD = sch.step("V", f_D, after=[lsp] + G("dtm"))
                guard["dtm"] = []

                ctp = ti % 2

                def f_C(e, _tn=tn, _cp=ctp):
                    e.tensor_tensor(CTM[0:_tn, E2 * _cp:E2 * (_cp + 1)]
                                    .rearrange("p (n s) -> p s n", s=PS),
                                    ELOG[0:_tn, :].rearrange("p (n s) -> p s n", s=PS),
                                    RN[0:_tn, :].unsqueeze(2).broadcast_to([_tn, PS, NE]),
                                    ALU.mult)
                lC = sch.step("G", f_C, after=[lrn] + G(f"ctm{ctp}"))
                guard[f"ctm{ctp}"] = []
                addg("elog", lD, lC)

                if debug and ti == 0:
                    addg("dtm", sch.dma("qg", dbg["dbg_D"], DTM[0:128, :], after=[lD]))
                    lCbar = sch.step("G", lambda e: e.drain(), after=[lC])
                    addg("ctm0", sch.dma("qg", dbg["dbg_C"], CTM[0:128, 0:E2], after=[lCbar]))

                spD = sch.dma("qy", d_D[t0:t0 + tn, :], DTM[0:tn, :], after=[lD])
                addg("dtm", spD)
                D_spills.append(spD)

                if pend_ct[0] is not None:
                    tile_tail += emit_ct(*pend_ct[0])
                pend_ct[0] = (ti, tn, ctp, lC)

                tile_tail += [lee, lD, lC, spD, lsp, lrn]
                addg("x1f", lgs[-1], *trs)

            prev_group = s1_steps + x1_steps + [lst] + tile_tail + trs
            if stop == "G0" and gi == 0:
                sch.step("Y", lambda e: None, after=prev_group)
                sch.check_deadlock()
                _emit(nc, {e: sch.runner(e) for e in Sched.ENGS})
                return nc

        if stop == "A":
            sch.step("Y", lambda e: None, after=list(prev_group) + D_spills + C_spills + x1_spills)
            sch.check_deadlock()
            runs = {e: sch.runner(e) for e in Sched.ENGS}
            _emit(nc, runs)
            return nc

        if pend_ct[0] is not None:
            prev_group = prev_group + emit_ct(*pend_ct[0])
            pend_ct[0] = None

        SCH = [list(range(i, min(i + 7, NT))) for i in range(0, NT, 7)]
        prevX = list(prev_group) + D_spills[-1:] + x1_spills[-1:]
        xa_done = [None] * 16
        for si, sc in enumerate(SCH):
            loads = []
            for j, ti in enumerate(sc):
                t0, tn = TILES[ti]
                qD = "qy" if j % 2 == 0 else "qg"
                loads.append(sch.dma(qD, DST[0:tn, j, :], d_D[t0:t0 + tn, :],
                                     after=prevX + [D_spills[ti]]))
                loads.append(sch.dma("qs", X1S[0:tn, j, :], d_x1[t0:t0 + tn, :],
                                     after=prevX + [x1_spills[ti]]))
            lx2_list = []
            for ec in range(16):
                bk = 2 * (ec % 2)

                def f_x(e, _sc=tuple(sc), _ec=ec, _bk=bk):
                    for half in range(2):
                        for j, ti in enumerate(_sc):
                            tn = TILES[ti][1]
                            e.matmul(PG[_bk + half][:, 0:512],
                                     DST[0:tn, j, 128 * _ec:128 * (_ec + 1)],
                                     X1S[0:tn, j, 512 * half:512 * (half + 1)],
                                     start=(j == 0), stop=(j == len(_sc) - 1))
                lx2 = sch.step("T", f_x,
                               after=loads + G(f"pg{bk}") + G(f"pg{bk + 1}"))
                lx2_list.append(lx2)

                if si == 0:
                    def f_xev(e, _ec=ec, _bk=bk):
                        e.tensor_copy(XACC[:, C * _ec:C * _ec + 512],
                                      PG[_bk][:, 0:512])
                        e.tensor_copy(XACC[:, C * _ec + 512:C * (_ec + 1)],
                                      PG[_bk + 1][:, 0:512])
                else:
                    def f_xev(e, _ec=ec, _bk=bk):
                        e.tensor_tensor(XACC[:, C * _ec:C * _ec + 512],
                                        XACC[:, C * _ec:C * _ec + 512],
                                        PG[_bk][:, 0:512], ALU.add)
                        e.tensor_tensor(XACC[:, C * _ec + 512:C * (_ec + 1)],
                                        XACC[:, C * _ec + 512:C * (_ec + 1)],
                                        PG[_bk + 1][:, 0:512], ALU.add)
                lev = sch.step("V", f_xev, after=[lx2])
                xa_done[ec] = lev
                setg(f"pg{bk}", lev)
                setg(f"pg{bk + 1}", lev)
            prevX = loads + lx2_list[-2:]

        xp_w = []
        for q, ecs in (("qg", range(0, 6)), ("qy", range(6, 11)),
                       ("qs", range(11, 16))):
            for ec in ecs:
                xp_w.append(sch.dma(q, d_Xp.ap()[128 * ec:128 * (ec + 1), :],
                                    XACC[:, C * ec:C * (ec + 1)],
                                    after=[xa_done[ec]]))

        if debug:
            dcp = sch.dma("qg", dbg["dbg_dD"], d_D, after=xp_w)
            dcp2 = sch.dma("qg", dbg["dbg_dx1"], d_x1, after=xp_w)
            dcp3 = sch.dma("qg", dbg["dbg_XpL"], d_Xp.ap(), after=xp_w)
            xp_w = xp_w + [dcp, dcp2, dcp3]

        def f_rsc(e):
            e.collective_compute("ReduceScatter", ALU.add,
                                 replica_groups=[list(range(NCORES))],
                                 ins=[d_Xp.ap().opt()], outs=[d_Xl.ap().opt()]).then_inc(ccsem, 1)
            e.wait_ge(ccsem, 1)
            e.nop()
        lrsc = sch.step("G", f_rsc, after=xp_w)

        lxl = sch.dma("qy", XLOC[:, :, :], d_Xl.ap().rearrange("(c p) d -> p c d", p=128),
                      after=[lrsc])
        if debug:
            setg("xpdbg", sch.dma("qg", dbg["dbg_Xp"], XLOC[:, 0, :], after=[lxl]))

        lch_done = []
        for ch in range(2):
            def f_cl(e, _ch=ch):
                e.tensor_scalar(XLOC[:, _ch, :], XLOC[:, _ch, :], -33000.0, 65000.0,
                                ALU.max, ALU.min)
                e.drain()
                e.reduce_sum(RZ[:, 0:1], XLOC[:, _ch, :], axis=AX.X)
                e.drain()
                e.tensor_scalar(RZ[:, 0:1], RZ[:, 0:1], 1.0 / C, None, ALU.mult)
                e.drain()
                e.tensor_scalar(XLOC[:, _ch, :], XLOC[:, _ch, :], RZ[:, 0:1], None,
                                ALU.subtract)
                e.drain()
                e.tensor_tensor(XSTG[:, :], XLOC[:, _ch, :], XLOC[:, _ch, :], ALU.mult)
                e.drain()
                e.reduce_sum(RZ[:, 1:2], XSTG[:, :], axis=AX.X)
                e.drain()
                e.tensor_scalar(RZ[:, 1:2], RZ[:, 1:2], 1.0 / C, None, ALU.mult)
            lcl = sch.step("V", f_cl, after=[lxl] + G("xpdbg"))

            def f_crs(e, _ch=ch):
                e.activation(RZ[:, 2:3], RZ[:, 1:2], AF.Ln, bias=EPSC[:, 0:1])
                e.drain()
                e.activation(RZ[:, 2:3], RZ[:, 2:3], AF.Exp, scale=-0.5)
            lcrs = sch.step("S", f_crs, after=[lcl])

            def f_cap(e, _ch=ch):
                e.tensor_scalar(XLOC[:, _ch, :], XLOC[:, _ch, :], RZ[:, 2:3], None, ALU.mult)
            lcap = sch.step("V", f_cap, after=[lcrs])

            def f_cth(e, _ch=ch):
                e.activation(ELOG[:, C * _ch:C * (_ch + 1)], XLOC[:, _ch, :], AF.Tanh)
            lch_done.append(sch.step("S", f_cth, after=[lcap]))

        if debug:
            sch.dma("qg", dbg["dbg_Xln"],
                    ELOG[:, 0:1024], after=lch_done)

        xlt = []
        for pr in range(4):
            ch, npp = pr // 2, pr % 2
            r0 = 64 * npp

            def f_xt(e, _ch=ch, _r0=r0):
                for dc in range(8):
                    e.transpose(PT[0:128, 128 * dc:128 * dc + 64],
                                ELOG[_r0:_r0 + 64, C * _ch + 128 * dc:C * _ch + 128 * (dc + 1)],
                                ID64[_r0:_r0 + 64, 0:64])
            lt1 = sch.step("T", f_xt, after=lch_done + G("pt"))

            def f_xte(e, _ch=ch, _npp=npp):
                for q in range(2):
                    n = 4 * _ch + 2 * _npp + q
                    e.copy(XLNT[:, :].rearrange("p (a s) -> p a s", s=PS)
                           [:, 8 * n:8 * (n + 1), :],
                           PT[:, :].rearrange("p (c s) -> p c s", s=128)
                           [:, :, 32 * q:32 * (q + 1)])
            lte = sch.step("S", f_xte, after=[lt1])
            setg("pt", lte)
            xlt.append(lte)

        yt_w = []
        lwe_prev = None
        for n in range(EPC):
            lwe = sch.dma("qg", EXPW[:, :, :],
                          expw_d[n].rearrange("(c p) k -> p c k", p=128),
                          after=xlt + yt_w[-1:] + prevX)
            lbe = sch.dma("qg", B_ex1[0:1, :], expb_d[n:n + 1, :], after=yt_w[-1:])

            def f_yt(e, _n=n):
                for half in range(2):
                    for dc in range(8):
                        e.matmul(PG[half][0:PS, 0:512],
                                 XLNT[:, :].rearrange("p (a s) -> p a s", s=PS)[:, 8 * _n + dc, :],
                                 EXPW[:, dc, 512 * half:512 * (half + 1)],
                                 start=(dc == 0), stop=False)
                    e.matmul(PG[half][0:PS, 0:512],
                             ONESR[0:1, 0:PS],
                             B_ex1[0:1, 512 * half:512 * (half + 1)],
                             start=False, stop=True)
            lyt = sch.step("T", f_yt, after=[lwe, lbe] + G("pg0") + G("pg1"))

            def f_ye(e):
                e.tensor_copy(YSTG[0:PS, 0:512], PG[0][0:PS, 0:512])
                e.tensor_copy(YSTG[0:PS, 512:1024], PG[1][0:PS, 0:512])
            lye = sch.step("V", f_ye, after=[lyt] + yt_w[-1:])
            setg("pg0", lye)
            setg("pg1", lye)
            yt_w.append(sch.dma("qy", d_Yl.ap()[PS * n:PS * (n + 1), :], YSTG[0:PS, :],
                                after=[lye]))

        def f_ag(e):
            e.collective_compute("AllGather", ALU.bypass,
                                 replica_groups=[list(range(NCORES))],
                                 ins=[d_Yl.ap().opt()], outs=[d_Yt.ap().opt()]).then_inc(ccsem, 1)
            e.wait_ge(ccsem, 2)
            e.nop()
        lag = sch.step("G", f_ag, after=yt_w)

        lytf = sch.dma("qy", W_phiT[:, :].rearrange("p (c k) -> p c k", k=C),
                       d_Yt.ap().rearrange("(c p) k -> p c k", p=128), after=[lag])
        if debug:
            sch.dma("qg", dbg["dbg_Yt"], W_phiT[:, 0:1024], after=[lytf])
            sch.dma("qg", dbg["dbg_YtF"], W_phiT[:, :], after=[lytf])

        if stop == "X":
            sch.step("Y", lambda e: None, after=[lytf])
            sch.check_deadlock()
            runs = {e: sch.runner(e) for e in Sched.ENGS}
            _emit(nc, runs)
            return nc

        out_dmas = []
        pend_o = [None]
        lc_prev = None
        for ti in range(NT):
            t0, tn = TILES[ti]
            lc1 = sch.dma("qy", CFM[:, :], d_C[ti, :, :],
                          after=[lytf, C_spill_by_tile[ti]] + G("cfm"))
            lx1b = sch.dma("qs", X1T[0:tn, :], d_x1[t0:t0 + tn, :],
                           after=[lytf, x1_spills[ti]] + G("x1t"))

            b0 = 0 if ti % 2 == 0 else 2
            b1 = b0 + 1
            so = 1024 * (ti % 2)
            rb = 4 * (ti % 2)

            def f_y(e, _tn=tn, _b0=b0, _b1=b1):
                for ec in range(16):
                    e.matmul(PG[_b0][0:_tn, 0:512],
                             CFM[:, :].rearrange("p (c w) -> p c w", w=124)[:, ec, 0:_tn],
                             W_phiT[:, C * ec:C * ec + 512],
                             start=(ec == 0), stop=(ec == 15))
                for ec in range(16):
                    e.matmul(PG[_b1][0:_tn, 0:512],
                             CFM[:, :].rearrange("p (c w) -> p c w", w=124)[:, ec, 0:_tn],
                             W_phiT[:, C * ec + 512:C * ec + 1024],
                             start=(ec == 0), stop=(ec == 15))
            ly = sch.step("T", f_y, after=[lc1, lytf] + G(f"pg{b0}") + G(f"pg{b1}"))
            setg("cfm", ly)

            def f_s2(e, _tn=tn, _b0=b0, _b1=b1, _so=so, _rb=rb):
                e.scalar_tensor_tensor(S2[0:_tn, _so:_so + 512], PG[_b0][0:_tn, 0:512], 1.0,
                                       X1T[0:_tn, 0:512], ALU.mult, ALU.add)
                e.scalar_tensor_tensor(S2[0:_tn, _so + 512:_so + 1024], PG[_b1][0:_tn, 0:512],
                                       1.0, X1T[0:_tn, 512:1024], ALU.mult, ALU.add)
                e.drain()
                e.reduce_sum(RZ[0:_tn, _rb:_rb + 1], S2[0:_tn, _so:_so + 1024], axis=AX.X)
                e.drain()
                e.tensor_scalar(RZ[0:_tn, _rb:_rb + 1], RZ[0:_tn, _rb:_rb + 1], 1.0 / C,
                                None, ALU.mult)
            ls2 = sch.step("V", f_s2, after=[ly, lx1b] + G(f"s2_{ti % 2}") + G("s2dbg2"))
            setg(f"pg{b0}", ls2)
            setg(f"pg{b1}", ls2)
            setg("x1t", ls2)

            if debug and ti == 0:
                setg("s2dbg2", sch.dma("qg", dbg["dbg_S2"], S2[0:128, 0:1024], after=[ls2]))

            def f_sq2(e, _tn=tn, _so=so, _rb=rb):
                e.activation(YSTG[0:_tn, :], S2[0:_tn, _so:_so + 1024], AF.Square,
                             accum_out=RZ[0:_tn, _rb + 1:_rb + 2])
            lsq2 = sch.step("S", f_sq2, after=[ls2] + G("junk"))

            def f_v2(e, _tn=tn, _rb=rb):
                e.tensor_scalar(RZ[0:_tn, _rb + 1:_rb + 2], RZ[0:_tn, _rb + 1:_rb + 2],
                                1.0 / C, None, ALU.mult)
                e.tensor_tensor(RZ[0:_tn, _rb + 2:_rb + 3], RZ[0:_tn, _rb:_rb + 1],
                                RZ[0:_tn, _rb:_rb + 1], ALU.mult)
                e.drain()
                e.tensor_tensor(RZ[0:_tn, _rb + 1:_rb + 2], RZ[0:_tn, _rb + 1:_rb + 2],
                                RZ[0:_tn, _rb + 2:_rb + 3], ALU.subtract)
            lv2 = sch.step("V", f_v2, after=[lsq2])

            def f_r2(e, _tn=tn, _rb=rb):
                e.activation(RZ[0:_tn, _rb + 3:_rb + 4], RZ[0:_tn, _rb + 1:_rb + 2], AF.Ln,
                             bias=EPSC[0:_tn, 0:1])
                e.drain()
                e.activation(RZ[0:_tn, _rb + 3:_rb + 4], RZ[0:_tn, _rb + 3:_rb + 4], AF.Exp,
                             scale=-0.5)
            lr2 = sch.step("S", f_r2, after=[lv2])
            setg("junk", lr2)

            def f_o(e, _tn=tn, _so=so, _rb=rb):
                e.tensor_scalar(YOUT[0:_tn, :], S2[0:_tn, _so:_so + 1024],
                                RZ[0:_tn, _rb:_rb + 1], RZ[0:_tn, _rb + 3:_rb + 4],
                                ALU.subtract, ALU.mult)
            lo = sch.step("V", f_o, after=[lr2] + G("yout") + G("odbg"))
            setg(f"s2_{ti % 2}", lo)

            if debug and ti == 0:
                setg("odbg", sch.dma("qg", dbg["dbg_O"], YOUT[0:128, :], after=[lo]))
            od = sch.dma("qy", out_d[t0:t0 + tn, :], YOUT[0:tn, :], after=[lo])
            setg("yout", od)
            out_dmas.append(od)

        if pend_o[0] is not None:
            emit_o(*pend_o[0])
        sch.step("Y", lambda e: None, after=out_dmas)

        sch.check_deadlock()
        runs = {e: sch.runner(e) for e in Sched.ENGS}
        _emit(nc, runs)

    return nc


def _emit(nc, runs):
    with nc.Block() as block:
        @block.tensor
        def _(t):
            runs["T"](t)

        @block.vector
        def _(v):
            runs["V"](v)

        @block.scalar
        def _(s):
            runs["S"](s)

        @block.gpsimd
        def _(g):
            runs["G"](g)

        @block.sync
        def _(sy):
            runs["Y"](sy)


def _build_consts():
    bf = ml_dtypes.bfloat16
    ident = np.eye(128, dtype=np.float32).astype(bf)
    ident64 = np.tile(np.eye(64, dtype=np.float32), (2, 1)).astype(bf)
    onescol = np.ones((128, 1), np.float32).astype(bf)
    onesrow = np.ones((1, 128), np.float32).astype(bf)
    maskL = np.zeros((3, 128, TT), np.float32)
    maskR = np.zeros((3, 128, TT), np.float32)
    for mv, (t0, tn) in ((0, TILES[1]), (1, TILES[0]), (2, TILES[-1])):
        for tp in range(tn):
            t = t0 + tp
            p = tp // 2
            gp = t // 2
            if gp - 1 >= 0:
                maskL[mv, 2 * p:2 * p + 4, tp] = 1.0
            if gp <= (L - 4) // 2:
                maskR[mv, 2 * p + 2:2 * p + 6, tp] = 1.0
    cnt = np.zeros((1, L), np.float32)
    W = (L - 4) // 2 + 1
    idx = (np.arange(W)[:, None] * 2 + np.arange(4)[None, :]).reshape(-1)
    np.add.at(cnt[0], idx, 1.0)
    return dict(ident=ident, ident64=ident64, onescol=onescol, onesrow=onesrow,
                maskL=maskL.astype(bf), maskR=maskR.astype(bf), cnt=cnt.astype(bf))


_NC_CACHE = {}


def kernel(L0_attn_x, L0_attn_w_qkv, L0_attn_w_out, L0_attn_b_out,
           L0_ffn_phi_w, L0_ffn_phi_b, L0_ffn_expert_w, L0_ffn_expert_b,
           debug=False, stop="full"):
    x = np.asarray(L0_attn_x, dtype=np.float32)
    wqkv = np.ascontiguousarray(np.asarray(L0_attn_w_qkv, dtype=np.float32))
    wout = np.ascontiguousarray(np.asarray(L0_attn_w_out, dtype=np.float32))
    bout = np.asarray(L0_attn_b_out, dtype=np.float32).reshape(1, C)
    phiT = np.ascontiguousarray(np.asarray(L0_ffn_phi_w, dtype=np.float32).T)
    phib = np.asarray(L0_ffn_phi_b, dtype=np.float32).reshape(1, E2)
    expw = np.ascontiguousarray(np.asarray(L0_ffn_expert_w, dtype=np.float32))
    expb = np.ascontiguousarray(np.asarray(L0_ffn_expert_b, dtype=np.float32))

    consts = _build_consts()
    key = ("nc", debug, stop)
    if key not in _NC_CACHE:
        _NC_CACHE[key] = build_nc(debug=debug, stop=stop)
    nc = _NC_CACHE[key]

    in_maps = []
    for b in range(NCORES):
        in_maps.append(dict(
            xT=np.ascontiguousarray(x[b].T),
            wqkv=wqkv, wout=wout, bout=bout, phiT=phiT, phib=phib,
            expw=np.ascontiguousarray(expw[b * EPC:(b + 1) * EPC]),
            expb=np.ascontiguousarray(expb[b * EPC:(b + 1) * EPC]),
            ident=consts["ident"], ident64=consts["ident64"],
            onescol=consts["onescol"], onesrow=consts["onesrow"],
            maskL=consts["maskL"], maskR=consts["maskR"], cnt=consts["cnt"],
        ))

    res = run_bass_kernel_spmd(nc, in_maps, core_ids=list(range(NCORES)))
    out = np.stack([np.asarray(res.results[b]["out"], dtype=np.float32)
                    for b in range(NCORES)])
    if debug:
        return out, res
    return out


# version 09
# speedup vs baseline: 1.1110x; 1.0191x over previous
import numpy as np
import ml_dtypes

import concourse.bass as bass
import concourse.mybir as mybir
from concourse.bass_utils import run_bass_kernel_spmd

F32 = mybir.dt.float32
BF16 = mybir.dt.bfloat16
AF = mybir.ActivationFunctionType
ALU = mybir.AluOpType
AX = mybir.AxisListType

NCORES = 8
L = 4096
C = 1024
HEADS = 16
NE = 64
PS = 32
E2 = NE * PS
EPC = NE // NCORES

TT = 122
TILES = [(i * TT, TT) for i in range(L // TT)] + [(L - L % TT, L % TT)]
NT = len(TILES)
GROUPS = [list(range(i, min(i + 4, NT))) for i in range(0, NT, 4)]
HALO = 4
SCALE = 1.0 / 32.0
USE_GP_TT = False


class Sched:
    ENGS = ("T", "V", "S", "G", "Y")
    QENG = {"qy": "Y", "qs": "S", "qg": "G"}

    NROT = 8
    EROT = 4

    def __init__(self):
        self.prog = {e: [] for e in self.ENGS}
        self.cnt = {e: 0 for e in self.ENGS}
        self.dcnt = {(q, j): 0 for q in ("qy", "qs", "qg") for j in range(self.NROT)}
        self.didx = {q: 0 for q in ("qy", "qs", "qg")}
        self.sems = None
        self.dsems = None

    def step(self, eng, fn, after=()):
        deps = []
        for d in after:
            if d is None:
                continue
            key, val = d
            if isinstance(key, tuple):
                deps.append((key, self.dcnt[key]))
            else:
                deps.append(d)
        self.prog[eng].append((tuple(deps), fn))
        self.cnt[eng] += 1
        return (eng, self.cnt[eng])

    def dma(self, q, out_ap, in_ap, after=(), **kw):
        sch = self
        j = self.didx[q] % self.NROT
        self.didx[q] += 1

        def fn(e, _o=out_ap, _i=in_ap, _kw=dict(kw), _k=(q, j)):
            e.dma_start(out=_o, in_=_i, **_kw).then_inc(sch.dsems[_k], 16)
        fn._dmaq = (q, j)
        self.step(self.QENG[q], fn, after=after)
        self.dcnt[(q, j)] += 16
        return ((q, j), self.dcnt[(q, j)])

    def dma_tr(self, q, out_ap, in_ap, after=()):
        sch = self
        j = self.didx[q] % self.NROT
        self.didx[q] += 1

        def fn(e, _o=out_ap, _i=in_ap, _k=(q, j)):
            e.dma_start_transpose(_o, _i).then_inc(sch.dsems[_k], 16)
        fn._dmaq = (q, j)
        self.step(self.QENG[q], fn, after=after)
        self.dcnt[(q, j)] += 16
        return ((q, j), self.dcnt[(q, j)])

    def check_deadlock(self):
        pos = {e: 0 for e in self.ENGS}
        done = {e: 0 for e in self.ENGS}
        dmadone = {k: 0 for k in self.dcnt}
        progressed = True
        while progressed:
            progressed = False
            for e in self.ENGS:
                while pos[e] < len(self.prog[e]):
                    waits, fn = self.prog[e][pos[e]]
                    ok = True
                    for key, val in waits:
                        if key == e:
                            continue
                        if isinstance(key, tuple):
                            if dmadone[key] < val:
                                ok = False
                                break
                        elif done[key] < val:
                            ok = False
                            break
                    if not ok:
                        break
                    pos[e] += 1
                    done[e] += 1
                    dk = getattr(fn, "_dmaq", None)
                    if dk:
                        dmadone[dk] += 16
                    progressed = True
        stuck = {e: (pos[e], len(self.prog[e])) for e in self.ENGS if pos[e] < len(self.prog[e])}
        if stuck:
            detail = {}
            for e in stuck:
                waits, fn = self.prog[e][pos[e]]
                detail[e] = (pos[e], waits[:6])
            raise RuntimeError(f"DEADLOCK: stuck={stuck} blocking_waits={detail}")

    def runner(self, ename):
        chunks = self.prog[ename]
        sch = self

        class Proxy:
            __slots__ = ("_e", "last")

            def __init__(self, e):
                self._e = e
                self.last = None

            def __getattr__(self, n):
                a = getattr(self._e, n)
                if callable(a):
                    px = self

                    def w(*args, **kw):
                        r = a(*args, **kw)
                        if r is not None and hasattr(r, "then_inc"):
                            px.last = r
                        return r
                    return w
                return a

        def run(e):
            px = Proxy(e)
            wm = {}
            nout = 0
            R = sch.EROT
            for waits, fn in chunks:
                for key, val in waits:
                    if key == ename:
                        continue
                    if isinstance(key, tuple):
                        if wm.get(key, 0) >= val:
                            continue
                        e.wait_ge(sch.dsems[key], val)
                        wm[key] = val
                    else:
                        j = (val - 1) % R
                        v = (val - 1) // R + 1
                        wk = (key, j)
                        if wm.get(wk, 0) >= v:
                            continue
                        e.wait_ge(sch.sems[(key, j)], v)
                        wm[wk] = v
                px.last = None
                fn(px)
                sem = sch.sems[(ename, nout % R)]
                nout += 1
                if getattr(fn, "_dmaq", None) is not None or px.last is None:
                    e.nop().then_inc(sem, 1)
                else:
                    px.last.then_inc(sem, 1)
        return run


class Arena:

    def __init__(self, nc_sbt, name, dtype):
        self.sbt = nc_sbt
        self.name = name
        self.dtype = dtype
        self.regions = {}
        self.hwm = 0

    def alloc(self, tag, cols):
        cur = self.regions.get(tag, 0)
        off = cur
        self.regions[tag] = cur + cols
        self.hwm = max(self.hwm, cur + cols)
        return off


def build_nc(debug=False, stop="full"):
    nc = bass.Bass("TRN2", target_bir_lowering=False, debug=False)

    xT = nc.dram_tensor("xT", [C, L], F32, kind="ExternalInput").ap()
    wqkv_d = nc.dram_tensor("wqkv", [C, 3 * C], F32, kind="ExternalInput").ap()
    wout_d = nc.dram_tensor("wout", [C, C], F32, kind="ExternalInput").ap()
    bout_d = nc.dram_tensor("bout", [1, C], F32, kind="ExternalInput").ap()
    phiT_d = nc.dram_tensor("phiT", [C, E2], F32, kind="ExternalInput").ap()
    phib_d = nc.dram_tensor("phib", [1, E2], F32, kind="ExternalInput").ap()
    expw_d = nc.dram_tensor("expw", [EPC, C, C], F32, kind="ExternalInput").ap()
    expb_d = nc.dram_tensor("expb", [EPC, C], F32, kind="ExternalInput").ap()
    ident_d = nc.dram_tensor("ident", [128, 128], BF16, kind="ExternalInput").ap()
    ident64_d = nc.dram_tensor("ident64", [128, 64], BF16, kind="ExternalInput").ap()
    onescol_d = nc.dram_tensor("onescol", [128, 1], BF16, kind="ExternalInput").ap()
    onesrow_d = nc.dram_tensor("onesrow", [1, 128], BF16, kind="ExternalInput").ap()
    maskL_d = nc.dram_tensor("maskL", [3, 128, TT], BF16, kind="ExternalInput").ap()
    maskR_d = nc.dram_tensor("maskR", [3, 128, TT], BF16, kind="ExternalInput").ap()
    cnt_d = nc.dram_tensor("cnt", [1, L], BF16, kind="ExternalInput").ap()

    out_d = nc.dram_tensor("out", [L, C], F32, kind="ExternalOutput").ap()

    d_x1 = nc.dram_tensor("d_x1", [L, C], BF16).ap()
    d_D = nc.dram_tensor("d_D", [L, E2], BF16).ap()
    d_C = nc.dram_tensor("d_C", [NT, 128, 16 * 124], BF16).ap()
    d_Xp = nc.dram_tensor("d_Xp", [E2, C], F32)
    d_Xl = nc.dram_tensor("d_Xl", [E2 // NCORES, C], F32)
    d_Yl = nc.dram_tensor("d_Yl", [E2 // NCORES, C], BF16)
    d_Yt = nc.dram_tensor("d_Yt", [E2, C], BF16)

    dbg = {}
    if debug:
        for name, shape, dt in [
            ("dbg_qk", [128, 496], F32), ("dbg_E", [128, 488], F32),
            ("dbg_EL", [128, 488], F32), ("dbg_vo", [128, 1056], F32),
            ("dbg_acc", [128, 1024], F32), ("dbg_s1", [128, 488], F32),
            ("dbg_x1", [128, 488], F32),
            ("dbg_elog", [128, 2048], F32), ("dbg_D", [128, 2048], F32),
            ("dbg_C", [128, 2048], F32), ("dbg_cfm", [128, 1984], F32),
            ("dbg_Xp", [128, 1024], F32), ("dbg_Xln", [128, 1024], F32),
            ("dbg_Yt", [128, 1024], F32), ("dbg_S2", [128, 1024], F32),
            ("dbg_YtF", [128, 16384], F32),
        ]:
            dbg[name] = nc.dram_tensor(name, shape, dt, kind="ExternalOutput").ap()
        for name, shape, dt in [
            ("dbg_dD", [L, E2], BF16), ("dbg_dx1", [L, C], BF16),
            ("dbg_XpL", [E2, C], F32),
            ("dbg_O", [128, 1024], F32),
        ]:
            dbg[name] = nc.dram_tensor(name, shape, dt, kind="ExternalOutput").ap()

    import contextlib
    stack = contextlib.ExitStack()
    sbt = lambda name, shape, dt: stack.enter_context(nc.sbuf_tensor(name, shape, dt))
    pst = lambda name, shape, dt=F32: stack.enter_context(nc.psum_tensor(name, shape, dt))

    with stack:
        WXB = sbt("WXB", [128, 8 * 3 * C * 2 + 8 * C * 2], mybir.dt.uint8)
        _wxb_addr = nc.lookup_mloc(WXB).addr

        def _overlay(name, shape, dtype, off):
            return nc.alloc_sbuf_tensor_at(name, shape, dtype,
                                           offset=_wxb_addr + off)

        W_qkv = _overlay("W_qkv", [128, 8 * 3 * C], BF16, 0)
        W_out = _overlay("W_out", [128, 8 * C], BF16, 8 * 3 * C * 2)
        XACC = _overlay("XACC", [128, 16 * C], F32, 0)
        W_phiT = sbt("W_phiT", [128, 8 * E2], BF16)
        B_out = sbt("B_out", [1, C], BF16)
        B_phi = sbt("B_phi", [1, E2], BF16)
        B_ex1 = sbt("B_ex1", [1, C], BF16)
        IDN = sbt("IDN", [128, 128], BF16)
        ID64 = sbt("ID64", [128, 64], BF16)
        ONESC = sbt("ONESC", [128, 1], BF16)
        ONESR = sbt("ONESR", [1, 128], BF16)
        MKL = sbt("MKL", [128, 3 * TT], BF16)
        MKR = sbt("MKR", [128, 3 * TT], BF16)
        CNTG = sbt("CNTG", [1, 512], BF16)
        EPSC = sbt("EPSC", [128, 1], F32)

        ab = Arena(None, "AB", BF16)
        o_QK = ab.alloc("A", 16 * 496)
        o_XH = ab.alloc("A", 8 * 496)
        o_S1 = ab.alloc("A", 8 * 488)
        o_X1F = ab.alloc("A", 8 * 512)
        o_ACCF = ab.alloc("A", 8 * 488)
        o_DST = ab.alloc("X", 7 * E2)
        o_X1S = ab.alloc("X", 7 * C)
        AB = sbt("AB", [128, ab.hwm], BF16)
        QK = AB[:, o_QK:o_QK + 16 * 496].rearrange("p (c t) -> p c t", t=496)
        XH = AB[:, o_XH:o_XH + 8 * 496].rearrange("p (c t) -> p c t", t=496)
        S1 = AB[:, o_S1:o_S1 + 8 * 488].rearrange("p (c t) -> p c t", t=488)
        X1F = AB[:, o_X1F:o_X1F + 8 * 512].rearrange("p (c t) -> p c t", t=512)
        ACCF = AB[:, o_ACCF:o_ACCF + 8 * 488].rearrange("p (c t) -> p c t", t=488)
        DST = AB[:, o_DST:o_DST + 7 * E2].rearrange("p (j e) -> p j e", e=E2)
        X1S = AB[:, o_X1S:o_X1S + 7 * C].rearrange("p (j e) -> p j e", e=C)
        EXPW = AB[:, o_DST:o_DST + 8 * C].rearrange("p (j e) -> p j e", e=C)

        af = Arena(None, "AF", F32)
        o_TMPX = af.alloc("A", 2 * 488)
        o_TMPL = af.alloc("A", 2 * 256)
        o_TMPR = af.alloc("A", 2 * 256)
        o_XLOC = af.alloc("X", 2 * C)
        o_XSTG = af.alloc("X", 1024)
        o_S2 = af.alloc("B", 2 * 1024)
        o_YOUT = af.alloc("B", 1024)
        AF_ = sbt("AF_", [128, af.hwm], F32)
        TMPX = AF_[:, o_TMPX:o_TMPX + 2 * 488]
        TMPL = AF_[:, o_TMPL:o_TMPL + 2 * 256]
        TMPR = AF_[:, o_TMPR:o_TMPR + 2 * 256]
        XLOC = AF_[:, o_XLOC:o_XLOC + 2 * C].rearrange("p (c t) -> p c t", t=C)
        XSTG = AF_[:, o_XSTG:o_XSTG + 1024]
        S2 = AF_[:, o_S2:o_S2 + 2 * 1024]
        YOUT = AF_[:, o_YOUT:o_YOUT + 1024]

        VO = sbt("VO", [128, 16 * 66], BF16)
        EE = sbt("EE", [128, 2 * 4 * TT], BF16)
        EL = sbt("EL", [128, 2 * 4 * TT], BF16)
        ER = sbt("ER", [128, 2 * 4 * TT], BF16)
        ACCT = sbt("ACCT", [128, 2048], BF16)
        RZ = sbt("RZ", [128, 8], F32)
        ELOG = sbt("ELOG", [128, E2], BF16)
        RP = sbt("RP", [128, NE], F32)
        RN = sbt("RN", [128, PS], F32)
        DTM = sbt("DTM", [128, E2], BF16)
        CTM = sbt("CTM", [128, 2 * E2], BF16)
        CFM = sbt("CFM", [128, 16 * 124], BF16)
        X1T = sbt("X1T", [128, C], BF16)
        XLNT = AB[:, o_X1S:o_X1S + EPC * 8 * PS]
        YSTG = AB[:, o_X1S + EPC * 8 * PS:o_X1S + EPC * 8 * PS + C]
        ROWA = sbt("ROWA", [1, 512], F32)
        ROWB = sbt("ROWB", [1, 512], F32)
        ROWC = sbt("ROWC", [1, 512], F32)
        RBA = sbt("RBA", [1, 512], BF16)
        RBB = sbt("RBB", [1, 512], BF16)

        PG = [pst(f"PG{i}", [128, 512]) for i in range(7)]
        PT = pst("PT", [128, 1024], BF16)

        sems = {(e, j): stack.enter_context(nc.semaphore(f"sem{e}{j}"))
                for e in Sched.ENGS for j in range(Sched.EROT)}
        dsems = {(q, j): stack.enter_context(nc.semaphore(f"dma{q}{j}"))
                 for q in ("qy", "qs", "qg") for j in range(Sched.NROT)}
        ccsem = stack.enter_context(nc.semaphore("ccsem"))

        sch = Sched()
        sch.sems = sems
        sch.dsems = dsems

        guard = {}

        def G(key):
            return list(guard.get(key, ()))

        def setg(key, *labels):
            guard[key] = list(labels)

        def addg(key, *labels):
            guard.setdefault(key, []).extend(labels)

        p0 = []
        p0.append(sch.dma("qg", IDN[:, :], ident_d))
        p0.append(sch.dma("qg", ID64[:, :], ident64_d))
        p0.append(sch.dma("qg", ONESC[:, :], onescol_d))
        p0.append(sch.dma("qg", ONESR[:, :], onesrow_d))
        p0.append(sch.dma("qg", MKL[:, :].rearrange("p (n t) -> p n t", t=TT),
                          maskL_d.rearrange("n p t -> p n t")))
        p0.append(sch.dma("qg", MKR[:, :].rearrange("p (n t) -> p n t", t=TT),
                          maskR_d.rearrange("n p t -> p n t")))
        p0.append(sch.dma("qg", B_out[:, :], bout_d))
        p0.append(sch.dma("qg", B_phi[:, :], phib_d))
        p0.append(sch.dma("qg", W_qkv[:, :].rearrange("p (c k) -> p c k", k=3 * C),
                          wqkv_d.rearrange("(c p) k -> p c k", p=128)))
        p0.append(sch.dma("qg", W_out[:, :].rearrange("p (c k) -> p c k", k=C),
                          wout_d.rearrange("(c p) k -> p c k", p=128)))
        p0.append(sch.dma("qg", W_phiT[:, :].rearrange("p (c k) -> p c k", k=E2),
                          phiT_d.rearrange("(c p) k -> p c k", p=128)))

        def f_init(e):
            e.memset(EPSC[:, :], 1e-5)
            e.memset(EE[:, :], 0.0)
            e.memset(VO[:, :], 1.0)
            e.memset(AB[:, :], 0.0)
            e.memset(AF_[:, :], 0.0)
        l_init = sch.step("V", f_init)

        def f_init2(e):
            for t in (EL, ER, ACCT, ELOG, RP, RN, DTM, CTM, CFM, X1T,
                      RZ, ROWA, ROWB, ROWC, RBA, RBB):
                e.memset(t[:, :], 0.0)
        l_init2 = sch.step("G", f_init2)
        consts = p0 + [l_init, l_init2]

        if stop == "P0":
            sch.step("Y", lambda e: None, after=consts)
            sch.check_deadlock()
            _emit(nc, {e: sch.runner(e) for e in Sched.ENGS})
            return nc

        x1_spills = []
        D_spills = []
        C_spills = []
        prev_group = list(consts)
        pend_ct = [None]
        C_spill_by_tile = {}

        def emit_ct(ti2, tn2, cp2, lC2):
            cts = []
            for rnd in range(2):
                def f_ct(e, _rnd=rnd, _tn=tn2, _cp=cp2):
                    for cc in range(8):
                        e.transpose(PT[0:128, 128 * cc:128 * cc + _tn],
                                    CTM[0:_tn, E2 * _cp + 128 * (8 * _rnd + cc):
                                        E2 * _cp + 128 * (8 * _rnd + cc + 1)],
                                    IDN[0:_tn, 0:_tn])
                lct = sch.step("T", f_ct, after=[lC2] + G("pt"))

                def f_cev(e, _rnd=rnd, _tn=tn2):
                    e.copy(CFM[:, :].rearrange("p (c w) -> p c w", w=124)
                           [:, 8 * _rnd:8 * (_rnd + 1), 0:_tn],
                           PT[:, :].rearrange("p (c t) -> p c t", t=128)[:, :, 0:_tn])
                lcev = sch.step("S", f_cev, after=[lct] + G("cfm"))
                setg("pt", lcev)
                cts.append(lcev)
            addg(f"ctm{cp2}", cts[-1])
            spC = sch.dma("qy", d_C[ti2, :, :], CFM[:, :], after=cts)
            setg("cfm", spC)
            C_spill_by_tile[ti2] = spC
            C_spills.append(spC)
            return [spC]

        xh_free = {}
        for gi, group in enumerate(GROUPS):
            g0 = TILES[group[0]][0]
            gn = sum(TILES[t][1] for t in group)
            h0, hn = g0 - HALO, gn + 2 * HALO
            XHg = XH
            cq = 0
            dep_xh = xh_free.get(0, list(consts))

            c0, c1 = max(h0, 0), min(h0 + hn, L)
            off = c0 - h0
            lh = sch.dma("qg", XHg[:, :, off:off + (c1 - c0)],
                         xT.rearrange("(c p) t -> p c t", p=128)[:, :, c0:c1],
                         after=dep_xh)
            lcnt = sch.dma("qg", CNTG[0:1, cq:cq + gn], cnt_d[0:1, g0:g0 + gn],
                           after=dep_xh)
            xh_ready = [lh]
            if off > 0 or off + (c1 - c0) < hn:
                def f_pad(e, _off=off, _w=c1 - c0, _hn=hn, _XHg=XHg):
                    if _off > 0:
                        e.memset(_XHg[:, :, 0:_off], 0.0)
                    if _off + _w < _hn:
                        e.memset(_XHg[:, :, _off + _w:_hn], 0.0)
                xh_ready.append(sch.step("G", f_pad, after=[lh]))

            qk_ready = []
            for qc in range(16):
                bank = qc % 2

                def f_qk(e, _qc=qc, _hn=hn, _bank=bank, _XHg=XHg):
                    for dc in range(8):
                        e.matmul(PG[_bank][:, 0:_hn],
                                 W_qkv[:, 3 * C * dc + 128 * _qc:3 * C * dc + 128 * (_qc + 1)],
                                 _XHg[:, dc, 0:_hn],
                                 start=(dc == 0), stop=(dc == 7))
                lm = sch.step("T", f_qk, after=xh_ready + G(f"pg{bank}"))

                def f_ev(e, _qc=qc, _hn=hn, _bank=bank):
                    e.copy(QK[:, _qc, 0:_hn], PG[_bank][:, 0:_hn])
                ev = sch.step("S", f_ev, after=[lm])
                setg(f"pg{bank}", ev)
                qk_ready.append(ev)
            qk_done = qk_ready[-2:]

            if debug and gi == 0:
                sch.dma("qg", dbg["dbg_qk"], QK[:, 8, 0:496], after=qk_ready)

            if stop == "G0QK" and gi == 0:
                sch.step("Y", lambda e: None, after=qk_ready)
                sch.check_deadlock()
                _emit(nc, {e: sch.runner(e) for e in Sched.ENGS})
                return nc

            s1_ready = []
            acc_done = []
            for ti in group:
                t0, tn = TILES[ti]
                kn = min(tn + 6, 128)
                tloc = t0 - g0
                koff = tloc - 2 + HALO
                qoff = tloc + HALO
                mvar = 1 if ti == 0 else (2 if ti == NT - 1 else 0)

                lvev = None
                for half in range(2):
                    def f_v(e, _koff=koff, _kn=kn, _half=half, _XHg=XHg):
                        for dc in range(8):
                            e.matmul(PG[6][0:_kn, 0:512],
                                     _XHg[:, dc, _koff:_koff + _kn],
                                     W_qkv[:, 3 * C * dc + 2048 + 512 * _half:
                                           3 * C * dc + 2048 + 512 * (_half + 1)],
                                     start=(dc == 0), stop=(dc == 7))
                    lv = sch.step("T", f_v, after=xh_ready + qk_done + G("pg6"))

                    def f_vev(e, _kn=kn, _half=half):
                        e.copy(
                            VO[0:_kn, 8 * 66 * _half:8 * 66 * (_half + 1)]
                            .rearrange("p (h c) -> p h c", c=66)[:, :, 0:64],
                            PG[6][0:_kn, 0:512].rearrange("p (h c) -> p h c", c=64))
                    lvev = sch.step("S", f_vev,
                                    after=[lv] + (G("vo") + G("vodbg") if half == 0 else []))
                    setg("pg6", lvev)

                if debug and ti == 0:
                    setg("vodbg", sch.dma("qg", dbg["dbg_vo"], VO[:, 0:1056], after=[lvev]))

                if stop == "T0V" and gi == 0 and ti == 0:
                    sch.step("Y", lambda e: None, after=[lvev])
                    sch.check_deadlock()
                    _emit(nc, {e: sch.runner(e) for e in Sched.ENGS})
                    return nc
                W4 = 4 * TT
                acctp = ti % 2

                def emit_u_cmb(hb, lm_pair, _tn=tn, _acctp=acctp):
                    heads = tuple(range(4 * hb, 4 * hb + 4))
                    p = hb % 2

                    def f_u(e, _heads=heads, _p=p, _tn2=_tn):
                        perm = (0, 2, 1, 3)
                        for j, h in enumerate(_heads):
                            cp = perm[j]
                            e.matmul(PG[4][0:_tn2, 65 * j:65 * j + 65],
                                     EL[:, W4 * _p + TT * cp:W4 * _p + TT * cp + _tn2],
                                     VO[:, 66 * h:66 * h + 65],
                                     start=True, stop=True)
                            e.matmul(PG[5][0:_tn2, 65 * j:65 * j + 65],
                                     ER[:, W4 * _p + TT * cp:W4 * _p + TT * cp + _tn2],
                                     VO[:, 66 * h:66 * h + 65],
                                     start=True, stop=True)
                    lu = sch.step("T", f_u, after=list(lm_pair) + G("pg4") + G("pg5"))
                    setg(f"lastu{p}", lu)

                    def f_cmb(e, _hb=hb, _tn2=_tn, _ap=_acctp):
                        zL = PG[4][0:_tn2, 0:260].rearrange("p (s c) -> p s c", c=65)[:, :, 64:65]
                        zR = PG[5][0:_tn2, 0:260].rearrange("p (s c) -> p s c", c=65)[:, :, 64:65]
                        e.tensor_scalar(RZ[0:_tn2, 0:4].unsqueeze(2), zL, 1e-30, None, ALU.add)
                        e.tensor_scalar(RZ[0:_tn2, 4:8].unsqueeze(2), zR, 1e-30, None, ALU.add)
                        e.drain()
                        e.reciprocal(RZ[0:_tn2, 0:8], RZ[0:_tn2, 0:8])
                        e.drain()
                        uL = PG[4][0:_tn2, 0:260].rearrange("p (s c) -> p s c", c=65)[:, :, 0:64]
                        uR = PG[5][0:_tn2, 0:260].rearrange("p (s c) -> p s c", c=65)[:, :, 0:64]
                        rL = RZ[0:_tn2, 0:4].unsqueeze(2).broadcast_to([_tn2, 4, 64])
                        rR = RZ[0:_tn2, 4:8].unsqueeze(2).broadcast_to([_tn2, 4, 64])
                        o = 256 * (_hb % 2)
                        e.tensor_tensor(TMPL[0:_tn2, o:o + 256]
                                        .rearrange("p (s c) -> p s c", c=64),
                                        uL, rL, ALU.mult)
                        e.tensor_tensor(TMPR[0:_tn2, o:o + 256]
                                        .rearrange("p (s c) -> p s c", c=64),
                                        uR, rR, ALU.mult)
                        e.drain()
                        e.tensor_tensor(ACCT[0:_tn2, 1024 * _ap + 256 * _hb:
                                             1024 * _ap + 256 * (_hb + 1)],
                                        TMPL[0:_tn2, o:o + 256], TMPR[0:_tn2, o:o + 256],
                                        ALU.add)
                    lc = sch.step("V", f_cmb, after=[lu] + G(f"acct{_acctp}"))
                    setg("pg4", lc)
                    setg("pg5", lc)
                    return lu, lc

                pend = None
                batch_last = None
                lu_prev = None
                for hb in range(4):
                    heads = tuple(range(4 * hb, 4 * hb + 4))
                    p = hb % 2

                    def f_gram(e, _heads=heads, _kn=kn, _tn=tn, _koff=koff, _qoff=qoff):
                        for j, h in enumerate(_heads):
                            poff = 64 * (h % 2)
                            bank = h % 2
                            slot = j // 2
                            e.matmul(PG[bank][0:_kn, TT * slot:TT * slot + _tn],
                                     QK[poff:poff + 64, 8 + h // 2, _koff:_koff + _kn],
                                     QK[poff:poff + 64, h // 2, _qoff:_qoff + _tn],
                                     start=True, stop=True, tile_position=(poff, 0))
                    lg = sch.step("T", f_gram, after=[lvev] + qk_done + G("pg0") + G("pg1"))

                    if pend is not None:
                        lu_prev, batch_last = emit_u_cmb(*pend)

                    def f_exp(e, _kn=kn, _p=p):
                        e.activation(EE[0:_kn, W4 * _p:W4 * _p + 2 * TT],
                                     PG[0][0:_kn, 0:2 * TT], AF.Exp, scale=SCALE)
                        e.activation(EE[0:_kn, W4 * _p + 2 * TT:W4 * _p + 4 * TT],
                                     PG[1][0:_kn, 0:2 * TT], AF.Exp, scale=SCALE)
                    le = sch.step("S", f_exp, after=[lg] + G(f"lastm{p}") + G("eedbg"))
                    setg("pg0", le)
                    setg("pg1", le)

                    def f_mL(e, _mv=mvar, _tn=tn, _p=p):
                        src = MKL[:, :].rearrange("p (n t) -> p n t", t=TT)[:, _mv, 0:_tn]
                        e.tensor_tensor(
                            EL[:, W4 * _p:W4 * (_p + 1)]
                            .rearrange("p (i t) -> p i t", t=TT)[:, :, 0:_tn],
                            EE[:, W4 * _p:W4 * (_p + 1)]
                            .rearrange("p (i t) -> p i t", t=TT)[:, :, 0:_tn],
                            src.unsqueeze(1).broadcast_to([128, 4, _tn]), ALU.mult)
                    lmL = sch.step("V", f_mL, after=[le] + G(f"lastu{p}") + G("eldbg"))

                    def f_mR(e, _mv=mvar, _tn=tn, _p=p):
                        src = MKR[:, :].rearrange("p (n t) -> p n t", t=TT)[:, _mv, 0:_tn]
                        e.tensor_tensor(
                            ER[:, W4 * _p:W4 * (_p + 1)]
                            .rearrange("p (i t) -> p i t", t=TT)[:, :, 0:_tn],
                            EE[:, W4 * _p:W4 * (_p + 1)]
                            .rearrange("p (i t) -> p i t", t=TT)[:, :, 0:_tn],
                            src.unsqueeze(1).broadcast_to([128, 4, _tn]), ALU.mult)
                    lmR = sch.step("V", f_mR, after=[le] + G(f"lastu{p}"))
                    setg(f"lastm{p}", lmL, lmR)

                    if debug and ti == 0 and hb == 0:
                        setg("eedbg", sch.dma("qg", dbg["dbg_E"], EE[:, 0:488], after=[lmL, lmR]))
                        setg("eldbg", sch.dma("qg", dbg["dbg_EL"], EL[:, 0:488], after=[lmL, lmR]))

                    pend = (hb, (lmL, lmR))
                lu_prev, batch_last = emit_u_cmb(*pend)

                def f_tr(e, _tn=tn, _ap=acctp):
                    for cc in range(8):
                        e.transpose(PT[0:128, 128 * cc:128 * cc + _tn],
                                    ACCT[0:_tn, 1024 * _ap + 128 * cc:
                                         1024 * _ap + 128 * (cc + 1)],
                                    IDN[0:_tn, 0:_tn])
                ltr = sch.step("T", f_tr, after=[batch_last] + G("pt"))
                setg(f"acct{acctp}", ltr)

                def f_trev(e, _tn=tn, _tloc=tloc):
                    e.copy(ACCF[:, :, _tloc:_tloc + _tn],
                           PT[:, :].rearrange("p (c t) -> p c t", t=128)[:, :, 0:_tn])
                ltrev = sch.step("S", f_trev, after=[ltr] + G("accdbg"))
                setg("pt", ltrev)
                setg("vo", lu_prev)
                acc_done.append(ltrev)

                if debug and ti == 0:
                    setg("accdbg", sch.dma("qg", dbg["dbg_acc"],
                         ACCF[:, :, :].rearrange("p c t -> p (c t)")[:, 0:1024], after=[ltrev]))

            if stop == "G0ATT" and gi == 0:
                sch.step("Y", lambda e: None, after=acc_done)
                sch.check_deadlock()
                _emit(nc, {e: sch.runner(e) for e in Sched.ENGS})
                return nc

            s1_steps = []
            for oc in range(8):
                bk = 6

                def f_wo(e, _oc=oc, _gn=gn, _bk=bk, _cq=cq):
                    for dc in range(8):
                        e.matmul(PG[_bk][:, 0:_gn],
                                 W_out[:, C * dc + 128 * _oc:C * dc + 128 * (_oc + 1)],
                                 ACCF[:, dc, 0:_gn],
                                 start=(dc == 0), stop=False)
                    e.matmul(PG[_bk][:, 0:_gn],
                             B_out[0:1, 128 * _oc:128 * (_oc + 1)],
                             CNTG[0:1, _cq:_cq + _gn],
                             start=False, stop=True)
                lw = sch.step("T", f_wo, after=acc_done + [lcnt] + G(f"pg{bk}"))

                def f_s1(e, _oc=oc, _gn=gn, _bk=bk, _XHg=XHg):
                    e.tensor_tensor(S1[:, _oc, 0:_gn], PG[_bk][:, 0:_gn],
                                    _XHg[:, _oc, HALO:HALO + _gn], ALU.add)
                ls1 = sch.step("V", f_s1, after=[lw] + G("s1dbg"))
                setg(f"pg{bk}", ls1)
                s1_steps.append(ls1)

            if debug and gi == 0:
                addg("s1dbg", sch.dma("qg", dbg["dbg_s1"], S1[:, 0, 0:488], after=s1_steps))

            def f_sq(e, _gn=gn):
                for dc in range(8):
                    e.activation(ACCF[:, dc, 0:_gn], S1[:, dc, 0:_gn], AF.Square)
            lsq = sch.step("S", f_sq, after=s1_steps + G("accdbg"))

            def f_stat(e, _gn=gn):
                for dc in range(8):
                    e.matmul(PG[4][0:1, 0:_gn], ONESC[:, 0:1], S1[:, dc, 0:_gn],
                             start=(dc == 0), stop=(dc == 7))
                for dc in range(8):
                    e.matmul(PG[5][0:1, 0:_gn], ONESC[:, 0:1], ACCF[:, dc, 0:_gn],
                             start=(dc == 0), stop=(dc == 7))
            lst = sch.step("T", f_stat, after=s1_steps + [lsq] + G("pg4") + G("pg5"))

            def f_rows(e, _gn=gn):
                e.tensor_scalar(ROWA[0:1, 0:_gn], PG[4][0:1, 0:_gn], 1.0 / C, None, ALU.mult)
                e.tensor_scalar(ROWB[0:1, 0:_gn], PG[5][0:1, 0:_gn], 1.0 / C, None, ALU.mult)
                e.drain()
                e.tensor_tensor(ROWC[0:1, 0:_gn], ROWA[0:1, 0:_gn], ROWA[0:1, 0:_gn], ALU.mult)
                e.drain()
                e.tensor_tensor(ROWC[0:1, 0:_gn], ROWB[0:1, 0:_gn], ROWC[0:1, 0:_gn],
                                ALU.subtract)
            lrows = sch.step("V", f_rows, after=[lst])

            def f_rs(e, _gn=gn):
                e.activation(ROWB[0:1, 0:_gn], ROWC[0:1, 0:_gn], AF.Ln, bias=EPSC[0:1, 0:1])
                e.drain()
                e.activation(ROWB[0:1, 0:_gn], ROWB[0:1, 0:_gn], AF.Exp, scale=-0.5)
            lrs = sch.step("S", f_rs, after=[lrows])

            def f_rows2(e, _gn=gn):
                e.drain()
                e.tensor_tensor(ROWC[0:1, 0:_gn], ROWA[0:1, 0:_gn], ROWB[0:1, 0:_gn], ALU.mult)
                e.tensor_copy(RBA[0:1, 0:_gn], ROWB[0:1, 0:_gn])
                e.drain()
                e.tensor_copy(RBB[0:1, 0:_gn], ROWC[0:1, 0:_gn])
            lrows2 = sch.step("V", f_rows2, after=[lrs])

            def f_bc(e, _gn=gn):
                e.matmul(PG[2][:, 0:_gn], ONESR[0:1, :], RBA[0:1, 0:_gn], start=True, stop=True)
                e.matmul(PG[3][:, 0:_gn], ONESR[0:1, :], RBB[0:1, 0:_gn], start=True, stop=True)
            lbc = sch.step("T", f_bc, after=[lrows2, lrows] + G("pg2") + G("pg3"))
            setg("pg4", lrows)
            setg("pg5", lrows)

            x1_steps = []
            for dc in range(8):
                def f_ap(e, _dc=dc, _gn=gn):
                    o = 488 * (_dc % 2)
                    e.tensor_tensor(TMPX[:, o:o + _gn], S1[:, _dc, 0:_gn], PG[2][:, 0:_gn],
                                    ALU.mult)
                    e.drain()
                    e.tensor_tensor(X1F[:, _dc, 0:_gn], TMPX[:, o:o + _gn], PG[3][:, 0:_gn],
                                    ALU.subtract)
                x1_steps.append(sch.step("V", f_ap,
                                          after=[lbc] + (G("x1f") if dc == 0 else [])))
            guard["x1f"] = []
            setg("pg2", x1_steps[-1])
            setg("pg3", x1_steps[-1])

            if debug and gi == 0:
                addg("x1f", sch.dma("qg", dbg["dbg_x1"], X1F[:, 0, 0:488], after=x1_steps))

            if stop == "G0X1" and gi == 0:
                sch.step("Y", lambda e: None, after=x1_steps)
                sch.check_deadlock()
                _emit(nc, {e: sch.runner(e) for e in Sched.ENGS})
                return nc

            tile_tail = []
            for ti in group:
                t0, tn = TILES[ti]
                tloc = t0 - g0

                trs = []
                for dc in range(8):
                    trs.append(sch.dma_tr("qy", X1T[0:128, 128 * dc:128 * (dc + 1)],
                                          X1F[:, dc, tloc:tloc + 128],
                                          after=x1_steps + G("x1t")))
                sp1 = sch.dma("qy", d_x1[t0:t0 + tn, :], X1T[0:tn, :], after=trs)
                setg("x1t", sp1)
                x1_spills.append(sp1)

                lgs = []
                lee = None
                for rnd in range(2):
                    def f_lg(e, _tn=tn, _tloc=tloc, _r=rnd):
                        for dc in range(8):
                            for ecl in range(2):
                                e.matmul(PG[2 + ecl][0:_tn, 0:512],
                                         X1F[:, dc, _tloc:_tloc + _tn],
                                         W_phiT[:, E2 * dc + 512 * (2 * _r + ecl):
                                                E2 * dc + 512 * (2 * _r + ecl + 1)],
                                         start=(dc == 0), stop=False)
                        for ecl in range(2):
                            e.matmul(PG[2 + ecl][0:_tn, 0:512],
                                     ONESR[0:1, 0:_tn],
                                     B_phi[0:1, 512 * (2 * _r + ecl):512 * (2 * _r + ecl + 1)],
                                     start=False, stop=True)
                    llg = sch.step("T", f_lg, after=x1_steps + G("pg2") + G("pg3"))
                    lgs.append(llg)

                    def f_eexp(e, _tn=tn, _r=rnd):
                        e.activation(ELOG[0:_tn, 1024 * _r:1024 * _r + 512],
                                     PG[2][0:_tn, 0:512], AF.Exp)
                        e.activation(ELOG[0:_tn, 1024 * _r + 512:1024 * (_r + 1)],
                                     PG[3][0:_tn, 0:512], AF.Exp)
                    lee = sch.step("S", f_eexp,
                                   after=[llg] + (G("elog") if rnd == 0 else []))
                    setg("pg2", lee)
                    setg("pg3", lee)
                guard["elog"] = []

                if debug and ti == 0:
                    addg("elog", sch.dma("qg", dbg["dbg_elog"], ELOG[0:128, :], after=[lee]))

                def f_sumP(e, _tn=tn):
                    e.reduce_sum(RP[0:_tn, :],
                                 ELOG[0:_tn, :].rearrange("p (n s) -> p n s", s=PS), axis=AX.X)
                    e.drain()
                    e.reciprocal(RP[0:_tn, :], RP[0:_tn, :])
                    e.drain()
                lsp = sch.step("V", f_sumP, after=[lee])

                def f_sumN(e, _tn=tn):
                    e.reduce_sum(RN[0:_tn, :],
                                 ELOG[0:_tn, :].rearrange("p (n s) -> p s n", s=PS), axis=AX.X)
                    e.drain()
                    e.reciprocal(RN[0:_tn, :], RN[0:_tn, :])
                    e.drain()
                lrn = sch.step("V", f_sumN, after=[lee])

                def f_D(e, _tn=tn):
                    e.tensor_tensor(DTM[0:_tn, :].rearrange("p (n s) -> p n s", s=PS),
                                    ELOG[0:_tn, :].rearrange("p (n s) -> p n s", s=PS),
                                    RP[0:_tn, :].unsqueeze(2).broadcast_to([_tn, NE, PS]),
                                    ALU.mult)
                lD = sch.step("V", f_D, after=[lsp] + G("dtm"))
                guard["dtm"] = []

                ctp = ti % 2

                def f_C(e, _tn=tn, _cp=ctp):
                    e.tensor_tensor(CTM[0:_tn, E2 * _cp:E2 * (_cp + 1)]
                                    .rearrange("p (n s) -> p s n", s=PS),
                                    ELOG[0:_tn, :].rearrange("p (n s) -> p s n", s=PS),
                                    RN[0:_tn, :].unsqueeze(2).broadcast_to([_tn, PS, NE]),
                                    ALU.mult)
                lC = sch.step("G", f_C, after=[lrn] + G(f"ctm{ctp}"))
                guard[f"ctm{ctp}"] = []
                addg("elog", lD, lC)

                if debug and ti == 0:
                    addg("dtm", sch.dma("qg", dbg["dbg_D"], DTM[0:128, :], after=[lD]))
                    lCbar = sch.step("G", lambda e: e.drain(), after=[lC])
                    addg("ctm0", sch.dma("qg", dbg["dbg_C"], CTM[0:128, 0:E2], after=[lCbar]))

                spD = sch.dma("qy", d_D[t0:t0 + tn, :], DTM[0:tn, :], after=[lD])
                addg("dtm", spD)
                D_spills.append(spD)

                if pend_ct[0] is not None:
                    tile_tail += emit_ct(*pend_ct[0])
                pend_ct[0] = (ti, tn, ctp, lC)

                tile_tail += [lee, lD, lC, spD, lsp, lrn]
                addg("x1f", lgs[-1], *trs)

            prev_group = s1_steps + x1_steps + [lst] + tile_tail + trs
            xh_free[0] = list(s1_steps)
            if stop == "G0" and gi == 0:
                sch.step("Y", lambda e: None, after=prev_group)
                sch.check_deadlock()
                _emit(nc, {e: sch.runner(e) for e in Sched.ENGS})
                return nc

        if stop == "A":
            sch.step("Y", lambda e: None, after=list(prev_group) + D_spills + C_spills + x1_spills)
            sch.check_deadlock()
            runs = {e: sch.runner(e) for e in Sched.ENGS}
            _emit(nc, runs)
            return nc

        if pend_ct[0] is not None:
            prev_group = prev_group + emit_ct(*pend_ct[0])
            pend_ct[0] = None

        SCH = [list(range(i, min(i + 7, NT))) for i in range(0, NT, 7)]
        prevX = list(prev_group) + D_spills[-1:] + x1_spills[-1:]
        xa_done = [None] * 16
        for si, sc in enumerate(SCH):
            loads = []
            for j, ti in enumerate(sc):
                t0, tn = TILES[ti]
                qD = "qy" if j % 2 == 0 else "qg"
                loads.append(sch.dma(qD, DST[0:tn, j, :], d_D[t0:t0 + tn, :],
                                     after=prevX + [D_spills[ti]]))
                loads.append(sch.dma("qs", X1S[0:tn, j, :], d_x1[t0:t0 + tn, :],
                                     after=prevX + [x1_spills[ti]]))
            lx2_list = []
            for ec in range(16):
                bk = 2 * (ec % 2)

                def f_x(e, _sc=tuple(sc), _ec=ec, _bk=bk):
                    for half in range(2):
                        for j, ti in enumerate(_sc):
                            tn = TILES[ti][1]
                            e.matmul(PG[_bk + half][:, 0:512],
                                     DST[0:tn, j, 128 * _ec:128 * (_ec + 1)],
                                     X1S[0:tn, j, 512 * half:512 * (half + 1)],
                                     start=(j == 0), stop=(j == len(_sc) - 1))
                lx2 = sch.step("T", f_x,
                               after=loads + G(f"pg{bk}") + G(f"pg{bk + 1}"))
                lx2_list.append(lx2)

                if si == 0:
                    def f_xev(e, _ec=ec, _bk=bk):
                        e.tensor_copy(XACC[:, C * _ec:C * _ec + 512],
                                      PG[_bk][:, 0:512])
                        e.tensor_copy(XACC[:, C * _ec + 512:C * (_ec + 1)],
                                      PG[_bk + 1][:, 0:512])
                else:
                    def f_xev(e, _ec=ec, _bk=bk):
                        e.tensor_tensor(XACC[:, C * _ec:C * _ec + 512],
                                        XACC[:, C * _ec:C * _ec + 512],
                                        PG[_bk][:, 0:512], ALU.add)
                        e.tensor_tensor(XACC[:, C * _ec + 512:C * (_ec + 1)],
                                        XACC[:, C * _ec + 512:C * (_ec + 1)],
                                        PG[_bk + 1][:, 0:512], ALU.add)
                lev = sch.step("V", f_xev, after=[lx2])
                xa_done[ec] = lev
                setg(f"pg{bk}", lev)
                setg(f"pg{bk + 1}", lev)
            prevX = loads + lx2_list[-2:]

        xp_w = []
        for q, ecs in (("qg", range(0, 6)), ("qy", range(6, 11)),
                       ("qs", range(11, 16))):
            for ec in ecs:
                xp_w.append(sch.dma(q, d_Xp.ap()[128 * ec:128 * (ec + 1), :],
                                    XACC[:, C * ec:C * (ec + 1)],
                                    after=[xa_done[ec]]))

        if debug:
            dcp = sch.dma("qg", dbg["dbg_dD"], d_D, after=xp_w)
            dcp2 = sch.dma("qg", dbg["dbg_dx1"], d_x1, after=xp_w)
            dcp3 = sch.dma("qg", dbg["dbg_XpL"], d_Xp.ap(), after=xp_w)
            xp_w = xp_w + [dcp, dcp2, dcp3]

        def f_rsc(e):
            e.collective_compute("ReduceScatter", ALU.add,
                                 replica_groups=[list(range(NCORES))],
                                 ins=[d_Xp.ap().opt()], outs=[d_Xl.ap().opt()]).then_inc(ccsem, 1)
            e.wait_ge(ccsem, 1)
            e.nop()
        lrsc = sch.step("G", f_rsc, after=xp_w)

        lxl = sch.dma("qy", XLOC[:, :, :], d_Xl.ap().rearrange("(c p) d -> p c d", p=128),
                      after=[lrsc])
        if debug:
            setg("xpdbg", sch.dma("qg", dbg["dbg_Xp"], XLOC[:, 0, :], after=[lxl]))

        lch_done = []
        for ch in range(2):
            def f_cl(e, _ch=ch):
                e.tensor_scalar(XLOC[:, _ch, :], XLOC[:, _ch, :], -33000.0, 65000.0,
                                ALU.max, ALU.min)
                e.drain()
                e.reduce_sum(RZ[:, 0:1], XLOC[:, _ch, :], axis=AX.X)
                e.drain()
                e.tensor_scalar(RZ[:, 0:1], RZ[:, 0:1], 1.0 / C, None, ALU.mult)
                e.drain()
                e.tensor_scalar(XLOC[:, _ch, :], XLOC[:, _ch, :], RZ[:, 0:1], None,
                                ALU.subtract)
                e.drain()
                e.tensor_tensor(XSTG[:, :], XLOC[:, _ch, :], XLOC[:, _ch, :], ALU.mult)
                e.drain()
                e.reduce_sum(RZ[:, 1:2], XSTG[:, :], axis=AX.X)
                e.drain()
                e.tensor_scalar(RZ[:, 1:2], RZ[:, 1:2], 1.0 / C, None, ALU.mult)
            lcl = sch.step("V", f_cl, after=[lxl] + G("xpdbg"))

            def f_crs(e, _ch=ch):
                e.activation(RZ[:, 2:3], RZ[:, 1:2], AF.Ln, bias=EPSC[:, 0:1])
                e.drain()
                e.activation(RZ[:, 2:3], RZ[:, 2:3], AF.Exp, scale=-0.5)
            lcrs = sch.step("S", f_crs, after=[lcl])

            def f_cap(e, _ch=ch):
                e.tensor_scalar(XLOC[:, _ch, :], XLOC[:, _ch, :], RZ[:, 2:3], None, ALU.mult)
            lcap = sch.step("V", f_cap, after=[lcrs])

            def f_cth(e, _ch=ch):
                e.activation(ELOG[:, C * _ch:C * (_ch + 1)], XLOC[:, _ch, :], AF.Tanh)
            lch_done.append(sch.step("S", f_cth, after=[lcap]))

        if debug:
            sch.dma("qg", dbg["dbg_Xln"],
                    ELOG[:, 0:1024], after=lch_done)

        xlt = []
        for pr in range(4):
            ch, npp = pr // 2, pr % 2
            r0 = 64 * npp

            def f_xt(e, _ch=ch, _r0=r0):
                for dc in range(8):
                    e.transpose(PT[0:128, 128 * dc:128 * dc + 64],
                                ELOG[_r0:_r0 + 64, C * _ch + 128 * dc:C * _ch + 128 * (dc + 1)],
                                ID64[_r0:_r0 + 64, 0:64])
            lt1 = sch.step("T", f_xt, after=lch_done + G("pt"))

            def f_xte(e, _ch=ch, _npp=npp):
                for q in range(2):
                    n = 4 * _ch + 2 * _npp + q
                    e.copy(XLNT[:, :].rearrange("p (a s) -> p a s", s=PS)
                           [:, 8 * n:8 * (n + 1), :],
                           PT[:, :].rearrange("p (c s) -> p c s", s=128)
                           [:, :, 32 * q:32 * (q + 1)])
            lte = sch.step("S", f_xte, after=[lt1])
            setg("pt", lte)
            xlt.append(lte)

        yt_w = []
        lwe_prev = None
        for n in range(EPC):
            lwe = sch.dma("qg", EXPW[:, :, :],
                          expw_d[n].rearrange("(c p) k -> p c k", p=128),
                          after=xlt + yt_w[-1:] + prevX)
            lbe = sch.dma("qg", B_ex1[0:1, :], expb_d[n:n + 1, :], after=yt_w[-1:])

            def f_yt(e, _n=n):
                for half in range(2):
                    for dc in range(8):
                        e.matmul(PG[half][0:PS, 0:512],
                                 XLNT[:, :].rearrange("p (a s) -> p a s", s=PS)[:, 8 * _n + dc, :],
                                 EXPW[:, dc, 512 * half:512 * (half + 1)],
                                 start=(dc == 0), stop=False)
                    e.matmul(PG[half][0:PS, 0:512],
                             ONESR[0:1, 0:PS],
                             B_ex1[0:1, 512 * half:512 * (half + 1)],
                             start=False, stop=True)
            lyt = sch.step("T", f_yt, after=[lwe, lbe] + G("pg0") + G("pg1"))

            def f_ye(e):
                e.tensor_copy(YSTG[0:PS, 0:512], PG[0][0:PS, 0:512])
                e.tensor_copy(YSTG[0:PS, 512:1024], PG[1][0:PS, 0:512])
            lye = sch.step("V", f_ye, after=[lyt] + yt_w[-1:])
            setg("pg0", lye)
            setg("pg1", lye)
            yt_w.append(sch.dma("qy", d_Yl.ap()[PS * n:PS * (n + 1), :], YSTG[0:PS, :],
                                after=[lye]))

        def f_ag(e):
            e.collective_compute("AllGather", ALU.bypass,
                                 replica_groups=[list(range(NCORES))],
                                 ins=[d_Yl.ap().opt()], outs=[d_Yt.ap().opt()]).then_inc(ccsem, 1)
            e.wait_ge(ccsem, 2)
            e.nop()
        lag = sch.step("G", f_ag, after=yt_w)

        lytf = sch.dma("qy", W_phiT[:, :].rearrange("p (c k) -> p c k", k=C),
                       d_Yt.ap().rearrange("(c p) k -> p c k", p=128), after=[lag])
        if debug:
            sch.dma("qg", dbg["dbg_Yt"], W_phiT[:, 0:1024], after=[lytf])
            sch.dma("qg", dbg["dbg_YtF"], W_phiT[:, :], after=[lytf])

        if stop == "X":
            sch.step("Y", lambda e: None, after=[lytf])
            sch.check_deadlock()
            runs = {e: sch.runner(e) for e in Sched.ENGS}
            _emit(nc, runs)
            return nc

        out_dmas = []
        pend_o = [None]
        lc_prev = None
        for ti in range(NT):
            t0, tn = TILES[ti]
            lc1 = sch.dma("qy", CFM[:, :], d_C[ti, :, :],
                          after=[lytf, C_spill_by_tile[ti]] + G("cfm"))
            lx1b = sch.dma("qs", X1T[0:tn, :], d_x1[t0:t0 + tn, :],
                           after=[lytf, x1_spills[ti]] + G("x1t"))

            b0 = 0 if ti % 2 == 0 else 2
            b1 = b0 + 1
            so = 1024 * (ti % 2)
            rb = 4 * (ti % 2)

            def f_y(e, _tn=tn, _b0=b0, _b1=b1):
                for ec in range(16):
                    e.matmul(PG[_b0][0:_tn, 0:512],
                             CFM[:, :].rearrange("p (c w) -> p c w", w=124)[:, ec, 0:_tn],
                             W_phiT[:, C * ec:C * ec + 512],
                             start=(ec == 0), stop=(ec == 15))
                for ec in range(16):
                    e.matmul(PG[_b1][0:_tn, 0:512],
                             CFM[:, :].rearrange("p (c w) -> p c w", w=124)[:, ec, 0:_tn],
                             W_phiT[:, C * ec + 512:C * ec + 1024],
                             start=(ec == 0), stop=(ec == 15))
            ly = sch.step("T", f_y, after=[lc1, lytf] + G(f"pg{b0}") + G(f"pg{b1}"))
            setg("cfm", ly)

            def f_s2(e, _tn=tn, _b0=b0, _b1=b1, _so=so, _rb=rb):
                e.scalar_tensor_tensor(S2[0:_tn, _so:_so + 512], PG[_b0][0:_tn, 0:512], 1.0,
                                       X1T[0:_tn, 0:512], ALU.mult, ALU.add)
                e.scalar_tensor_tensor(S2[0:_tn, _so + 512:_so + 1024], PG[_b1][0:_tn, 0:512],
                                       1.0, X1T[0:_tn, 512:1024], ALU.mult, ALU.add)
                e.drain()
                e.reduce_sum(RZ[0:_tn, _rb:_rb + 1], S2[0:_tn, _so:_so + 1024], axis=AX.X)
                e.drain()
                e.tensor_scalar(RZ[0:_tn, _rb:_rb + 1], RZ[0:_tn, _rb:_rb + 1], 1.0 / C,
                                None, ALU.mult)
            ls2 = sch.step("V", f_s2, after=[ly, lx1b] + G(f"s2_{ti % 2}") + G("s2dbg2"))
            setg(f"pg{b0}", ls2)
            setg(f"pg{b1}", ls2)
            setg("x1t", ls2)

            if debug and ti == 0:
                setg("s2dbg2", sch.dma("qg", dbg["dbg_S2"], S2[0:128, 0:1024], after=[ls2]))

            def f_sq2(e, _tn=tn, _so=so, _rb=rb):
                e.activation(YSTG[0:_tn, :], S2[0:_tn, _so:_so + 1024], AF.Square,
                             accum_out=RZ[0:_tn, _rb + 1:_rb + 2])
            lsq2 = sch.step("S", f_sq2, after=[ls2] + G("junk"))

            def f_v2(e, _tn=tn, _rb=rb):
                e.tensor_scalar(RZ[0:_tn, _rb + 1:_rb + 2], RZ[0:_tn, _rb + 1:_rb + 2],
                                1.0 / C, None, ALU.mult)
                e.tensor_tensor(RZ[0:_tn, _rb + 2:_rb + 3], RZ[0:_tn, _rb:_rb + 1],
                                RZ[0:_tn, _rb:_rb + 1], ALU.mult)
                e.drain()
                e.tensor_tensor(RZ[0:_tn, _rb + 1:_rb + 2], RZ[0:_tn, _rb + 1:_rb + 2],
                                RZ[0:_tn, _rb + 2:_rb + 3], ALU.subtract)
            lv2 = sch.step("V", f_v2, after=[lsq2])

            def f_r2(e, _tn=tn, _rb=rb):
                e.activation(RZ[0:_tn, _rb + 3:_rb + 4], RZ[0:_tn, _rb + 1:_rb + 2], AF.Ln,
                             bias=EPSC[0:_tn, 0:1])
                e.drain()
                e.activation(RZ[0:_tn, _rb + 3:_rb + 4], RZ[0:_tn, _rb + 3:_rb + 4], AF.Exp,
                             scale=-0.5)
            lr2 = sch.step("S", f_r2, after=[lv2])
            setg("junk", lr2)

            def f_o(e, _tn=tn, _so=so, _rb=rb):
                e.tensor_scalar(YOUT[0:_tn, :], S2[0:_tn, _so:_so + 1024],
                                RZ[0:_tn, _rb:_rb + 1], RZ[0:_tn, _rb + 3:_rb + 4],
                                ALU.subtract, ALU.mult)
            lo = sch.step("V", f_o, after=[lr2] + G("yout") + G("odbg"))
            setg(f"s2_{ti % 2}", lo)

            if debug and ti == 0:
                setg("odbg", sch.dma("qg", dbg["dbg_O"], YOUT[0:128, :], after=[lo]))
            od = sch.dma("qy", out_d[t0:t0 + tn, :], YOUT[0:tn, :], after=[lo])
            setg("yout", od)
            out_dmas.append(od)

        if pend_o[0] is not None:
            emit_o(*pend_o[0])
        sch.step("Y", lambda e: None, after=out_dmas)

        sch.check_deadlock()
        runs = {e: sch.runner(e) for e in Sched.ENGS}
        _emit(nc, runs)

    return nc


def _emit(nc, runs):
    with nc.Block() as block:
        @block.tensor
        def _(t):
            runs["T"](t)

        @block.vector
        def _(v):
            runs["V"](v)

        @block.scalar
        def _(s):
            runs["S"](s)

        @block.gpsimd
        def _(g):
            runs["G"](g)

        @block.sync
        def _(sy):
            runs["Y"](sy)


def _build_consts():
    bf = ml_dtypes.bfloat16
    ident = np.eye(128, dtype=np.float32).astype(bf)
    ident64 = np.tile(np.eye(64, dtype=np.float32), (2, 1)).astype(bf)
    onescol = np.ones((128, 1), np.float32).astype(bf)
    onesrow = np.ones((1, 128), np.float32).astype(bf)
    maskL = np.zeros((3, 128, TT), np.float32)
    maskR = np.zeros((3, 128, TT), np.float32)
    for mv, (t0, tn) in ((0, TILES[1]), (1, TILES[0]), (2, TILES[-1])):
        for tp in range(tn):
            t = t0 + tp
            p = tp // 2
            gp = t // 2
            if gp - 1 >= 0:
                maskL[mv, 2 * p:2 * p + 4, tp] = 1.0
            if gp <= (L - 4) // 2:
                maskR[mv, 2 * p + 2:2 * p + 6, tp] = 1.0
    cnt = np.zeros((1, L), np.float32)
    W = (L - 4) // 2 + 1
    idx = (np.arange(W)[:, None] * 2 + np.arange(4)[None, :]).reshape(-1)
    np.add.at(cnt[0], idx, 1.0)
    return dict(ident=ident, ident64=ident64, onescol=onescol, onesrow=onesrow,
                maskL=maskL.astype(bf), maskR=maskR.astype(bf), cnt=cnt.astype(bf))


_NC_CACHE = {}


def kernel(L0_attn_x, L0_attn_w_qkv, L0_attn_w_out, L0_attn_b_out,
           L0_ffn_phi_w, L0_ffn_phi_b, L0_ffn_expert_w, L0_ffn_expert_b,
           debug=False, stop="full"):
    x = np.asarray(L0_attn_x, dtype=np.float32)
    wqkv = np.ascontiguousarray(np.asarray(L0_attn_w_qkv, dtype=np.float32))
    wout = np.ascontiguousarray(np.asarray(L0_attn_w_out, dtype=np.float32))
    bout = np.asarray(L0_attn_b_out, dtype=np.float32).reshape(1, C)
    phiT = np.ascontiguousarray(np.asarray(L0_ffn_phi_w, dtype=np.float32).T)
    phib = np.asarray(L0_ffn_phi_b, dtype=np.float32).reshape(1, E2)
    expw = np.ascontiguousarray(np.asarray(L0_ffn_expert_w, dtype=np.float32))
    expb = np.ascontiguousarray(np.asarray(L0_ffn_expert_b, dtype=np.float32))

    consts = _build_consts()
    key = ("nc", debug, stop)
    if key not in _NC_CACHE:
        _NC_CACHE[key] = build_nc(debug=debug, stop=stop)
    nc = _NC_CACHE[key]

    in_maps = []
    for b in range(NCORES):
        in_maps.append(dict(
            xT=np.ascontiguousarray(x[b].T),
            wqkv=wqkv, wout=wout, bout=bout, phiT=phiT, phib=phib,
            expw=np.ascontiguousarray(expw[b * EPC:(b + 1) * EPC]),
            expb=np.ascontiguousarray(expb[b * EPC:(b + 1) * EPC]),
            ident=consts["ident"], ident64=consts["ident64"],
            onescol=consts["onescol"], onesrow=consts["onesrow"],
            maskL=consts["maskL"], maskR=consts["maskR"], cnt=consts["cnt"],
        ))

    res = run_bass_kernel_spmd(nc, in_maps, core_ids=list(range(NCORES)))
    out = np.stack([np.asarray(res.results[b]["out"], dtype=np.float32)
                    for b in range(NCORES)])
    if debug:
        return out, res
    return out


# version 10 (final)
# speedup vs baseline: 1.1937x; 1.0744x over previous
import numpy as np
import ml_dtypes

import concourse.bass as bass
import concourse.mybir as mybir
from concourse.bass_utils import run_bass_kernel_spmd

F32 = mybir.dt.float32
BF16 = mybir.dt.bfloat16
AF = mybir.ActivationFunctionType
ALU = mybir.AluOpType
AX = mybir.AxisListType

NCORES = 8
L = 4096
C = 1024
HEADS = 16
NE = 64
PS = 32
E2 = NE * PS
EPC = NE // NCORES

TT = 122
TILES = [(i * TT, TT) for i in range(L // TT)] + [(L - L % TT, L % TT)]
NT = len(TILES)
GROUPS = [list(range(i, min(i + 4, NT))) for i in range(0, NT, 4)]
HALO = 4
SCALE = 1.0 / 32.0
USE_GP_TT = False


class Sched:
    ENGS = ("T", "V", "S", "G", "Y")
    QENG = {"qy": "Y", "qs": "S", "qg": "G"}

    NROT = 8
    EROT = 4

    def __init__(self):
        self.prog = {e: [] for e in self.ENGS}
        self.cnt = {e: 0 for e in self.ENGS}
        self.dcnt = {(q, j): 0 for q in ("qy", "qs", "qg") for j in range(self.NROT)}
        self.didx = {q: 0 for q in ("qy", "qs", "qg")}
        self.sems = None
        self.dsems = None

    def step(self, eng, fn, after=()):
        deps = []
        for d in after:
            if d is None:
                continue
            key, val = d
            if isinstance(key, tuple):
                deps.append((key, self.dcnt[key]))
            else:
                deps.append(d)
        self.prog[eng].append((tuple(deps), fn))
        self.cnt[eng] += 1
        return (eng, self.cnt[eng])

    def dma(self, q, out_ap, in_ap, after=(), **kw):
        sch = self
        j = self.didx[q] % self.NROT
        self.didx[q] += 1

        def fn(e, _o=out_ap, _i=in_ap, _kw=dict(kw), _k=(q, j)):
            e.dma_start(out=_o, in_=_i, **_kw).then_inc(sch.dsems[_k], 16)
        fn._dmaq = (q, j)
        self.step(self.QENG[q], fn, after=after)
        self.dcnt[(q, j)] += 16
        return ((q, j), self.dcnt[(q, j)])

    def dma_tr(self, q, out_ap, in_ap, after=()):
        sch = self
        j = self.didx[q] % self.NROT
        self.didx[q] += 1

        def fn(e, _o=out_ap, _i=in_ap, _k=(q, j)):
            e.dma_start_transpose(_o, _i).then_inc(sch.dsems[_k], 16)
        fn._dmaq = (q, j)
        self.step(self.QENG[q], fn, after=after)
        self.dcnt[(q, j)] += 16
        return ((q, j), self.dcnt[(q, j)])

    def check_deadlock(self):
        pos = {e: 0 for e in self.ENGS}
        done = {e: 0 for e in self.ENGS}
        dmadone = {k: 0 for k in self.dcnt}
        progressed = True
        while progressed:
            progressed = False
            for e in self.ENGS:
                while pos[e] < len(self.prog[e]):
                    waits, fn = self.prog[e][pos[e]]
                    ok = True
                    for key, val in waits:
                        if key == e:
                            continue
                        if isinstance(key, tuple):
                            if dmadone[key] < val:
                                ok = False
                                break
                        elif done[key] < val:
                            ok = False
                            break
                    if not ok:
                        break
                    pos[e] += 1
                    done[e] += 1
                    dk = getattr(fn, "_dmaq", None)
                    if dk:
                        dmadone[dk] += 16
                    progressed = True
        stuck = {e: (pos[e], len(self.prog[e])) for e in self.ENGS if pos[e] < len(self.prog[e])}
        if stuck:
            detail = {}
            for e in stuck:
                waits, fn = self.prog[e][pos[e]]
                detail[e] = (pos[e], waits[:6])
            raise RuntimeError(f"DEADLOCK: stuck={stuck} blocking_waits={detail}")

    def runner(self, ename):
        chunks = self.prog[ename]
        sch = self

        class Proxy:
            __slots__ = ("_e", "last")

            def __init__(self, e):
                self._e = e
                self.last = None

            def __getattr__(self, n):
                a = getattr(self._e, n)
                if callable(a):
                    px = self

                    def w(*args, **kw):
                        r = a(*args, **kw)
                        if r is not None and hasattr(r, "then_inc"):
                            px.last = r
                        return r
                    return w
                return a

        def run(e):
            px = Proxy(e)
            wm = {}
            nout = 0
            R = sch.EROT
            for waits, fn in chunks:
                for key, val in waits:
                    if key == ename:
                        continue
                    if isinstance(key, tuple):
                        if wm.get(key, 0) >= val:
                            continue
                        e.wait_ge(sch.dsems[key], val)
                        wm[key] = val
                    else:
                        j = (val - 1) % R
                        v = (val - 1) // R + 1
                        wk = (key, j)
                        if wm.get(wk, 0) >= v:
                            continue
                        e.wait_ge(sch.sems[(key, j)], v)
                        wm[wk] = v
                px.last = None
                fn(px)
                sem = sch.sems[(ename, nout % R)]
                nout += 1
                if getattr(fn, "_dmaq", None) is not None or px.last is None:
                    e.nop().then_inc(sem, 1)
                else:
                    px.last.then_inc(sem, 1)
        return run


class Arena:

    def __init__(self, nc_sbt, name, dtype):
        self.sbt = nc_sbt
        self.name = name
        self.dtype = dtype
        self.regions = {}
        self.hwm = 0

    def alloc(self, tag, cols):
        cur = self.regions.get(tag, 0)
        off = cur
        self.regions[tag] = cur + cols
        self.hwm = max(self.hwm, cur + cols)
        return off


def build_nc(debug=False, stop="full"):
    nc = bass.Bass("TRN2", target_bir_lowering=False, debug=False)

    xT = nc.dram_tensor("xT", [C, L], F32, kind="ExternalInput").ap()
    wqkv_d = nc.dram_tensor("wqkv", [C, 3 * C], F32, kind="ExternalInput").ap()
    wout_d = nc.dram_tensor("wout", [C, C], F32, kind="ExternalInput").ap()
    bout_d = nc.dram_tensor("bout", [1, C], F32, kind="ExternalInput").ap()
    phiT_d = nc.dram_tensor("phiT", [C, E2], F32, kind="ExternalInput").ap()
    phib_d = nc.dram_tensor("phib", [1, E2], F32, kind="ExternalInput").ap()
    expw_d = nc.dram_tensor("expw", [EPC, C, C], F32, kind="ExternalInput").ap()
    expb_d = nc.dram_tensor("expb", [EPC, C], F32, kind="ExternalInput").ap()
    ident_d = nc.dram_tensor("ident", [128, 128], BF16, kind="ExternalInput").ap()
    ident64_d = nc.dram_tensor("ident64", [128, 64], BF16, kind="ExternalInput").ap()
    onescol_d = nc.dram_tensor("onescol", [128, 1], BF16, kind="ExternalInput").ap()
    onesrow_d = nc.dram_tensor("onesrow", [1, 128], BF16, kind="ExternalInput").ap()
    maskL_d = nc.dram_tensor("maskL", [3, 128, TT], BF16, kind="ExternalInput").ap()
    maskR_d = nc.dram_tensor("maskR", [3, 128, TT], BF16, kind="ExternalInput").ap()
    cnt_d = nc.dram_tensor("cnt", [1, L], BF16, kind="ExternalInput").ap()

    out_d = nc.dram_tensor("out", [L, C], F32, kind="ExternalOutput").ap()

    d_x1 = nc.dram_tensor("d_x1", [L, C], BF16).ap()
    d_D = nc.dram_tensor("d_D", [L, E2], BF16).ap()
    d_C = nc.dram_tensor("d_C", [NT, 128, 16 * 124], BF16).ap()
    d_Xp = nc.dram_tensor("d_Xp", [E2, C], F32)
    d_Xl = nc.dram_tensor("d_Xl", [E2 // NCORES, C], F32)
    d_Yl = nc.dram_tensor("d_Yl", [E2 // NCORES, C], BF16)
    d_Yt = nc.dram_tensor("d_Yt", [E2, C], BF16)

    dbg = {}
    if debug:
        for name, shape, dt in [
            ("dbg_qk", [128, 496], F32), ("dbg_E", [128, 488], F32),
            ("dbg_EL", [128, 488], F32), ("dbg_vo", [128, 1056], F32),
            ("dbg_acc", [128, 1024], F32), ("dbg_s1", [128, 488], F32),
            ("dbg_x1", [128, 488], F32),
            ("dbg_elog", [128, 2048], F32), ("dbg_D", [128, 2048], F32),
            ("dbg_C", [128, 2048], F32), ("dbg_cfm", [128, 1984], F32),
            ("dbg_Xp", [128, 1024], F32), ("dbg_Xln", [128, 1024], F32),
            ("dbg_Yt", [128, 1024], F32), ("dbg_S2", [128, 1024], F32),
            ("dbg_YtF", [128, 16384], F32),
        ]:
            dbg[name] = nc.dram_tensor(name, shape, dt, kind="ExternalOutput").ap()
        for name, shape, dt in [
            ("dbg_dD", [L, E2], BF16), ("dbg_dx1", [L, C], BF16),
            ("dbg_XpL", [E2, C], F32),
            ("dbg_O", [128, 1024], F32),
        ]:
            dbg[name] = nc.dram_tensor(name, shape, dt, kind="ExternalOutput").ap()

    import contextlib
    stack = contextlib.ExitStack()
    sbt = lambda name, shape, dt: stack.enter_context(nc.sbuf_tensor(name, shape, dt))
    pst = lambda name, shape, dt=F32: stack.enter_context(nc.psum_tensor(name, shape, dt))

    with stack:
        WXB = sbt("WXB", [128, 8 * 3 * C * 2 + 8 * C * 2], mybir.dt.uint8)
        _wxb_addr = nc.lookup_mloc(WXB).addr

        def _overlay(name, shape, dtype, off):
            return nc.alloc_sbuf_tensor_at(name, shape, dtype,
                                           offset=_wxb_addr + off)

        W_qkv = _overlay("W_qkv", [128, 8 * 3 * C], BF16, 0)
        W_out = _overlay("W_out", [128, 8 * C], BF16, 8 * 3 * C * 2)
        XACC = _overlay("XACC", [128, 16 * C], F32, 0)
        W_phiT = sbt("W_phiT", [128, 8 * E2], BF16)
        B_out = sbt("B_out", [1, C], BF16)
        B_phi = sbt("B_phi", [1, E2], BF16)
        B_ex1 = sbt("B_ex1", [1, C], BF16)
        IDN = sbt("IDN", [128, 128], BF16)
        ID64 = sbt("ID64", [128, 64], BF16)
        ONESC = sbt("ONESC", [128, 1], BF16)
        ONESR = sbt("ONESR", [1, 128], BF16)
        MKL = sbt("MKL", [128, 3 * TT], BF16)
        MKR = sbt("MKR", [128, 3 * TT], BF16)
        CNTG = sbt("CNTG", [1, 512], BF16)
        EPSC = sbt("EPSC", [128, 1], F32)

        ab = Arena(None, "AB", BF16)
        o_QK = ab.alloc("A", 16 * 496)
        o_XH = ab.alloc("A", 8 * 496)
        o_S1 = ab.alloc("A", 8 * 488)
        o_X1F = ab.alloc("A", 8 * 512)
        o_ACCF = ab.alloc("A", 8 * 488)
        o_DST = ab.alloc("X", 7 * E2)
        o_X1S = ab.alloc("X", 7 * C)
        AB = sbt("AB", [128, ab.hwm], BF16)
        QK = AB[:, o_QK:o_QK + 16 * 496].rearrange("p (c t) -> p c t", t=496)
        XH = AB[:, o_XH:o_XH + 8 * 496].rearrange("p (c t) -> p c t", t=496)
        S1 = AB[:, o_S1:o_S1 + 8 * 488].rearrange("p (c t) -> p c t", t=488)
        X1F = AB[:, o_X1F:o_X1F + 8 * 512].rearrange("p (c t) -> p c t", t=512)
        ACCF = AB[:, o_ACCF:o_ACCF + 8 * 488].rearrange("p (c t) -> p c t", t=488)
        DST = AB[:, o_DST:o_DST + 7 * E2].rearrange("p (j e) -> p j e", e=E2)
        X1S = AB[:, o_X1S:o_X1S + 7 * C].rearrange("p (j e) -> p j e", e=C)
        EXPW = AB[:, o_DST:o_DST + 8 * C].rearrange("p (j e) -> p j e", e=C)

        af = Arena(None, "AF", F32)
        o_TMPX = af.alloc("A", 2 * 488)
        o_TMPL = af.alloc("A", 2 * 256)
        o_TMPR = af.alloc("A", 2 * 256)
        o_XLOC = af.alloc("X", 2 * C)
        o_XSTG = af.alloc("X", 1024)
        o_S2 = af.alloc("B", 2 * 1024)
        o_YOUT = af.alloc("B", 1024)
        AF_ = sbt("AF_", [128, af.hwm], F32)
        TMPX = AF_[:, o_TMPX:o_TMPX + 2 * 488]
        TMPL = AF_[:, o_TMPL:o_TMPL + 2 * 256]
        TMPR = AF_[:, o_TMPR:o_TMPR + 2 * 256]
        XLOC = AF_[:, o_XLOC:o_XLOC + 2 * C].rearrange("p (c t) -> p c t", t=C)
        XSTG = AF_[:, o_XSTG:o_XSTG + 1024]
        S2 = AF_[:, o_S2:o_S2 + 2 * 1024]
        YOUT = AF_[:, o_YOUT:o_YOUT + 1024]

        VO = sbt("VO", [128, 16 * 66], BF16)
        EE = sbt("EE", [128, 2 * 4 * TT], BF16)
        EL = sbt("EL", [128, 2 * 4 * TT], BF16)
        ER = sbt("ER", [128, 2 * 4 * TT], BF16)
        ACCT = sbt("ACCT", [128, 2048], BF16)
        RZ = sbt("RZ", [128, 8], F32)
        ELOG = sbt("ELOG", [128, E2], BF16)
        RP = sbt("RP", [128, NE], F32)
        RN = sbt("RN", [128, PS], F32)
        DTM = sbt("DTM", [128, E2], BF16)
        CTM = sbt("CTM", [128, 2 * E2], BF16)
        CFM = sbt("CFM", [128, 16 * 124], BF16)
        X1T = sbt("X1T", [128, C], BF16)
        XLNT = AB[:, o_X1S:o_X1S + EPC * 8 * PS]
        YSTG = AB[:, o_X1S + EPC * 8 * PS:o_X1S + EPC * 8 * PS + C]
        ROWA = sbt("ROWA", [1, 512], F32)
        ROWB = sbt("ROWB", [1, 512], F32)
        ROWC = sbt("ROWC", [1, 512], F32)
        RBA = sbt("RBA", [1, 512], BF16)
        RBB = sbt("RBB", [1, 512], BF16)

        PG = [pst(f"PG{i}", [128, 512]) for i in range(7)]
        PT = pst("PT", [128, 1024], BF16)

        sems = {(e, j): stack.enter_context(nc.semaphore(f"sem{e}{j}"))
                for e in Sched.ENGS for j in range(Sched.EROT)}
        dsems = {(q, j): stack.enter_context(nc.semaphore(f"dma{q}{j}"))
                 for q in ("qy", "qs", "qg") for j in range(Sched.NROT)}
        ccsem = stack.enter_context(nc.semaphore("ccsem"))

        sch = Sched()
        sch.sems = sems
        sch.dsems = dsems

        guard = {}

        def G(key):
            return list(guard.get(key, ()))

        def setg(key, *labels):
            guard[key] = list(labels)

        def addg(key, *labels):
            guard.setdefault(key, []).extend(labels)

        p0 = []
        p0.append(sch.dma("qg", IDN[:, :], ident_d))
        p0.append(sch.dma("qg", ID64[:, :], ident64_d))
        p0.append(sch.dma("qg", ONESC[:, :], onescol_d))
        p0.append(sch.dma("qg", ONESR[:, :], onesrow_d))
        p0.append(sch.dma("qg", MKL[:, :].rearrange("p (n t) -> p n t", t=TT),
                          maskL_d.rearrange("n p t -> p n t")))
        p0.append(sch.dma("qg", MKR[:, :].rearrange("p (n t) -> p n t", t=TT),
                          maskR_d.rearrange("n p t -> p n t")))
        p0.append(sch.dma("qg", B_out[:, :], bout_d))
        p0.append(sch.dma("qg", B_phi[:, :], phib_d))
        p0.append(sch.dma("qg", W_qkv[:, :].rearrange("p (c k) -> p c k", k=3 * C),
                          wqkv_d.rearrange("(c p) k -> p c k", p=128)))
        p0.append(sch.dma("qg", W_out[:, :].rearrange("p (c k) -> p c k", k=C),
                          wout_d.rearrange("(c p) k -> p c k", p=128)))
        p0.append(sch.dma("qg", W_phiT[:, :].rearrange("p (c k) -> p c k", k=E2),
                          phiT_d.rearrange("(c p) k -> p c k", p=128)))

        def f_init(e):
            e.memset(EPSC[:, :], 1e-5)
            e.memset(EE[:, :], 0.0)
            e.memset(VO[:, :], 1.0)
            e.memset(AB[:, :], 0.0)
            e.memset(AF_[:, :], 0.0)
        l_init = sch.step("V", f_init)

        def f_init2(e):
            for t in (EL, ER, ACCT, ELOG, RP, RN, DTM, CTM, CFM, X1T,
                      RZ, ROWA, ROWB, ROWC, RBA, RBB):
                e.memset(t[:, :], 0.0)
        l_init2 = sch.step("G", f_init2)
        consts = p0 + [l_init, l_init2]

        if stop == "P0":
            sch.step("Y", lambda e: None, after=consts)
            sch.check_deadlock()
            _emit(nc, {e: sch.runner(e) for e in Sched.ENGS})
            return nc

        x1_spills = []
        D_spills = []
        C_spills = []
        prev_group = list(consts)
        pend_ct = [None]
        C_spill_by_tile = {}

        def emit_ct(ti2, tn2, cp2, lC2):
            cts = []
            for rnd in range(2):
                def f_ct(e, _rnd=rnd, _tn=tn2, _cp=cp2):
                    for cc in range(8):
                        e.transpose(PT[0:128, 128 * cc:128 * cc + _tn],
                                    CTM[0:_tn, E2 * _cp + 128 * (8 * _rnd + cc):
                                        E2 * _cp + 128 * (8 * _rnd + cc + 1)],
                                    IDN[0:_tn, 0:_tn])
                lct = sch.step("T", f_ct, after=[lC2] + G("pt"))

                def f_cev(e, _rnd=rnd, _tn=tn2):
                    e.copy(CFM[:, :].rearrange("p (c w) -> p c w", w=124)
                           [:, 8 * _rnd:8 * (_rnd + 1), 0:_tn],
                           PT[:, :].rearrange("p (c t) -> p c t", t=128)[:, :, 0:_tn])
                lcev = sch.step("S", f_cev, after=[lct] + G("cfm"))
                setg("pt", lcev)
                cts.append(lcev)
            addg(f"ctm{cp2}", cts[-1])
            spC = sch.dma("qy", d_C[ti2, :, :], CFM[:, :], after=cts)
            setg("cfm", spC)
            C_spill_by_tile[ti2] = spC
            C_spills.append(spC)
            return [spC]

        xh_free = {}
        for gi, group in enumerate(GROUPS):
            g0 = TILES[group[0]][0]
            gn = sum(TILES[t][1] for t in group)
            h0, hn = g0 - HALO, gn + 2 * HALO
            XHg = XH
            cq = 0
            dep_xh = xh_free.get(0, list(consts))

            c0, c1 = max(h0, 0), min(h0 + hn, L)
            off = c0 - h0
            lh = sch.dma("qg", XHg[:, :, off:off + (c1 - c0)],
                         xT.rearrange("(c p) t -> p c t", p=128)[:, :, c0:c1],
                         after=dep_xh)
            lcnt = sch.dma("qg", CNTG[0:1, cq:cq + gn], cnt_d[0:1, g0:g0 + gn],
                           after=dep_xh)
            xh_ready = [lh]
            if off > 0 or off + (c1 - c0) < hn:
                def f_pad(e, _off=off, _w=c1 - c0, _hn=hn, _XHg=XHg):
                    if _off > 0:
                        e.memset(_XHg[:, :, 0:_off], 0.0)
                    if _off + _w < _hn:
                        e.memset(_XHg[:, :, _off + _w:_hn], 0.0)
                xh_ready.append(sch.step("G", f_pad, after=[lh]))

            qk_ready = []
            for qc in range(16):
                bank = qc % 2

                def f_qk(e, _qc=qc, _hn=hn, _bank=bank, _XHg=XHg):
                    for dc in range(8):
                        e.matmul(PG[_bank][:, 0:_hn],
                                 W_qkv[:, 3 * C * dc + 128 * _qc:3 * C * dc + 128 * (_qc + 1)],
                                 _XHg[:, dc, 0:_hn],
                                 start=(dc == 0), stop=(dc == 7))
                lm = sch.step("T", f_qk, after=xh_ready + G(f"pg{bank}"))

                def f_ev(e, _qc=qc, _hn=hn, _bank=bank):
                    e.copy(QK[:, _qc, 0:_hn], PG[_bank][:, 0:_hn])
                ev = sch.step("S", f_ev, after=[lm])
                setg(f"pg{bank}", ev)
                qk_ready.append(ev)
            qk_done = qk_ready[-2:]

            if debug and gi == 0:
                sch.dma("qg", dbg["dbg_qk"], QK[:, 8, 0:496], after=qk_ready)

            if stop == "G0QK" and gi == 0:
                sch.step("Y", lambda e: None, after=qk_ready)
                sch.check_deadlock()
                _emit(nc, {e: sch.runner(e) for e in Sched.ENGS})
                return nc

            s1_ready = []
            acc_done = []
            for ti in group:
                t0, tn = TILES[ti]
                kn = min(tn + 6, 128)
                tloc = t0 - g0
                koff = tloc - 2 + HALO
                qoff = tloc + HALO
                mvar = 1 if ti == 0 else (2 if ti == NT - 1 else 0)

                lvev = None
                for half in range(2):
                    def f_v(e, _koff=koff, _kn=kn, _half=half, _XHg=XHg):
                        for dc in range(8):
                            e.matmul(PG[6][0:_kn, 0:512],
                                     _XHg[:, dc, _koff:_koff + _kn],
                                     W_qkv[:, 3 * C * dc + 2048 + 512 * _half:
                                           3 * C * dc + 2048 + 512 * (_half + 1)],
                                     start=(dc == 0), stop=(dc == 7))
                    lv = sch.step("T", f_v, after=xh_ready + qk_done + G("pg6"))

                    def f_vev(e, _kn=kn, _half=half):
                        e.copy(
                            VO[0:_kn, 8 * 66 * _half:8 * 66 * (_half + 1)]
                            .rearrange("p (h c) -> p h c", c=66)[:, :, 0:64],
                            PG[6][0:_kn, 0:512].rearrange("p (h c) -> p h c", c=64))
                    lvev = sch.step("S", f_vev,
                                    after=[lv] + (G("vo") + G("vodbg") if half == 0 else []))
                    setg("pg6", lvev)

                if debug and ti == 0:
                    setg("vodbg", sch.dma("qg", dbg["dbg_vo"], VO[:, 0:1056], after=[lvev]))

                if stop == "T0V" and gi == 0 and ti == 0:
                    sch.step("Y", lambda e: None, after=[lvev])
                    sch.check_deadlock()
                    _emit(nc, {e: sch.runner(e) for e in Sched.ENGS})
                    return nc
                W4 = 4 * TT
                acctp = ti % 2

                def emit_u_cmb(hb, lm_pair, _tn=tn, _acctp=acctp):
                    heads = tuple(range(4 * hb, 4 * hb + 4))
                    p = hb % 2

                    def f_u(e, _heads=heads, _p=p, _tn2=_tn):
                        perm = (0, 2, 1, 3)
                        for j, h in enumerate(_heads):
                            cp = perm[j]
                            e.matmul(PG[4][0:_tn2, 65 * j:65 * j + 65],
                                     EL[:, W4 * _p + TT * cp:W4 * _p + TT * cp + _tn2],
                                     VO[:, 66 * h:66 * h + 65],
                                     start=True, stop=True)
                            e.matmul(PG[5][0:_tn2, 65 * j:65 * j + 65],
                                     ER[:, W4 * _p + TT * cp:W4 * _p + TT * cp + _tn2],
                                     VO[:, 66 * h:66 * h + 65],
                                     start=True, stop=True)
                    lu = sch.step("T", f_u, after=list(lm_pair) + G("pg4") + G("pg5"))
                    setg(f"lastu{p}", lu)

                    def f_cmb(e, _hb=hb, _tn2=_tn, _ap=_acctp):
                        zL = PG[4][0:_tn2, 0:260].rearrange("p (s c) -> p s c", c=65)[:, :, 64:65]
                        zR = PG[5][0:_tn2, 0:260].rearrange("p (s c) -> p s c", c=65)[:, :, 64:65]
                        e.tensor_scalar(RZ[0:_tn2, 0:4].unsqueeze(2), zL, 1e-30, None, ALU.add)
                        e.tensor_scalar(RZ[0:_tn2, 4:8].unsqueeze(2), zR, 1e-30, None, ALU.add)
                        e.drain()
                        e.reciprocal(RZ[0:_tn2, 0:8], RZ[0:_tn2, 0:8])
                        e.drain()
                        uL = PG[4][0:_tn2, 0:260].rearrange("p (s c) -> p s c", c=65)[:, :, 0:64]
                        uR = PG[5][0:_tn2, 0:260].rearrange("p (s c) -> p s c", c=65)[:, :, 0:64]
                        rL = RZ[0:_tn2, 0:4].unsqueeze(2).broadcast_to([_tn2, 4, 64])
                        rR = RZ[0:_tn2, 4:8].unsqueeze(2).broadcast_to([_tn2, 4, 64])
                        o = 256 * (_hb % 2)
                        e.tensor_tensor(TMPL[0:_tn2, o:o + 256]
                                        .rearrange("p (s c) -> p s c", c=64),
                                        uL, rL, ALU.mult)
                        e.tensor_tensor(TMPR[0:_tn2, o:o + 256]
                                        .rearrange("p (s c) -> p s c", c=64),
                                        uR, rR, ALU.mult)
                        e.drain()
                        e.tensor_tensor(ACCT[0:_tn2, 1024 * _ap + 256 * _hb:
                                             1024 * _ap + 256 * (_hb + 1)],
                                        TMPL[0:_tn2, o:o + 256], TMPR[0:_tn2, o:o + 256],
                                        ALU.add)
                    lc = sch.step("V", f_cmb, after=[lu] + G(f"acct{_acctp}"))
                    setg("pg4", lc)
                    setg("pg5", lc)
                    return lu, lc

                pend = None
                batch_last = None
                lu_prev = None
                for hb in range(4):
                    heads = tuple(range(4 * hb, 4 * hb + 4))
                    p = hb % 2

                    def f_gram(e, _heads=heads, _kn=kn, _tn=tn, _koff=koff, _qoff=qoff):
                        for j, h in enumerate(_heads):
                            poff = 64 * (h % 2)
                            bank = h % 2
                            slot = j // 2
                            e.matmul(PG[bank][0:_kn, TT * slot:TT * slot + _tn],
                                     QK[poff:poff + 64, 8 + h // 2, _koff:_koff + _kn],
                                     QK[poff:poff + 64, h // 2, _qoff:_qoff + _tn],
                                     start=True, stop=True, tile_position=(poff, 0))
                    lg = sch.step("T", f_gram, after=[lvev] + qk_done + G("pg0") + G("pg1"))

                    if pend is not None:
                        lu_prev, batch_last = emit_u_cmb(*pend)

                    def f_exp(e, _kn=kn, _p=p):
                        e.activation(EE[0:_kn, W4 * _p:W4 * _p + 2 * TT],
                                     PG[0][0:_kn, 0:2 * TT], AF.Exp, scale=SCALE)
                        e.activation(EE[0:_kn, W4 * _p + 2 * TT:W4 * _p + 4 * TT],
                                     PG[1][0:_kn, 0:2 * TT], AF.Exp, scale=SCALE)
                    le = sch.step("S", f_exp, after=[lg] + G(f"lastm{p}") + G("eedbg"))
                    setg("pg0", le)
                    setg("pg1", le)

                    def f_mL(e, _mv=mvar, _tn=tn, _p=p):
                        src = MKL[:, :].rearrange("p (n t) -> p n t", t=TT)[:, _mv, 0:_tn]
                        e.tensor_tensor(
                            EL[:, W4 * _p:W4 * (_p + 1)]
                            .rearrange("p (i t) -> p i t", t=TT)[:, :, 0:_tn],
                            EE[:, W4 * _p:W4 * (_p + 1)]
                            .rearrange("p (i t) -> p i t", t=TT)[:, :, 0:_tn],
                            src.unsqueeze(1).broadcast_to([128, 4, _tn]), ALU.mult)
                    lmL = sch.step("V", f_mL, after=[le] + G(f"lastu{p}") + G("eldbg"))

                    def f_mR(e, _mv=mvar, _tn=tn, _p=p):
                        src = MKR[:, :].rearrange("p (n t) -> p n t", t=TT)[:, _mv, 0:_tn]
                        e.tensor_tensor(
                            ER[:, W4 * _p:W4 * (_p + 1)]
                            .rearrange("p (i t) -> p i t", t=TT)[:, :, 0:_tn],
                            EE[:, W4 * _p:W4 * (_p + 1)]
                            .rearrange("p (i t) -> p i t", t=TT)[:, :, 0:_tn],
                            src.unsqueeze(1).broadcast_to([128, 4, _tn]), ALU.mult)
                    lmR = sch.step("V", f_mR, after=[le] + G(f"lastu{p}"))
                    setg(f"lastm{p}", lmL, lmR)

                    if debug and ti == 0 and hb == 0:
                        setg("eedbg", sch.dma("qg", dbg["dbg_E"], EE[:, 0:488], after=[lmL, lmR]))
                        setg("eldbg", sch.dma("qg", dbg["dbg_EL"], EL[:, 0:488], after=[lmL, lmR]))

                    pend = (hb, (lmL, lmR))
                lu_prev, batch_last = emit_u_cmb(*pend)

                def f_tr(e, _tn=tn, _ap=acctp):
                    for cc in range(8):
                        e.transpose(PT[0:128, 128 * cc:128 * cc + _tn],
                                    ACCT[0:_tn, 1024 * _ap + 128 * cc:
                                         1024 * _ap + 128 * (cc + 1)],
                                    IDN[0:_tn, 0:_tn])
                ltr = sch.step("T", f_tr, after=[batch_last] + G("pt"))
                setg(f"acct{acctp}", ltr)

                def f_trev(e, _tn=tn, _tloc=tloc):
                    e.copy(ACCF[:, :, _tloc:_tloc + _tn],
                           PT[:, :].rearrange("p (c t) -> p c t", t=128)[:, :, 0:_tn])
                ltrev = sch.step("S", f_trev, after=[ltr] + G("accdbg"))
                setg("pt", ltrev)
                setg("vo", lu_prev)
                acc_done.append(ltrev)

                if debug and ti == 0:
                    setg("accdbg", sch.dma("qg", dbg["dbg_acc"],
                         ACCF[:, :, :].rearrange("p c t -> p (c t)")[:, 0:1024], after=[ltrev]))

            if stop == "G0ATT" and gi == 0:
                sch.step("Y", lambda e: None, after=acc_done)
                sch.check_deadlock()
                _emit(nc, {e: sch.runner(e) for e in Sched.ENGS})
                return nc

            s1_steps = []
            for oc in range(8):
                bk = 6

                def f_wo(e, _oc=oc, _gn=gn, _bk=bk, _cq=cq):
                    for dc in range(8):
                        e.matmul(PG[_bk][:, 0:_gn],
                                 W_out[:, C * dc + 128 * _oc:C * dc + 128 * (_oc + 1)],
                                 ACCF[:, dc, 0:_gn],
                                 start=(dc == 0), stop=False)
                    e.matmul(PG[_bk][:, 0:_gn],
                             B_out[0:1, 128 * _oc:128 * (_oc + 1)],
                             CNTG[0:1, _cq:_cq + _gn],
                             start=False, stop=True)
                lw = sch.step("T", f_wo, after=acc_done + [lcnt] + G(f"pg{bk}"))

                def f_s1(e, _oc=oc, _gn=gn, _bk=bk, _XHg=XHg):
                    e.tensor_tensor(S1[:, _oc, 0:_gn], PG[_bk][:, 0:_gn],
                                    _XHg[:, _oc, HALO:HALO + _gn], ALU.add)
                ls1 = sch.step("V", f_s1, after=[lw] + G("s1dbg"))
                setg(f"pg{bk}", ls1)
                s1_steps.append(ls1)

            if debug and gi == 0:
                addg("s1dbg", sch.dma("qg", dbg["dbg_s1"], S1[:, 0, 0:488], after=s1_steps))

            def f_sq(e, _gn=gn):
                for dc in range(8):
                    e.activation(ACCF[:, dc, 0:_gn], S1[:, dc, 0:_gn], AF.Square)
            lsq = sch.step("S", f_sq, after=s1_steps + G("accdbg"))

            def f_stat(e, _gn=gn):
                for dc in range(8):
                    e.matmul(PG[4][0:1, 0:_gn], ONESC[:, 0:1], S1[:, dc, 0:_gn],
                             start=(dc == 0), stop=(dc == 7))
                for dc in range(8):
                    e.matmul(PG[5][0:1, 0:_gn], ONESC[:, 0:1], ACCF[:, dc, 0:_gn],
                             start=(dc == 0), stop=(dc == 7))
            lst = sch.step("T", f_stat, after=s1_steps + [lsq] + G("pg4") + G("pg5"))

            def f_rows(e, _gn=gn):
                e.tensor_scalar(ROWA[0:1, 0:_gn], PG[4][0:1, 0:_gn], 1.0 / C, None, ALU.mult)
                e.tensor_scalar(ROWB[0:1, 0:_gn], PG[5][0:1, 0:_gn], 1.0 / C, None, ALU.mult)
                e.drain()
                e.tensor_tensor(ROWC[0:1, 0:_gn], ROWA[0:1, 0:_gn], ROWA[0:1, 0:_gn], ALU.mult)
                e.drain()
                e.tensor_tensor(ROWC[0:1, 0:_gn], ROWB[0:1, 0:_gn], ROWC[0:1, 0:_gn],
                                ALU.subtract)
            lrows = sch.step("V", f_rows, after=[lst])

            def f_rs(e, _gn=gn):
                e.activation(ROWB[0:1, 0:_gn], ROWC[0:1, 0:_gn], AF.Ln, bias=EPSC[0:1, 0:1])
                e.drain()
                e.activation(ROWB[0:1, 0:_gn], ROWB[0:1, 0:_gn], AF.Exp, scale=-0.5)
            lrs = sch.step("S", f_rs, after=[lrows])

            def f_rows2(e, _gn=gn):
                e.drain()
                e.tensor_tensor(ROWC[0:1, 0:_gn], ROWA[0:1, 0:_gn], ROWB[0:1, 0:_gn], ALU.mult)
                e.tensor_copy(RBA[0:1, 0:_gn], ROWB[0:1, 0:_gn])
                e.drain()
                e.tensor_copy(RBB[0:1, 0:_gn], ROWC[0:1, 0:_gn])
            lrows2 = sch.step("V", f_rows2, after=[lrs])

            def f_bc(e, _gn=gn):
                e.matmul(PG[2][:, 0:_gn], ONESR[0:1, :], RBA[0:1, 0:_gn], start=True, stop=True)
                e.matmul(PG[3][:, 0:_gn], ONESR[0:1, :], RBB[0:1, 0:_gn], start=True, stop=True)
            lbc = sch.step("T", f_bc, after=[lrows2, lrows] + G("pg2") + G("pg3"))
            setg("pg4", lrows)
            setg("pg5", lrows)

            x1_steps = []
            for dc in range(8):
                def f_ap(e, _dc=dc, _gn=gn):
                    o = 488 * (_dc % 2)
                    e.tensor_tensor(TMPX[:, o:o + _gn], S1[:, _dc, 0:_gn], PG[2][:, 0:_gn],
                                    ALU.mult)
                    e.drain()
                    e.tensor_tensor(X1F[:, _dc, 0:_gn], TMPX[:, o:o + _gn], PG[3][:, 0:_gn],
                                    ALU.subtract)
                x1_steps.append(sch.step("V", f_ap,
                                          after=[lbc] + (G("x1f") if dc == 0 else [])))
            guard["x1f"] = []
            setg("pg2", x1_steps[-1])
            setg("pg3", x1_steps[-1])

            if debug and gi == 0:
                addg("x1f", sch.dma("qg", dbg["dbg_x1"], X1F[:, 0, 0:488], after=x1_steps))

            if stop == "G0X1" and gi == 0:
                sch.step("Y", lambda e: None, after=x1_steps)
                sch.check_deadlock()
                _emit(nc, {e: sch.runner(e) for e in Sched.ENGS})
                return nc

            tile_tail = []
            for ti in group:
                t0, tn = TILES[ti]
                tloc = t0 - g0

                trs = []
                for dc in range(8):
                    trs.append(sch.dma_tr("qy", X1T[0:128, 128 * dc:128 * (dc + 1)],
                                          X1F[:, dc, tloc:tloc + 128],
                                          after=x1_steps + G("x1t")))
                sp1 = sch.dma("qy", d_x1[t0:t0 + tn, :], X1T[0:tn, :], after=trs)
                setg("x1t", sp1)
                x1_spills.append(sp1)

                lgs = []
                lee = None
                for rnd in range(2):
                    def f_lg(e, _tn=tn, _tloc=tloc, _r=rnd):
                        for dc in range(8):
                            for ecl in range(2):
                                e.matmul(PG[2 + ecl][0:_tn, 0:512],
                                         X1F[:, dc, _tloc:_tloc + _tn],
                                         W_phiT[:, E2 * dc + 512 * (2 * _r + ecl):
                                                E2 * dc + 512 * (2 * _r + ecl + 1)],
                                         start=(dc == 0), stop=False)
                        for ecl in range(2):
                            e.matmul(PG[2 + ecl][0:_tn, 0:512],
                                     ONESR[0:1, 0:_tn],
                                     B_phi[0:1, 512 * (2 * _r + ecl):512 * (2 * _r + ecl + 1)],
                                     start=False, stop=True)
                    llg = sch.step("T", f_lg, after=x1_steps + G("pg2") + G("pg3"))
                    lgs.append(llg)

                    def f_eexp(e, _tn=tn, _r=rnd):
                        e.activation(ELOG[0:_tn, 1024 * _r:1024 * _r + 512],
                                     PG[2][0:_tn, 0:512], AF.Exp)
                        e.activation(ELOG[0:_tn, 1024 * _r + 512:1024 * (_r + 1)],
                                     PG[3][0:_tn, 0:512], AF.Exp)
                    lee = sch.step("S", f_eexp,
                                   after=[llg] + (G("elog") if rnd == 0 else []))
                    setg("pg2", lee)
                    setg("pg3", lee)
                guard["elog"] = []

                if debug and ti == 0:
                    addg("elog", sch.dma("qg", dbg["dbg_elog"], ELOG[0:128, :], after=[lee]))

                def f_sumP(e, _tn=tn):
                    e.reduce_sum(RP[0:_tn, :],
                                 ELOG[0:_tn, :].rearrange("p (n s) -> p n s", s=PS), axis=AX.X)
                    e.drain()
                    e.reciprocal(RP[0:_tn, :], RP[0:_tn, :])
                    e.drain()
                lsp = sch.step("V", f_sumP, after=[lee])

                def f_sumN(e, _tn=tn):
                    e.reduce_sum(RN[0:_tn, :],
                                 ELOG[0:_tn, :].rearrange("p (n s) -> p s n", s=PS), axis=AX.X)
                    e.drain()
                    e.reciprocal(RN[0:_tn, :], RN[0:_tn, :])
                    e.drain()
                lrn = sch.step("V", f_sumN, after=[lee])

                def f_D(e, _tn=tn):
                    e.tensor_tensor(DTM[0:_tn, :].rearrange("p (n s) -> p n s", s=PS),
                                    ELOG[0:_tn, :].rearrange("p (n s) -> p n s", s=PS),
                                    RP[0:_tn, :].unsqueeze(2).broadcast_to([_tn, NE, PS]),
                                    ALU.mult)
                lD = sch.step("V", f_D, after=[lsp] + G("dtm"))
                guard["dtm"] = []

                ctp = ti % 2

                def f_C(e, _tn=tn, _cp=ctp):
                    e.tensor_tensor(CTM[0:_tn, E2 * _cp:E2 * (_cp + 1)]
                                    .rearrange("p (n s) -> p s n", s=PS),
                                    ELOG[0:_tn, :].rearrange("p (n s) -> p s n", s=PS),
                                    RN[0:_tn, :].unsqueeze(2).broadcast_to([_tn, PS, NE]),
                                    ALU.mult)
                lC = sch.step("G", f_C, after=[lrn] + G(f"ctm{ctp}"))
                guard[f"ctm{ctp}"] = []
                addg("elog", lD, lC)

                if debug and ti == 0:
                    addg("dtm", sch.dma("qg", dbg["dbg_D"], DTM[0:128, :], after=[lD]))
                    lCbar = sch.step("G", lambda e: e.drain(), after=[lC])
                    addg("ctm0", sch.dma("qg", dbg["dbg_C"], CTM[0:128, 0:E2], after=[lCbar]))

                spD = sch.dma("qy", d_D[t0:t0 + tn, :], DTM[0:tn, :], after=[lD])
                addg("dtm", spD)
                D_spills.append(spD)

                if pend_ct[0] is not None:
                    tile_tail += emit_ct(*pend_ct[0])
                pend_ct[0] = (ti, tn, ctp, lC)

                tile_tail += [lee, lD, lC, spD, lsp, lrn]
                addg("x1f", lgs[-1], *trs)

            prev_group = s1_steps + x1_steps + [lst] + tile_tail + trs
            xh_free[0] = list(s1_steps)
            if stop == "G0" and gi == 0:
                sch.step("Y", lambda e: None, after=prev_group)
                sch.check_deadlock()
                _emit(nc, {e: sch.runner(e) for e in Sched.ENGS})
                return nc

        if stop == "A":
            sch.step("Y", lambda e: None, after=list(prev_group) + D_spills + C_spills + x1_spills)
            sch.check_deadlock()
            runs = {e: sch.runner(e) for e in Sched.ENGS}
            _emit(nc, runs)
            return nc

        if pend_ct[0] is not None:
            prev_group = prev_group + emit_ct(*pend_ct[0])
            pend_ct[0] = None

        SCH = [list(range(i, min(i + 7, NT))) for i in range(0, NT, 7)]
        prevX = list(prev_group) + D_spills[-1:] + x1_spills[-1:]
        xa_done = [None] * 16
        for si, sc in enumerate(SCH):
            loads = []
            for j, ti in enumerate(sc):
                t0, tn = TILES[ti]
                qD = "qy" if j % 2 == 0 else "qg"
                loads.append(sch.dma(qD, DST[0:tn, j, :], d_D[t0:t0 + tn, :],
                                     after=prevX + [D_spills[ti]]))
                loads.append(sch.dma("qs", X1S[0:tn, j, :], d_x1[t0:t0 + tn, :],
                                     after=prevX + [x1_spills[ti]]))
            lx2_list = []
            for ec in range(16):
                bk = 2 * (ec % 2)

                def f_x(e, _sc=tuple(sc), _ec=ec, _bk=bk):
                    for half in range(2):
                        for j, ti in enumerate(_sc):
                            tn = TILES[ti][1]
                            e.matmul(PG[_bk + half][:, 0:512],
                                     DST[0:tn, j, 128 * _ec:128 * (_ec + 1)],
                                     X1S[0:tn, j, 512 * half:512 * (half + 1)],
                                     start=(j == 0), stop=(j == len(_sc) - 1))
                lx2 = sch.step("T", f_x,
                               after=loads + G(f"pg{bk}") + G(f"pg{bk + 1}"))
                lx2_list.append(lx2)

                if si == 0:
                    def f_xev(e, _ec=ec, _bk=bk):
                        e.tensor_copy(XACC[:, C * _ec:C * _ec + 512],
                                      PG[_bk][:, 0:512])
                        e.tensor_copy(XACC[:, C * _ec + 512:C * (_ec + 1)],
                                      PG[_bk + 1][:, 0:512])
                else:
                    def f_xev(e, _ec=ec, _bk=bk):
                        e.tensor_tensor(XACC[:, C * _ec:C * _ec + 512],
                                        XACC[:, C * _ec:C * _ec + 512],
                                        PG[_bk][:, 0:512], ALU.add)
                        e.tensor_tensor(XACC[:, C * _ec + 512:C * (_ec + 1)],
                                        XACC[:, C * _ec + 512:C * (_ec + 1)],
                                        PG[_bk + 1][:, 0:512], ALU.add)
                lev = sch.step("V", f_xev, after=[lx2])
                xa_done[ec] = lev
                setg(f"pg{bk}", lev)
                setg(f"pg{bk + 1}", lev)
            prevX = loads + lx2_list[-2:]

        xp_w = []
        for q, ecs in (("qg", range(0, 6)), ("qy", range(6, 11)),
                       ("qs", range(11, 16))):
            for ec in ecs:
                xp_w.append(sch.dma(q, d_Xp.ap()[128 * ec:128 * (ec + 1), :],
                                    XACC[:, C * ec:C * (ec + 1)],
                                    after=[xa_done[ec]]))

        if debug:
            dcp = sch.dma("qg", dbg["dbg_dD"], d_D, after=xp_w)
            dcp2 = sch.dma("qg", dbg["dbg_dx1"], d_x1, after=xp_w)
            dcp3 = sch.dma("qg", dbg["dbg_XpL"], d_Xp.ap(), after=xp_w)
            xp_w = xp_w + [dcp, dcp2, dcp3]

        def f_rsc(e):
            e.collective_compute("ReduceScatter", ALU.add,
                                 replica_groups=[list(range(NCORES))],
                                 ins=[d_Xp.ap().opt()], outs=[d_Xl.ap().opt()]).then_inc(ccsem, 1)
            e.wait_ge(ccsem, 1)
            e.nop()
        lrsc = sch.step("G", f_rsc, after=xp_w)

        lxl = sch.dma("qy", XLOC[:, :, :], d_Xl.ap().rearrange("(c p) d -> p c d", p=128),
                      after=[lrsc])
        if debug:
            setg("xpdbg", sch.dma("qg", dbg["dbg_Xp"], XLOC[:, 0, :], after=[lxl]))

        lch_done = []
        for ch in range(2):
            def f_cl(e, _ch=ch):
                e.tensor_scalar(XLOC[:, _ch, :], XLOC[:, _ch, :], -33000.0, 65000.0,
                                ALU.max, ALU.min)
                e.drain()
                e.reduce_sum(RZ[:, 0:1], XLOC[:, _ch, :], axis=AX.X)
                e.drain()
                e.tensor_scalar(RZ[:, 0:1], RZ[:, 0:1], 1.0 / C, None, ALU.mult)
                e.drain()
                e.tensor_scalar(XLOC[:, _ch, :], XLOC[:, _ch, :], RZ[:, 0:1], None,
                                ALU.subtract)
                e.drain()
                e.tensor_tensor(XSTG[:, :], XLOC[:, _ch, :], XLOC[:, _ch, :], ALU.mult)
                e.drain()
                e.reduce_sum(RZ[:, 1:2], XSTG[:, :], axis=AX.X)
                e.drain()
                e.tensor_scalar(RZ[:, 1:2], RZ[:, 1:2], 1.0 / C, None, ALU.mult)
            lcl = sch.step("V", f_cl, after=[lxl] + G("xpdbg"))

            def f_crs(e, _ch=ch):
                e.activation(RZ[:, 2:3], RZ[:, 1:2], AF.Ln, bias=EPSC[:, 0:1])
                e.drain()
                e.activation(RZ[:, 2:3], RZ[:, 2:3], AF.Exp, scale=-0.5)
            lcrs = sch.step("S", f_crs, after=[lcl])

            def f_cap(e, _ch=ch):
                e.tensor_scalar(XLOC[:, _ch, :], XLOC[:, _ch, :], RZ[:, 2:3], None, ALU.mult)
            lcap = sch.step("V", f_cap, after=[lcrs])

            def f_cth(e, _ch=ch):
                e.activation(ELOG[:, C * _ch:C * (_ch + 1)], XLOC[:, _ch, :], AF.Tanh)
            lch_done.append(sch.step("S", f_cth, after=[lcap]))

        if debug:
            sch.dma("qg", dbg["dbg_Xln"],
                    ELOG[:, 0:1024], after=lch_done)

        xlt = []
        for pr in range(4):
            ch, npp = pr // 2, pr % 2
            r0 = 64 * npp

            def f_xt(e, _ch=ch, _r0=r0):
                for dc in range(8):
                    e.transpose(PT[0:128, 128 * dc:128 * dc + 64],
                                ELOG[_r0:_r0 + 64, C * _ch + 128 * dc:C * _ch + 128 * (dc + 1)],
                                ID64[_r0:_r0 + 64, 0:64])
            lt1 = sch.step("T", f_xt, after=lch_done + G("pt"))

            def f_xte(e, _ch=ch, _npp=npp):
                for q in range(2):
                    n = 4 * _ch + 2 * _npp + q
                    e.copy(XLNT[:, :].rearrange("p (a s) -> p a s", s=PS)
                           [:, 8 * n:8 * (n + 1), :],
                           PT[:, :].rearrange("p (c s) -> p c s", s=128)
                           [:, :, 32 * q:32 * (q + 1)])
            lte = sch.step("S", f_xte, after=[lt1])
            setg("pt", lte)
            xlt.append(lte)

        yt_w = []
        lwe_prev = None
        for n in range(EPC):
            lwe = sch.dma("qg", EXPW[:, :, :],
                          expw_d[n].rearrange("(c p) k -> p c k", p=128),
                          after=xlt + yt_w[-1:] + prevX)
            lbe = sch.dma("qg", B_ex1[0:1, :], expb_d[n:n + 1, :], after=yt_w[-1:])

            def f_yt(e, _n=n):
                for half in range(2):
                    for dc in range(8):
                        e.matmul(PG[half][0:PS, 0:512],
                                 XLNT[:, :].rearrange("p (a s) -> p a s", s=PS)[:, 8 * _n + dc, :],
                                 EXPW[:, dc, 512 * half:512 * (half + 1)],
                                 start=(dc == 0), stop=False)
                    e.matmul(PG[half][0:PS, 0:512],
                             ONESR[0:1, 0:PS],
                             B_ex1[0:1, 512 * half:512 * (half + 1)],
                             start=False, stop=True)
            lyt = sch.step("T", f_yt, after=[lwe, lbe] + G("pg0") + G("pg1"))

            def f_ye(e):
                e.tensor_copy(YSTG[0:PS, 0:512], PG[0][0:PS, 0:512])
                e.tensor_copy(YSTG[0:PS, 512:1024], PG[1][0:PS, 0:512])
            lye = sch.step("V", f_ye, after=[lyt] + yt_w[-1:])
            setg("pg0", lye)
            setg("pg1", lye)
            yt_w.append(sch.dma("qy", d_Yl.ap()[PS * n:PS * (n + 1), :], YSTG[0:PS, :],
                                after=[lye]))

        def f_ag(e):
            e.collective_compute("AllGather", ALU.bypass,
                                 replica_groups=[list(range(NCORES))],
                                 ins=[d_Yl.ap().opt()], outs=[d_Yt.ap().opt()]).then_inc(ccsem, 1)
            e.wait_ge(ccsem, 2)
            e.nop()
        lag = sch.step("G", f_ag, after=yt_w)

        lytf = sch.dma("qy", W_phiT[:, :].rearrange("p (c k) -> p c k", k=C),
                       d_Yt.ap().rearrange("(c p) k -> p c k", p=128), after=[lag])
        if debug:
            sch.dma("qg", dbg["dbg_Yt"], W_phiT[:, 0:1024], after=[lytf])
            sch.dma("qg", dbg["dbg_YtF"], W_phiT[:, :], after=[lytf])

        if stop == "X":
            sch.step("Y", lambda e: None, after=[lytf])
            sch.check_deadlock()
            runs = {e: sch.runner(e) for e in Sched.ENGS}
            _emit(nc, runs)
            return nc

        out_dmas = []
        pend_o = [None]
        lc_prev = None
        for ti in range(NT):
            t0, tn = TILES[ti]
            qrot = ("qy", "qs", "qg")
            lc1 = sch.dma(qrot[ti % 3], CFM[:, :], d_C[ti, :, :],
                          after=[lytf, C_spill_by_tile[ti]] + G("cfm"))
            lx1b = sch.dma(qrot[(ti + 1) % 3], X1T[0:tn, :], d_x1[t0:t0 + tn, :],
                           after=[lytf, x1_spills[ti]] + G("x1t"))

            b0 = 0 if ti % 2 == 0 else 2
            b1 = b0 + 1
            so = 1024 * (ti % 2)
            rb = 4 * (ti % 2)

            def f_y(e, _tn=tn, _b0=b0, _b1=b1):
                for ec in range(16):
                    e.matmul(PG[_b0][0:_tn, 0:512],
                             CFM[:, :].rearrange("p (c w) -> p c w", w=124)[:, ec, 0:_tn],
                             W_phiT[:, C * ec:C * ec + 512],
                             start=(ec == 0), stop=(ec == 15))
                for ec in range(16):
                    e.matmul(PG[_b1][0:_tn, 0:512],
                             CFM[:, :].rearrange("p (c w) -> p c w", w=124)[:, ec, 0:_tn],
                             W_phiT[:, C * ec + 512:C * ec + 1024],
                             start=(ec == 0), stop=(ec == 15))
            ly = sch.step("T", f_y, after=[lc1, lytf] + G(f"pg{b0}") + G(f"pg{b1}"))
            setg("cfm", ly)

            def f_s2(e, _tn=tn, _b0=b0, _b1=b1, _so=so, _rb=rb):
                e.scalar_tensor_tensor(S2[0:_tn, _so:_so + 512], PG[_b0][0:_tn, 0:512], 1.0,
                                       X1T[0:_tn, 0:512], ALU.mult, ALU.add)
                e.scalar_tensor_tensor(S2[0:_tn, _so + 512:_so + 1024], PG[_b1][0:_tn, 0:512],
                                       1.0, X1T[0:_tn, 512:1024], ALU.mult, ALU.add)
                e.drain()
                e.reduce_sum(RZ[0:_tn, _rb:_rb + 1], S2[0:_tn, _so:_so + 1024], axis=AX.X)
                e.drain()
                e.tensor_scalar(RZ[0:_tn, _rb:_rb + 1], RZ[0:_tn, _rb:_rb + 1], 1.0 / C,
                                None, ALU.mult)
            ls2 = sch.step("V", f_s2, after=[ly, lx1b] + G(f"s2_{ti % 2}") + G("s2dbg2"))
            setg(f"pg{b0}", ls2)
            setg(f"pg{b1}", ls2)
            setg("x1t", ls2)

            if debug and ti == 0:
                setg("s2dbg2", sch.dma("qg", dbg["dbg_S2"], S2[0:128, 0:1024], after=[ls2]))

            def f_sq2(e, _tn=tn, _so=so, _rb=rb):
                e.activation(YSTG[0:_tn, :], S2[0:_tn, _so:_so + 1024], AF.Square,
                             accum_out=RZ[0:_tn, _rb + 1:_rb + 2])
            lsq2 = sch.step("S", f_sq2, after=[ls2] + G("junk"))

            def f_v2(e, _tn=tn, _rb=rb):
                e.tensor_scalar(RZ[0:_tn, _rb + 1:_rb + 2], RZ[0:_tn, _rb + 1:_rb + 2],
                                1.0 / C, None, ALU.mult)
                e.tensor_tensor(RZ[0:_tn, _rb + 2:_rb + 3], RZ[0:_tn, _rb:_rb + 1],
                                RZ[0:_tn, _rb:_rb + 1], ALU.mult)
                e.drain()
                e.tensor_tensor(RZ[0:_tn, _rb + 1:_rb + 2], RZ[0:_tn, _rb + 1:_rb + 2],
                                RZ[0:_tn, _rb + 2:_rb + 3], ALU.subtract)
            lv2 = sch.step("V", f_v2, after=[lsq2])

            def f_r2(e, _tn=tn, _rb=rb):
                e.activation(RZ[0:_tn, _rb + 3:_rb + 4], RZ[0:_tn, _rb + 1:_rb + 2], AF.Ln,
                             bias=EPSC[0:_tn, 0:1])
                e.drain()
                e.activation(RZ[0:_tn, _rb + 3:_rb + 4], RZ[0:_tn, _rb + 3:_rb + 4], AF.Exp,
                             scale=-0.5)
            lr2 = sch.step("S", f_r2, after=[lv2])
            setg("junk", lr2)

            def f_o(e, _tn=tn, _so=so, _rb=rb):
                e.tensor_scalar(S2[0:_tn, _so:_so + 1024], S2[0:_tn, _so:_so + 1024],
                                RZ[0:_tn, _rb:_rb + 1], RZ[0:_tn, _rb + 3:_rb + 4],
                                ALU.subtract, ALU.mult)
            lo = sch.step("V", f_o, after=[lr2] + G("odbg") + G("s2dbg2"))

            if debug and ti == 0:
                setg("odbg", sch.dma("qg", dbg["dbg_O"], S2[0:128, 0:1024], after=[lo]))
            od = sch.dma(qrot[(ti + 2) % 3], out_d[t0:t0 + tn, :],
                         S2[0:tn, so:so + 1024], after=[lo])
            setg(f"s2_{ti % 2}", od)
            out_dmas.append(od)

        if pend_o[0] is not None:
            emit_o(*pend_o[0])
        sch.step("Y", lambda e: None, after=out_dmas)

        sch.check_deadlock()
        runs = {e: sch.runner(e) for e in Sched.ENGS}
        _emit(nc, runs)

    return nc


def _emit(nc, runs):
    with nc.Block() as block:
        @block.tensor
        def _(t):
            runs["T"](t)

        @block.vector
        def _(v):
            runs["V"](v)

        @block.scalar
        def _(s):
            runs["S"](s)

        @block.gpsimd
        def _(g):
            runs["G"](g)

        @block.sync
        def _(sy):
            runs["Y"](sy)


def _build_consts():
    bf = ml_dtypes.bfloat16
    ident = np.eye(128, dtype=np.float32).astype(bf)
    ident64 = np.tile(np.eye(64, dtype=np.float32), (2, 1)).astype(bf)
    onescol = np.ones((128, 1), np.float32).astype(bf)
    onesrow = np.ones((1, 128), np.float32).astype(bf)
    maskL = np.zeros((3, 128, TT), np.float32)
    maskR = np.zeros((3, 128, TT), np.float32)
    for mv, (t0, tn) in ((0, TILES[1]), (1, TILES[0]), (2, TILES[-1])):
        for tp in range(tn):
            t = t0 + tp
            p = tp // 2
            gp = t // 2
            if gp - 1 >= 0:
                maskL[mv, 2 * p:2 * p + 4, tp] = 1.0
            if gp <= (L - 4) // 2:
                maskR[mv, 2 * p + 2:2 * p + 6, tp] = 1.0
    cnt = np.zeros((1, L), np.float32)
    W = (L - 4) // 2 + 1
    idx = (np.arange(W)[:, None] * 2 + np.arange(4)[None, :]).reshape(-1)
    np.add.at(cnt[0], idx, 1.0)
    return dict(ident=ident, ident64=ident64, onescol=onescol, onesrow=onesrow,
                maskL=maskL.astype(bf), maskR=maskR.astype(bf), cnt=cnt.astype(bf))


_NC_CACHE = {}


def kernel(L0_attn_x, L0_attn_w_qkv, L0_attn_w_out, L0_attn_b_out,
           L0_ffn_phi_w, L0_ffn_phi_b, L0_ffn_expert_w, L0_ffn_expert_b,
           debug=False, stop="full"):
    x = np.asarray(L0_attn_x, dtype=np.float32)
    wqkv = np.ascontiguousarray(np.asarray(L0_attn_w_qkv, dtype=np.float32))
    wout = np.ascontiguousarray(np.asarray(L0_attn_w_out, dtype=np.float32))
    bout = np.asarray(L0_attn_b_out, dtype=np.float32).reshape(1, C)
    phiT = np.ascontiguousarray(np.asarray(L0_ffn_phi_w, dtype=np.float32).T)
    phib = np.asarray(L0_ffn_phi_b, dtype=np.float32).reshape(1, E2)
    expw = np.ascontiguousarray(np.asarray(L0_ffn_expert_w, dtype=np.float32))
    expb = np.ascontiguousarray(np.asarray(L0_ffn_expert_b, dtype=np.float32))

    consts = _build_consts()
    key = ("nc", debug, stop)
    if key not in _NC_CACHE:
        _NC_CACHE[key] = build_nc(debug=debug, stop=stop)
    nc = _NC_CACHE[key]

    in_maps = []
    for b in range(NCORES):
        in_maps.append(dict(
            xT=np.ascontiguousarray(x[b].T),
            wqkv=wqkv, wout=wout, bout=bout, phiT=phiT, phib=phib,
            expw=np.ascontiguousarray(expw[b * EPC:(b + 1) * EPC]),
            expb=np.ascontiguousarray(expb[b * EPC:(b + 1) * EPC]),
            ident=consts["ident"], ident64=consts["ident64"],
            onescol=consts["onescol"], onesrow=consts["onesrow"],
            maskL=consts["maskL"], maskR=consts["maskR"], cnt=consts["cnt"],
        ))

    res = run_bass_kernel_spmd(nc, in_maps, core_ids=list(range(NCORES)))
    out = np.stack([np.asarray(res.results[b]["out"], dtype=np.float32)
                    for b in range(NCORES)])
    if debug:
        return out, res
    return out
